# version 04
# baseline (speedup 1.0000x reference)
import numpy as np
import ml_dtypes
import contextlib
import concourse.bass as bass
import concourse.mybir as mybir
from concourse.bass_utils import run_bass_kernel_spmd

F32 = mybir.dt.float32
BF16 = mybir.dt.bfloat16
AF = mybir.ActivationFunctionType
ALU = mybir.AluOpType
AX = mybir.AxisListType

S = 8192
D = 2048
HD = 128
NH = 16
NKV = 4
I_FF = 1408
NE = 16
SI = 2816
NT = 2048
EXP_BIAS = -8.0
EXP_SCALE = float(1.0 / np.sqrt(HD))
MASK_NEG = -1.0e9


class Ctr:
    def __init__(self, sem):
        self.sem = sem
        self.n = 0

    def inc(self, inst, k=1):
        inst.then_inc(self.sem, k)
        self.n += k
        return self.n


def build(debug=False, upto='p6'):
    nc = bass.Bass("TRN2", target_bir_lowering=False, debug=False)

    xb = nc.dram_tensor("xb", [S, D], F32, kind="ExternalInput").ap()
    xq = nc.dram_tensor("xq", [NT, D], F32, kind="ExternalInput").ap()
    wq = nc.dram_tensor("wq", [D, D], F32, kind="ExternalInput").ap()
    wk = nc.dram_tensor("wk", [D, NKV * HD], F32, kind="ExternalInput").ap()
    wv = nc.dram_tensor("wv", [D, NKV * HD], F32, kind="ExternalInput").ap()
    wo = nc.dram_tensor("wo", [D, D], F32, kind="ExternalInput").ap()
    gwT = nc.dram_tensor("gwT", [D, NE], F32, kind="ExternalInput").ap()
    wgu = nc.dram_tensor("wgu", [NE, D, 2 * I_FF], F32, kind="ExternalInput").ap()
    wdn = nc.dram_tensor("wdn", [NE, I_FF, D], F32, kind="ExternalInput").ap()
    sgu = nc.dram_tensor("sgu", [D, 2 * SI], F32, kind="ExternalInput").ap()
    sdn = nc.dram_tensor("sdn", [SI, D], F32, kind="ExternalInput").ap()
    identin = nc.dram_tensor("identin", [128, 128], F32, kind="ExternalInput").ap()
    ropeK = nc.dram_tensor("ropeK", [2, 128, S], F32, kind="ExternalInput").ap()
    ropeQ = nc.dram_tensor("ropeQ", [2, 128, NT], F32, kind="ExternalInput").ap()
    maskD = nc.dram_tensor("maskD", [8, 128, 512], F32, kind="ExternalInput").ap()

    out = nc.dram_tensor("out", [NT, D], F32, kind="ExternalOutput").ap()

    skind = "ExternalOutput" if debug else "Internal"
    xn_scr = nc.dram_tensor("xn_scr", [S, D], BF16, kind=skind).ap()
    xnq_scr = nc.dram_tensor("xnq_scr", [NT, D], BF16, kind=skind).ap()
    xnqlo_scr = nc.dram_tensor("xnqlo_scr", [NT, D], BF16, kind=skind).ap()
    oat_scr = nc.dram_tensor("oat_scr", [D, NT], BF16, kind=skind).ap()
    s_scr = nc.dram_tensor("s_scr", [D, NT], BF16, kind=skind).ap()
    m_scr = nc.dram_tensor("m_scr", [D, NT], BF16, kind=skind).ap()
    cmb_scr = nc.dram_tensor("cmb_scr", [NE, NT], F32, kind=skind).ap()
    dbg_q = nc.dram_tensor("dbg_q", [128, 8 * NT], BF16, kind=skind).ap()
    dbg_k = nc.dram_tensor("dbg_k", [128, 2 * S], BF16, kind=skind).ap()
    dbg_v = nc.dram_tensor("dbg_v", [128, 64 * 256], BF16, kind=skind).ap()

    with contextlib.ExitStack() as stk:
        cst = stk.enter_context(nc.sbuf_tensor("cst", [128, 8], F32))
        ones_bf = stk.enter_context(nc.sbuf_tensor("ones_bf", [128, 1], BF16))
        ones_row = stk.enter_context(nc.sbuf_tensor("ones_row", [1, 128], BF16))
        ones_rf = stk.enter_context(nc.sbuf_tensor("ones_rf", [1, 128], F32))
        ident = stk.enter_context(nc.sbuf_tensor("ident", [128, 128], F32))
        attn_scope = contextlib.ExitStack()
        masksb = attn_scope.enter_context(nc.sbuf_tensor("masksb", [128, 8, 512], BF16))
        identb = attn_scope.enter_context(nc.sbuf_tensor("identb", [128, 128], BF16))

        ph = stk.enter_context(nc.semaphore("ph"))
        c_init = Ctr(stk.enter_context(nc.semaphore("c_init")))
        phn = [0]

        block = stk.enter_context(nc.Block())
        mk_sem = lambda n: stk.enter_context(nc.semaphore(n))

        def barrier(tag):
            phn[0] += 5
            tgt = phn[0]
            for name in ("sync", "scalar", "vector", "tensor", "gpsimd"):
                sec = getattr(block, name)

                @sec
                def _(eng, tgt=tgt):
                    eng.nop().then_inc(ph, 1)
                    eng.wait_ge(ph, tgt)

        @block.gpsimd
        def _(g):
            g.memset(cst[:, 0:1], 1e-5)
            g.memset(cst[:, 1:2], 1.0)
            g.memset(cst[:, 2:3], -1.0)
            g.memset(cst[:, 3:4], 4.0)
            g.memset(cst[:, 4:5], 16.0)
            g.memset(cst[:, 5:6], EXP_SCALE)
            g.memset(cst[:, 6:7], EXP_BIAS)
            g.memset(ones_bf[:, :], 1.0)
            g.memset(ones_row[:, :], 1.0)
            c_init.inc(g.memset(ones_rf[:, :], 1.0))

        @block.sync
        def _(s):
            c_init.inc(s.dma_start(ident[:, :], identin), 16)

        @block.gpsimd
        def _(g):
            c_init.inc(g.dma_start(masksb[:, :, :], maskD.rearrange("kc k q -> k kc q")), 16)
            c_init.inc(g.dma_start(identb[:, :], identin), 16)

        def wait_init(eng):
            eng.wait_ge(c_init.sem, 49)

        for _name in ("sync", "scalar", "vector", "tensor", "gpsimd"):
            @getattr(block, _name)
            def _(eng):
                wait_init(eng)

        with contextlib.ExitStack() as ps:
            NXB = 4
            xbuf = ps.enter_context(nc.sbuf_tensor("p1_xbuf", [128, NXB, D], F32))
            xnb = ps.enter_context(nc.sbuf_tensor("p1_xnb", [128, 2, D], BF16))
            lob = ps.enter_context(nc.sbuf_tensor("p1_lob", [128, 2, D], BF16))
            xnf = ps.enter_context(nc.sbuf_tensor("p1_xnf", [128, D], F32))
            stats = ps.enter_context(nc.sbuf_tensor("p1_stats", [128, 4, 6], F32))
            aggr = ps.enter_context(nc.sbuf_tensor("p1_aggr", [128, 2], F32))
            mv = ps.enter_context(nc.sbuf_tensor("p1_mv", [128, 4], F32))
            c_ld = Ctr(mk_sem("p1ld"))
            c_v = Ctr(mk_sem("p1v"))
            c_sc = Ctr(mk_sem("p1sc"))
            c_st = Ctr(mk_sem("p1st"))

            tiles = [("b", i) for i in range(64)] + [("l", i) for i in range(16)]
            cnt = [8 if k == "l" else 6 for k, _ in tiles]
            vc = [0]
            for cn in cnt:
                vc.append(vc[-1] + cn)
            nst = [2 if k == "l" else 1 for k, _ in tiles]
            spre = [0]
            for sn in nst:
                spre.append(spre[-1] + sn)

            @block.sync
            def _(s):
                for t, (kind, i) in enumerate(tiles):
                    src_ = xb[i * 128:(i + 1) * 128, :] if kind == "b" else xq[i * 128:(i + 1) * 128, :]
                    if t >= NXB:
                        s.wait_ge(c_v.sem, vc[t - NXB + 1])
                    c_ld.inc(s.dma_start(xbuf[:, t % NXB, :], src_), 16)

            @block.vector
            def _(v):
                for t, (kind, i) in enumerate(tiles):
                    v.wait_ge(c_ld.sem, 16 * (t + 1))
                    x_t = xbuf[:, t % NXB, :]
                    for j in range(3):
                        v.bn_stats(stats[:, j, :], x_t[:, j * 512:(j + 1) * 512])
                    c_v.inc(v.bn_stats(stats[:, 3, :], x_t[:, 3 * 512:4 * 512]))
                    v.wait_ge(c_v.sem, c_v.n)
                    c_v.inc(v.bn_aggr(aggr[:, :], stats[:, :, :]))
                    v.wait_ge(c_v.sem, c_v.n)
                    c_v.inc(v.tensor_scalar(mv[:, 0:1], aggr[:, 1:2], cst[:, 0:1], None, ALU.add))
                    v.wait_ge(c_sc.sem, t + 1)
                    c_v.inc(v.reciprocal(mv[:, 2:3], mv[:, 1:2]))
                    v.wait_ge(c_v.sem, c_v.n)
                    c_v.inc(v.scalar_tensor_tensor(mv[:, 3:4], aggr[:, 0:1], cst[:, 2:3], mv[:, 2:3],
                                                   ALU.mult, ALU.mult))
                    v.wait_ge(c_v.sem, c_v.n)
                    if t >= 2:
                        v.wait_ge(c_st.sem, 16 * spre[t - 1])
                    if kind == "l":
                        c_v.inc(v.tensor_scalar(xnf[:, :], x_t, mv[:, 2:3], mv[:, 3:4], ALU.mult, ALU.add))
                        v.wait_ge(c_v.sem, c_v.n)
                        c_v.inc(v.tensor_copy(xnb[:, t % 2, :], xnf[:, :]))
                        v.wait_ge(c_v.sem, c_v.n)
                        c_v.inc(v.tensor_sub(lob[:, t % 2, :], xnf[:, :], xnb[:, t % 2, :]))
                    else:
                        c_v.inc(v.tensor_scalar(xnb[:, t % 2, :], x_t, mv[:, 2:3], mv[:, 3:4],
                                                ALU.mult, ALU.add))

            @block.scalar
            def _(sc):
                for t in range(len(tiles)):
                    sc.wait_ge(c_v.sem, vc[t] + 3)
                    c_sc.inc(sc.activation(mv[:, 1:2], mv[:, 0:1], AF.Sqrt))

            @block.gpsimd
            def _(g):
                for t, (kind, i) in enumerate(tiles):
                    g.wait_ge(c_v.sem, vc[t + 1])
                    if kind == "b":
                        c_st.inc(g.dma_start(xn_scr[i * 128:(i + 1) * 128, :], xnb[:, t % 2, :]), 16)
                    else:
                        c_st.inc(g.dma_start(xnq_scr[i * 128:(i + 1) * 128, :], xnb[:, t % 2, :]), 16)
                        c_st.inc(g.dma_start(xnqlo_scr[i * 128:(i + 1) * 128, :], lob[:, t % 2, :]), 16)

            @block.sync
            def _(s):
                s.wait_ge(c_st.sem, 16 * spre[-1])

            @block.vector
            def _(v):
                v.wait_ge(c_v.sem, vc[-1])

            @block.scalar
            def _(sc):
                sc.wait_ge(c_sc.sem, len(tiles))

            @block.gpsimd
            def _(g2):
                g2.wait_ge(c_st.sem, 16 * spre[-1])
        barrier("ln")

        for pas in range(2):
            emit_attention_pass(nc, block, pas, dict(
                cst=cst, ones_bf=ones_bf, ones_row=ones_row, ones_rf=ones_rf,
                masksb=masksb, identb=identb, wq=wq, wk=wk, wv=wv, xn_scr=xn_scr,
                xnq_scr=xnq_scr, ropeK=ropeK, ropeQ=ropeQ, oat_scr=oat_scr,
                mk_sem=mk_sem,
                dbg=(dict(q=dbg_q, k=dbg_k, v=dbg_v) if (debug and pas == 0) else None)))
            barrier(f"attn{pas}")
        attn_scope.close()
        if upto == 'p3':
            return nc

        with contextlib.ExitStack() as ps:
            sb4 = lambda n, shp, dt: ps.enter_context(nc.sbuf_tensor("p4" + n, shp, dt))
            wo_sb = sb4("wo", [128, 16, D], BF16)
            o_sb = sb4("o", [128, 2, 16, 512], BF16)
            s_sb = sb4("s", [128, 2, 512], BF16)
            pso = [ps.enter_context(nc.psum_tensor("p4p0", [128, 512], F32)),
                   ps.enter_context(nc.psum_tensor("p4p1", [128, 512], F32))]
            c_lw = Ctr(mk_sem("p4lw"))
            c_lo = Ctr(mk_sem("p4lo"))
            c_mm = Ctr(mk_sem("p4mm"))
            c_ev = Ctr(mk_sem("p4ev"))
            c_st4 = Ctr(mk_sem("p4st"))

            @block.gpsimd
            def _(gp):
                for i in range(4):
                    c_lw.inc(gp.dma_start(wo_sb[:, 4 * i:4 * (i + 1), :],
                                          wo[512 * i:512 * (i + 1), :].rearrange("(hc p) d -> p hc d", p=128)), 16)

            @block.sync
            def _(s):
                for qs4 in range(4):
                    if qs4 >= 2:
                        s.wait_ge(c_mm.sem, 16 * (qs4 - 1))
                    for hc in range(16):
                        c_lo.inc(s.dma_start(o_sb[:, qs4 % 2, hc, :],
                                             oat_scr[hc * 128:(hc + 1) * 128, qs4 * 512:(qs4 + 1) * 512]), 16)

            nmm = 0
            for qs4 in range(4):
                for dc in range(16):
                    @block.tensor
                    def _(t, qs4=qs4, dc=dc, n=nmm):
                        if n == 0:
                            t.wait_ge(c_lw.sem, 16 * 4)
                        if dc == 0:
                            t.wait_ge(c_lo.sem, 16 * 16 * (qs4 + 1))
                        if n >= 2:
                            t.wait_ge(c_ev.sem, n - 1)
                        mm = None
                        for hc in range(16):
                            mm = t.matmul(pso[n % 2][:, :], wo_sb[:, hc, dc * 128:(dc + 1) * 128],
                                          o_sb[:, qs4 % 2, hc, :], start=(hc == 0), stop=(hc == 15))
                        c_mm.inc(mm)

                    @block.vector
                    def _(v, qs4=qs4, dc=dc, n=nmm):
                        v.wait_ge(c_mm.sem, n + 1)
                        if n >= 2:
                            v.wait_ge(c_st4.sem, 16 * (n - 1))
                        c_ev.inc(v.tensor_copy(s_sb[:, n % 2, :], pso[n % 2][:, :]))

                    @block.scalar
                    def _(sc, qs4=qs4, dc=dc, n=nmm):
                        sc.wait_ge(c_ev.sem, n + 1)
                        c_st4.inc(sc.dma_start(s_scr[dc * 128:(dc + 1) * 128, qs4 * 512:(qs4 + 1) * 512],
                                               s_sb[:, n % 2, :]), 16)
                    nmm += 1

            @block.scalar
            def _(sc):
                sc.wait_ge(c_st4.sem, 16 * 64)
        barrier("oproj")
        if upto == 'p4':
            return nc

        emit_ffn(nc, block, dict(
            cst=cst, ones_rf=ones_rf, ident=ident, mk_sem=mk_sem,
            gwT=gwT, wgu=wgu, wdn=wdn, sgu=sgu, sdn=sdn,
            xnq_scr=xnq_scr, xnqlo_scr=xnqlo_scr, s_scr=s_scr, m_scr=m_scr, cmb_scr=cmb_scr))
        barrier("ffn")
        if upto == 'p5':
            return nc

        with contextlib.ExitStack() as ps:
            sb6 = lambda n, shp, dt: ps.enter_context(nc.sbuf_tensor("p6" + n, shp, dt))
            m_tm = sb6("m", [128, 2, D], BF16)
            x_tm = sb6("x", [128, 2, D], F32)
            o_tm = sb6("o", [128, 2, D], F32)
            c_l6 = Ctr(mk_sem("p6l"))
            c_v6 = Ctr(mk_sem("p6v"))
            c_o6 = Ctr(mk_sem("p6o"))

            for qt in range(16):
                @block.sync
                def _(s, qt=qt):
                    if qt >= 2:
                        s.wait_ge(c_v6.sem, qt - 1)
                    c_l6.inc(s.dma_start(m_tm[:, qt % 2, :],
                                         m_scr[:, qt * 128:(qt + 1) * 128], transpose=True), 16)
                    c_l6.inc(s.dma_start(x_tm[:, qt % 2, :], xq[qt * 128:(qt + 1) * 128, :]), 16)

                @block.vector
                def _(v, qt=qt):
                    v.wait_ge(c_l6.sem, 32 * (qt + 1))
                    if qt >= 2:
                        v.wait_ge(c_o6.sem, 16 * (qt - 1))
                    c_v6.inc(v.tensor_tensor(o_tm[:, qt % 2, :], x_tm[:, qt % 2, :],
                                             m_tm[:, qt % 2, :], ALU.add))

                @block.scalar
                def _(sc, qt=qt):
                    sc.wait_ge(c_v6.sem, qt + 1)
                    c_o6.inc(sc.dma_start(out[qt * 128:(qt + 1) * 128, :], o_tm[:, qt % 2, :]), 16)

            @block.scalar
            def _(sc):
                sc.wait_ge(c_o6.sem, 16 * 16)

    return nc


def emit_ffn(nc, block, g):
    cst, ones_rf, ident, mk_sem = g["cst"], g["ones_rf"], g["ident"], g["mk_sem"]
    gwT, wgu, wdn, sgu, sdn = g["gwT"], g["wgu"], g["wdn"], g["sgu"], g["sdn"]
    xnq_scr, s_scr, m_scr, cmb_scr = g["xnq_scr"], g["s_scr"], g["m_scr"], g["cmb_scr"]
    xnqlo_scr = g["xnqlo_scr"]
    with contextlib.ExitStack() as ps:
        sb = lambda n, shp, dt: ps.enter_context(nc.sbuf_tensor("f_" + n, shp, dt))
        xnq_f = sb("xn", [128, 16, NT], BF16)
        c_ldx = Ctr(mk_sem("fldx"))
        c_rt = Ctr(mk_sem("frt"))
        c_rv = Ctr(mk_sem("frv"))
        c_rs = Ctr(mk_sem("frs"))
        c_cst = Ctr(mk_sem("fcst"))

        @block.sync
        def _(s):
            for dc in range(16):
                for q4 in range(4):
                    c_ldx.inc(s.dma_start(xnq_f[:, dc, q4 * 512:(q4 + 1) * 512],
                                          xnq_scr[q4 * 512:(q4 + 1) * 512, dc * 128:(dc + 1) * 128],
                                          transpose=True), 16)

        with contextlib.ExitStack() as rs:
            sbr = lambda n, shp, dt: rs.enter_context(nc.sbuf_tensor("fr" + n, shp, dt))
            cmbT = sbr("cT", [16, NT], F32)
            xnqlo_f = sbr("xlo", [128, 16, NT], BF16)
            gw_sb = sbr("gw", [128, 16, 16], BF16)
            gwf_sb = sbr("gwf", [128, 16, 16], F32)
            gwlo_sb = sbr("gwlo", [128, 16, 16], BF16)
            lg = sbr("lg", [128, 16], F32)
            gm = sbr("gm", [128, 4], F32)
            t6 = sbr("t6", [128, 8], F32)
            keep = sbr("keep", [128, 4], F32)
            ml = sbr("ml", [128, 16], F32)
            ex = sbr("ex", [128, 16], F32)
            red = sbr("red", [128, 4], F32)
            probs = sbr("probs", [128, 16], F32)
            vv = sbr("vv", [128, 8], F32)
            ind = sbr("ind", [128, 4], F32)
            eqs = sbr("eqs", [128, 16], F32)
            work = sbr("work", [128, 16], F32)
            work2 = sbr("work2", [128, 16], F32)
            strict = sbr("strict", [128, 16], F32)
            cums = sbr("cums", [128, 16], F32)
            cums2 = sbr("cums2", [128, 16], F32)
            sel = sbr("sel", [128, 16], F32)
            cmb = sbr("cmb", [128, 16], F32)
            plog = rs.enter_context(nc.psum_tensor("frpl", [128, 16], F32))
            ptp = rs.enter_context(nc.psum_tensor("frpt", [16, 128], F32))

            @block.gpsimd
            def _(gp):
                c_ldx.inc(gp.dma_start(gw_sb[:, :, :], gwT.rearrange("(dc p) e -> p dc e", p=128)), 16)

            @block.sync
            def _(s):
                c_ldx.inc(s.dma_start(gwf_sb[:, :, :], gwT.rearrange("(dc p) e -> p dc e", p=128)), 16)
                for dc in range(16):
                    for q4 in range(4):
                        c_ldx.inc(s.dma_start(xnqlo_f[:, dc, q4 * 512:(q4 + 1) * 512],
                                              xnqlo_scr[q4 * 512:(q4 + 1) * 512, dc * 128:(dc + 1) * 128],
                                              transpose=True), 16)

            @block.vector
            def _(v):
                v.wait_ge(c_ldx.sem, 16 * 66)
                c_rv.inc(v.tensor_sub(gwlo_sb[:, :, :], gwf_sb[:, :, :], gw_sb[:, :, :]))
                v.wait_ge(c_rv.sem, c_rv.n)

            rv_marks = []
            for tt in range(16):
                @block.tensor
                def _(t, tt=tt):
                    if tt == 0:
                        t.wait_ge(c_ldx.sem, 16 * 130)
                        t.wait_ge(c_rv.sem, 1)
                    if tt >= 1:
                        t.wait_ge(c_rv.sem, rv_marks[tt - 1]["end"])
                    mm = None
                    sl = slice(tt * 128, (tt + 1) * 128)
                    for dc in range(16):
                        t.matmul(plog[:, :], xnq_f[:, dc, sl], gw_sb[:, dc, :],
                                 start=(dc == 0), stop=False, skip_group_check=True)
                    for dc in range(16):
                        t.matmul(plog[:, :], xnq_f[:, dc, sl], gwlo_sb[:, dc, :],
                                 start=False, stop=False, skip_group_check=True)
                    for dc in range(16):
                        mm = t.matmul(plog[:, :], xnqlo_f[:, dc, sl], gw_sb[:, dc, :],
                                      start=False, stop=(dc == 15), skip_group_check=True)
                    c_rt.inc(mm)

                marks = {}
                rv_marks.append(marks)

                @block.vector
                def _(v, tt=tt, marks=marks):
                    def stepw(inst):
                        c_rv.inc(inst)
                        v.wait_ge(c_rv.sem, c_rv.n)
                    v.wait_ge(c_rt.sem, 2 * tt + 1)
                    stepw(v.tensor_copy(lg[:, :], plog[:, :]))
                    lgv = lg[:, :].rearrange("p (g j) -> p g j", j=4)
                    v.tensor_tensor(t6[:, 0:4], lgv[:, :, 0], lgv[:, :, 1], ALU.max)
                    stepw(v.tensor_tensor(keep[:, 0:4], lgv[:, :, 2], lgv[:, :, 3], ALU.max))
                    stepw(v.tensor_tensor(gm[:, :], t6[:, 0:4], keep[:, :], ALU.max))
                    v.tensor_tensor(t6[:, 0:1], gm[:, 0:1], gm[:, 1:2], ALU.max)
                    v.tensor_tensor(t6[:, 1:2], gm[:, 0:1], gm[:, 1:2], ALU.min)
                    v.tensor_tensor(t6[:, 2:3], gm[:, 2:3], gm[:, 3:4], ALU.max)
                    stepw(v.tensor_tensor(t6[:, 3:4], gm[:, 2:3], gm[:, 3:4], ALU.min))
                    v.tensor_tensor(t6[:, 4:5], t6[:, 0:1], t6[:, 2:3], ALU.min)
                    stepw(v.tensor_tensor(t6[:, 5:6], t6[:, 1:2], t6[:, 3:4], ALU.max))
                    stepw(v.tensor_tensor(t6[:, 6:7], t6[:, 4:5], t6[:, 5:6], ALU.max))
                    stepw(v.tensor_scalar(keep[:, :], gm[:, :], t6[:, 6:7], None, ALU.is_ge))
                    for j in range(4):
                        inst = v.tensor_tensor(ml[:, :].rearrange("p (g j) -> p g j", j=4)[:, :, j],
                                               lgv[:, :, j], keep[:, :], ALU.mult)
                    stepw(inst)
                    stepw(v.reduce_max(red[:, 0:1], ml[:, :], axis=AX.X, negate=True))
                    marks["rm"] = c_rv.n
                    v.wait_ge(c_rs.sem, tt + 1)
                    stepw(v.reduce_sum(red[:, 1:2], ex[:, :], axis=AX.X))
                    stepw(v.reciprocal(red[:, 2:3], red[:, 1:2]))
                    stepw(v.tensor_scalar(probs[:, :], ex[:, :], red[:, 2:3], None, ALU.mult))
                    stepw(v.tensor_copy(work[:, :], probs[:, :]))
                    for r in range(4):
                        stepw(v.reduce_max(vv[:, r:r + 1], work[:, :], axis=AX.X))
                        if r < 3:
                            stepw(v.tensor_scalar(eqs[:, :], work[:, :], vv[:, r:r + 1], None, ALU.is_ge))
                            stepw(v.reduce_sum(vv[:, 4 + r:5 + r], eqs[:, :], axis=AX.X))
                            if r > 0:
                                stepw(v.tensor_tensor(vv[:, 4 + r:5 + r], vv[:, 4 + r:5 + r],
                                                      vv[:, 3 + r:4 + r], ALU.add))
                            v.scalar_tensor_tensor(work2[:, :], eqs[:, :], cst[:, 2:3], work[:, :],
                                                   ALU.mult, ALU.add)
                            stepw(v.nop())
                            stepw(v.tensor_copy(work[:, :], work2[:, :]))
                    v.tensor_scalar(ind[:, 0:1], vv[:, 4:5], cst[:, 3:4], None, ALU.is_ge)
                    v.tensor_scalar(ind[:, 1:2], vv[:, 5:6], cst[:, 3:4], None, ALU.is_ge)
                    stepw(v.tensor_scalar(ind[:, 2:3], vv[:, 6:7], cst[:, 3:4], None, ALU.is_ge))
                    stepw(v.tensor_tensor(ind[:, 1:2], ind[:, 1:2], ind[:, 0:1], ALU.subtract))
                    v.tensor_tensor(ind[:, 2:3], ind[:, 2:3], ind[:, 0:1], ALU.subtract)
                    stepw(v.nop())
                    stepw(v.tensor_tensor(ind[:, 2:3], ind[:, 2:3], ind[:, 1:2], ALU.subtract))
                    v.tensor_tensor(vv[:, 7:8], ind[:, 0:1], vv[:, 0:1], ALU.mult)
                    stepw(v.tensor_tensor(t6[:, 7:8], ind[:, 1:2], vv[:, 1:2], ALU.mult))
                    stepw(v.tensor_tensor(vv[:, 7:8], vv[:, 7:8], t6[:, 7:8], ALU.add))
                    stepw(v.tensor_tensor(t6[:, 7:8], ind[:, 2:3], vv[:, 2:3], ALU.mult))
                    stepw(v.tensor_tensor(vv[:, 7:8], vv[:, 7:8], t6[:, 7:8], ALU.add))
                    stepw(v.tensor_tensor(t6[:, 7:8], ind[:, 0:1], ind[:, 1:2], ALU.add))
                    stepw(v.tensor_tensor(t6[:, 7:8], t6[:, 7:8], ind[:, 2:3], ALU.add))
                    stepw(v.scalar_tensor_tensor(t6[:, 7:8], t6[:, 7:8], cst[:, 2:3], cst[:, 1:2],
                                                 ALU.mult, ALU.add))
                    stepw(v.tensor_tensor(t6[:, 7:8], t6[:, 7:8], vv[:, 3:4], ALU.mult))
                    stepw(v.tensor_tensor(vv[:, 7:8], vv[:, 7:8], t6[:, 7:8], ALU.add))
                    v.tensor_scalar(strict[:, :], probs[:, :], vv[:, 7:8], None, ALU.is_gt)
                    stepw(v.tensor_scalar(eqs[:, :], probs[:, :], vv[:, 7:8], None, ALU.is_equal))
                    stepw(v.reduce_sum(ind[:, 3:4], strict[:, :], axis=AX.X, negate=True))
                    stepw(v.tensor_scalar(ind[:, 3:4], ind[:, 3:4], cst[:, 3:4], None, ALU.add))
                    stepw(v.tensor_copy(cums[:, :], eqs[:, :]))
                    for k, (s_, d_) in zip([1, 2, 4, 8],
                                           [(cums, cums2), (cums2, cums), (cums, cums2), (cums2, cums)]):
                        v.tensor_copy(d_[:, 0:k], s_[:, 0:k])
                        stepw(v.tensor_tensor(d_[:, k:16], s_[:, k:16], s_[:, 0:16 - k], ALU.add))
                    stepw(v.tensor_scalar(cums2[:, :], cums[:, :], ind[:, 3:4], None, ALU.is_le))
                    stepw(v.tensor_tensor(sel[:, :], eqs[:, :], cums2[:, :], ALU.mult))
                    stepw(v.tensor_tensor(sel[:, :], sel[:, :], strict[:, :], ALU.add))
                    stepw(v.scalar_tensor_tensor(cmb[:, :], sel[:, :], cst[:, 4:5], probs[:, :],
                                                 ALU.mult, ALU.mult))
                    marks["cmb"] = c_rv.n
                    v.wait_ge(c_rt.sem, 2 * (tt + 1))
                    stepw(v.tensor_copy(cmbT[:, tt * 128:(tt + 1) * 128], ptp[0:16, :]))
                    marks["end"] = c_rv.n

                @block.scalar
                def _(sc, tt=tt, marks=marks):
                    sc.wait_ge(c_rv.sem, marks["rm"])
                    c_rs.inc(sc.activation(ex[:, :], ml[:, :], AF.Exp, bias=red[:, 0:1]))

                @block.tensor
                def _(t, tt=tt, marks=marks):
                    t.wait_ge(c_rv.sem, marks["cmb"])
                    c_rt.inc(t.transpose(ptp[0:16, 0:128], cmb[:, 0:16], ident[:, :]))

            @block.sync
            def _(s):
                s.wait_ge(c_rv.sem, rv_marks[-1]["end"])
                c_cst.inc(s.dma_start(cmb_scr, cmbT[:, :]), 16)

        Y = sb("Y", [128, 16, 1024], F32)
        hs = sb("hs", [128, 11, 1024], BF16)
        wgu_sb = sb("wgu", [128, 3, 16, 256], BF16)
        wd_sb = sb("wd", [128, 2, 11, 256], BF16)
        cbc = sb("cbc", [128, 1024], F32)
        cr = sb("cr", [1, 1024], F32)
        t1s = sb("t1", [128, 2, 512], BF16)
        t2s = sb("t2", [128, 1, 512], F32)
        g_sb = sb("g", [128, 2, 1024], BF16)
        ss_sb = sb("ss", [128, 1, 1024], BF16)
        m_sb = sb("m", [128, 1, 1024], BF16)
        pg = [ps.enter_context(nc.psum_tensor("fpg%d" % i, [128, 512], F32)) for i in range(2)]
        pu = [ps.enter_context(nc.psum_tensor("fpu%d" % i, [128, 512], F32)) for i in range(2)]
        py = [ps.enter_context(nc.psum_tensor("fpy%d" % i, [128, 512], F32)) for i in range(2)]
        pbc2 = ps.enter_context(nc.psum_tensor("fpbc", [128, 512], F32))
        c_wl = Ctr(mk_sem("fwl"))
        c_tg = Ctr(mk_sem("ftg"))
        c_ty = Ctr(mk_sem("fty"))
        c_tb2 = Ctr(mk_sem("ftb"))
        c_s5 = Ctr(mk_sem("fs5"))
        c_v5 = Ctr(mk_sem("fv5"))
        c_vy = Ctr(mk_sem("fvy"))
        c_vb = Ctr(mk_sem("fvb"))
        c_crl = Ctr(mk_sem("fcrl"))
        c_sld = Ctr(mk_sem("fsld"))
        c_mst = Ctr(mk_sem("fmst"))

        experts = [("sA", None), ("sB", None)] + [("e", i) for i in range(NE)]

        for half in range(2):
            h0 = half * 1024
            for exi, (kind, ei) in enumerate(experts):
                first = exi == 0
                routed = kind == "e"
                if routed:
                    @block.sync
                    def _(s, ei=ei, half=half):
                        s.wait_ge(c_cst.sem, 16)
                        s.wait_ge(c_tb2.sem, c_tb2.n)
                        c_crl.inc(s.dma_start(cr[:, :], cmb_scr[ei:ei + 1, h0:h0 + 1024]), 16)

                    @block.tensor
                    def _(t):
                        t.wait_ge(c_crl.sem, c_crl.n)
                        t.wait_ge(c_vb.sem, c_vb.n)
                        c_tb2.inc(t.matmul(pbc2[:, :], ones_rf[:, :], cr[:, 0:512],
                                           start=True, stop=True))

                    @block.vector
                    def _(v):
                        v.wait_ge(c_tb2.sem, c_tb2.n)
                        c_vb.inc(v.tensor_copy(cbc[:, 0:512], pbc2[:, :]))

                    @block.tensor
                    def _(t):
                        t.wait_ge(c_vb.sem, c_vb.n)
                        c_tb2.inc(t.matmul(pbc2[:, :], ones_rf[:, :], cr[:, 512:1024],
                                           start=True, stop=True))

                    @block.vector
                    def _(v):
                        v.wait_ge(c_tb2.sem, c_tb2.n)
                        c_vb.inc(v.tensor_copy(cbc[:, 512:1024], pbc2[:, :]))

                if exi == 0 and half == 0:
                    pg_free = {0: None, 1: None}
                    py_free = {0: None, 1: None}
                for c in range(11):
                    par = (exi * 11 + c) % 3
                    if kind == "e":
                        gsrc = wgu[ei][:, c * 128:(c + 1) * 128]
                        usrc = wgu[ei][:, I_FF + c * 128:I_FF + (c + 1) * 128]
                    elif kind == "sA":
                        gsrc = sgu[:, c * 128:(c + 1) * 128]
                        usrc = sgu[:, SI + c * 128:SI + (c + 1) * 128]
                    else:
                        gsrc = sgu[:, I_FF + c * 128:I_FF + (c + 1) * 128]
                        usrc = sgu[:, SI + I_FF + c * 128:SI + I_FF + (c + 1) * 128]

                    @block.gpsimd
                    def _(gp, gsrc=gsrc, usrc=usrc, par=par, ntg=c_tg.n):
                        if ntg >= 4:
                            gp.wait_ge(c_tg.sem, ntg - 3)
                        c_wl.inc(gp.dma_start(wgu_sb[:, par, :, 0:128],
                                              gsrc.rearrange("(dc p) f -> p dc f", p=128)), 16)
                        c_wl.inc(gp.dma_start(wgu_sb[:, par, :, 128:256],
                                              usrc.rearrange("(dc p) f -> p dc f", p=128)), 16)

                    for ts in range(2):
                        sl = slice(h0 + ts * 512, h0 + (ts + 1) * 512)

                        @block.tensor
                        def _(t, par=par, ts=ts, sl=sl, nwl=c_wl.n):
                            if ts == 0:
                                t.wait_ge(c_wl.sem, nwl)
                            if pg_free[ts] is not None:
                                s_n, v_n = pg_free[ts]
                                t.wait_ge(c_s5.sem, s_n)
                                t.wait_ge(c_v5.sem, v_n)
                            mm = None
                            for dc in range(16):
                                mm = t.matmul(pg[ts][:, :], wgu_sb[:, par, dc, 0:128],
                                              xnq_f[:, dc, sl], start=(dc == 0), stop=(dc == 15))
                            for dc in range(16):
                                mm = t.matmul(pu[ts][:, :], wgu_sb[:, par, dc, 128:256],
                                              xnq_f[:, dc, sl], start=(dc == 0), stop=(dc == 15))
                            c_tg.inc(mm)

                        @block.scalar
                        def _(sc, ts=ts, ntg=c_tg.n):
                            sc.wait_ge(c_tg.sem, ntg)
                            c_s5.inc(sc.activation(t1s[:, ts, :], pg[ts][:, :], AF.Silu))

                        @block.vector
                        def _(v, c=c, ts=ts, ntg=c_tg.n, ns=c_s5.n, routed=routed):
                            v.wait_ge(c_tg.sem, ntg)
                            if routed:
                                v.wait_ge(c_vb.sem, c_vb.n)
                                c_v5.inc(v.tensor_tensor(t2s[:, 0, :], pu[ts][:, :],
                                                         cbc[:, ts * 512:(ts + 1) * 512], ALU.mult))
                                v.wait_ge(c_v5.sem, c_v5.n)
                                v.wait_ge(c_s5.sem, ns)
                                c_v5.inc(v.tensor_tensor(hs[:, c, ts * 512:(ts + 1) * 512],
                                                         t1s[:, ts, :], t2s[:, 0, :], ALU.mult))
                            else:
                                v.wait_ge(c_s5.sem, ns)
                                c_v5.inc(v.tensor_tensor(hs[:, c, ts * 512:(ts + 1) * 512],
                                                         t1s[:, ts, :], pu[ts][:, :], ALU.mult))
                        pg_free[ts] = (c_s5.n, c_v5.n)

                for fg in range(8):
                    par = (exi * 8 + fg) % 2
                    if kind == "e":
                        dsrc = wdn[ei][:, fg * 256:(fg + 1) * 256]
                    elif kind == "sA":
                        dsrc = sdn[0:I_FF, fg * 256:(fg + 1) * 256]
                    else:
                        dsrc = sdn[I_FF:SI, fg * 256:(fg + 1) * 256]

                    @block.gpsimd
                    def _(gp, dsrc=dsrc, par=par, nty=c_ty.n):
                        if nty >= 2:
                            gp.wait_ge(c_ty.sem, nty - 1)
                        c_wl.inc(gp.dma_start(wd_sb[:, par, :, :],
                                              dsrc.rearrange("(ic p) f -> p ic f", p=128)), 16)

                    for fc in range(2):
                        for ts in range(2):
                            sl = slice(ts * 512, (ts + 1) * 512)
                            ypar = ts

                            @block.tensor
                            def _(t, par=par, fc=fc, ts=ts, sl=sl, nwl=c_wl.n, nv=c_v5.n, fg=fg):
                                if fc == 0 and ts == 0:
                                    t.wait_ge(c_wl.sem, nwl)
                                    if fg == 0:
                                        t.wait_ge(c_v5.sem, nv)
                                if py_free[ts] is not None:
                                    t.wait_ge(c_vy.sem, py_free[ts])
                                mm = None
                                for ic in range(11):
                                    mm = t.matmul(py[ts][:, :], wd_sb[:, par, ic, fc * 128:(fc + 1) * 128],
                                                  hs[:, ic, sl], start=(ic == 0), stop=(ic == 10))
                                c_ty.inc(mm)

                            @block.vector
                            def _(v, fg=fg, fc=fc, ts=ts, nty=c_ty.n, first=first, half=half):
                                v.wait_ge(c_ty.sem, nty)
                                ysl = Y[:, fg * 2 + fc, ts * 512:(ts + 1) * 512]
                                if first:
                                    c_vy.inc(v.tensor_copy(ysl, py[ts][:, :]))
                                else:
                                    c_vy.inc(v.tensor_tensor(ysl, ysl, py[ts][:, :], ALU.add))
                            py_free[ts] = c_vy.n

            mmul_hist = {}
            for dc in range(16):
                @block.scalar
                def _(sc, dc=dc, nvy=c_vy.n):
                    sc.wait_ge(c_vy.sem, nvy)
                    if (dc - 2) in mmul_hist:
                        sc.wait_ge(c_v5.sem, mmul_hist[dc - 2])
                    c_s5.inc(sc.activation(g_sb[:, dc % 2, :], Y[:, dc, :], AF.Silu))

                @block.sync
                def _(s, dc=dc, half=half):
                    if (dc - 1) in mmul_hist:
                        s.wait_ge(c_v5.sem, mmul_hist[dc - 1])
                    c_sld.inc(s.dma_start(ss_sb[:, 0, :],
                                          s_scr[dc * 128:(dc + 1) * 128, h0:h0 + 1024]), 16)

                @block.vector
                def _(v, dc=dc, ns=c_s5.n, nsl=c_sld.n, half=half):
                    v.wait_ge(c_s5.sem, ns)
                    v.wait_ge(c_sld.sem, nsl)
                    if c_mst.n >= 16:
                        v.wait_ge(c_mst.sem, c_mst.n)
                    c_v5.inc(v.tensor_tensor(m_sb[:, 0, :], g_sb[:, dc % 2, :],
                                             ss_sb[:, 0, :], ALU.mult))
                mmul_hist[dc] = c_v5.n

                @block.scalar
                def _(sc, dc=dc, half=half, nv=c_v5.n):
                    sc.wait_ge(c_v5.sem, nv)
                    c_mst.inc(sc.dma_start(m_scr[dc * 128:(dc + 1) * 128, h0:h0 + 1024],
                                           m_sb[:, 0, :]), 16)

        @block.scalar
        def _(sc):
            sc.wait_ge(c_mst.sem, 16 * 32)


def emit_attention_pass(nc, block, pas, g):
    cst, ones_row, ones_rf = g["cst"], g["ones_row"], g["ones_rf"]
    ones_bf, masksb, identb = g["ones_bf"], g["masksb"], g["identb"]
    wq, wk, wv = g["wq"], g["wk"], g["wv"]
    xn_scr, xnq_scr, ropeK, ropeQ, oat_scr = g["xn_scr"], g["xnq_scr"], g["ropeK"], g["ropeQ"], g["oat_scr"]
    mk_sem = g["mk_sem"]
    P = f"a{pas}_"
    with contextlib.ExitStack() as ps:
        sb = lambda n, shp, dt: ps.enter_context(nc.sbuf_tensor(P + n, shp, dt))
        pt_ = lambda n, shp: ps.enter_context(nc.psum_tensor(P + n, shp, F32))
        sem = lambda n: Ctr(mk_sem(P + n))

        q_all = sb("q_all", [128, 4, 8, 2, 256], BF16)
        kcache = None
        vcache = None

        with contextlib.ExitStack() as qs:
            sbq = lambda n, shp, dt: qs.enter_context(nc.sbuf_tensor(P + "q" + n, shp, dt))
            wq_sb = sbq("w", [128, 16, 1024], BF16)
            xnq_f = sbq("xn", [128, 16, NT], BF16)
            ropeq = sbq("rope", [128, 2, NT], F32)
            qsw = sbq("sw", [128, 512], F32)
            qt1 = sbq("t1", [128, 512], F32)
            qt2 = sbq("t2", [128, 512], F32)
            pq = [qs.enter_context(nc.psum_tensor(P + "qp0", [128, 512], F32)),
                  qs.enter_context(nc.psum_tensor(P + "qp1", [128, 512], F32))]
            c_ld = sem("qld")
            c_mm = sem("qmm")
            c_v = sem("qv")

            @block.gpsimd
            def _(gp):
                c_ld.inc(gp.dma_start(wq_sb[:, :, :],
                                      wq[:, pas * 1024:(pas + 1) * 1024].rearrange("(dc p) c -> p dc c", p=128)), 16)

            @block.sync
            def _(s):
                c_ld.inc(s.dma_start(ropeq[:, 0, :], ropeQ[0]), 16)
                c_ld.inc(s.dma_start(ropeq[:, 1, :], ropeQ[1]), 16)
                for dc in range(16):
                    for q4 in range(4):
                        c_ld.inc(s.dma_start(xnq_f[:, dc, q4 * 512:(q4 + 1) * 512],
                                             xnq_scr[q4 * 512:(q4 + 1) * 512, dc * 128:(dc + 1) * 128],
                                             transpose=True), 16)

            n_chunk = 0
            for h in range(8):
                for q4 in range(4):
                    par = n_chunk % 2
                    sl = slice(q4 * 512, (q4 + 1) * 512)

                    @block.tensor
                    def _(t, h=h, par=par, sl=sl, n=n_chunk):
                        if n == 0:
                            t.wait_ge(c_ld.sem, 16 * 67)
                        if n >= 2:
                            t.wait_ge(c_v.sem, 3 * (n - 2) + 2)
                        mm = None
                        for dc in range(16):
                            mm = t.matmul(pq[par][:, :], wq_sb[:, dc, h * 128:(h + 1) * 128],
                                          xnq_f[:, dc, sl], start=(dc == 0), stop=(dc == 15))
                        c_mm.inc(mm)

                    @block.vector
                    def _(v, h=h, par=par, sl=sl, n=n_chunk):
                        v.wait_ge(c_mm.sem, n + 1)
                        v.tensor_scalar(qsw[0:64, :], pq[par][64:128, :], cst[0:64, 2:3], None, ALU.mult)
                        c_v.inc(v.tensor_copy(qsw[64:128, :], pq[par][0:64, :]))
                        v.tensor_tensor(qt1[:, :], pq[par][:, :], ropeq[:, 0, sl], ALU.mult)
                        v.wait_ge(c_v.sem, 3 * n + 1)
                        c_v.inc(v.tensor_tensor(qt2[:, :], qsw[:, :], ropeq[:, 1, sl], ALU.mult))
                        v.wait_ge(c_v.sem, 3 * n + 2)
                        dst = q_all[:, h // 2, 2 * q4:2 * q4 + 2, h % 2, :]
                        c_v.inc(v.tensor_tensor(dst, qt1[:, :].rearrange("p (e c) -> p e c", c=256),
                                                qt2[:, :].rearrange("p (e c) -> p e c", c=256), ALU.add))
                    n_chunk += 1

            @block.tensor
            def _(t):
                t.wait_ge(c_v.sem, 3 * 32)
            c_qv = c_v

        if g.get("dbg"):
            @block.sync
            def _(s):
                s.wait_ge(c_qv.sem, 96)
                s.dma_start(g["dbg"]["q"], q_all[:, :, :]).then_inc(c_qv.sem, 16)
            c_qv.n += 16

        kcache = sb("kcache", [128, 2, S], BF16)
        vcache = sb("vcache", [128, 64, 256], BF16)
        with contextlib.ExitStack() as ks:
            sbk = lambda n, shp, dt: ks.enter_context(nc.sbuf_tensor(P + "k" + n, shp, dt))
            wk_sb = sbk("wk", [128, 16, 256], BF16)
            wv_sb = sbk("wv", [128, 16, 256], BF16)
            xnb_f = sbk("xnb", [128, 16, 1024], BF16)
            ropek = sbk("rope", [128, 2, 1024], F32)
            ksw = sbk("sw", [128, 512], F32)
            kt1 = sbk("t1", [128, 512], F32)
            kt2 = sbk("t2", [128, 512], F32)
            pk = [ks.enter_context(nc.psum_tensor(P + "kp0", [128, 512], F32)),
                  ks.enter_context(nc.psum_tensor(P + "kp1", [128, 512], F32))]
            pv = [ks.enter_context(nc.psum_tensor(P + "vp0", [128, 256], F32)),
                  ks.enter_context(nc.psum_tensor(P + "vp1", [128, 256], F32))]
            c_ld = sem("kld")
            c_mmk = sem("kmmk")
            c_mmv = sem("kmmv")
            c_vk = sem("kvk")
            c_vv = sem("kvv")

            @block.gpsimd
            def _(gp):
                gp.wait_ge(c_qv.sem, 96)
                c_ld.inc(gp.dma_start(wk_sb[:, :, :],
                                      wk[:, pas * 256:(pas + 1) * 256].rearrange("(dc p) c -> p dc c", p=128)), 16)
                c_ld.inc(gp.dma_start(wv_sb[:, :, :],
                                      wv[:, pas * 256:(pas + 1) * 256].rearrange("(dc p) c -> p dc c", p=128)), 16)
                gp.wait_ge(c_ld.sem, 32)

            for kb in range(8):
                @block.sync
                def _(s, kb=kb):
                    if kb == 0:
                        s.wait_ge(c_qv.sem, 3 * 32)
                    if kb > 0:
                        s.wait_ge(c_mmv.sem, 8 * kb)
                    for dc in range(16):
                        c_ld.inc(s.dma_start(xnb_f[:, dc, :],
                                             xn_scr[kb * 1024:(kb + 1) * 1024, dc * 128:(dc + 1) * 128],
                                             transpose=True), 16)
                    c_ld.inc(s.dma_start(ropek[:, 0, :], ropeK[0][:, kb * 1024:(kb + 1) * 1024]), 16)
                    c_ld.inc(s.dma_start(ropek[:, 1, :], ropeK[1][:, kb * 1024:(kb + 1) * 1024]), 16)

                for kvl in range(2):
                    for kss in range(2):
                        n = kb * 4 + kvl * 2 + kss
                        par = n % 2
                        sl = slice(kss * 512, (kss + 1) * 512)

                        @block.tensor
                        def _(t, kb=kb, kvl=kvl, n=n, par=par, sl=sl):
                            if kvl == 0 and kss == 0:
                                t.wait_ge(c_ld.sem, 16 * (2 + 18 * (kb + 1)))
                            if n >= 2:
                                t.wait_ge(c_vk.sem, 3 * (n - 2) + 2)
                            mm = None
                            for dc in range(16):
                                mm = t.matmul(pk[par][:, :], wk_sb[:, dc, kvl * 128:(kvl + 1) * 128],
                                              xnb_f[:, dc, sl], start=(dc == 0), stop=(dc == 15))
                            c_mmk.inc(mm)

                        @block.vector
                        def _(v, kb=kb, kvl=kvl, kss=kss, n=n, par=par, sl=sl):
                            v.wait_ge(c_mmk.sem, n + 1)
                            v.tensor_scalar(ksw[0:64, :], pk[par][64:128, :], cst[0:64, 2:3], None, ALU.mult)
                            c_vk.inc(v.tensor_copy(ksw[64:128, :], pk[par][0:64, :]))
                            v.tensor_tensor(kt1[:, :], pk[par][:, :], ropek[:, 0, sl], ALU.mult)
                            v.wait_ge(c_vk.sem, 3 * n + 1)
                            c_vk.inc(v.tensor_tensor(kt2[:, :], ksw[:, :], ropek[:, 1, sl], ALU.mult))
                            v.wait_ge(c_vk.sem, 3 * n + 2)
                            c_vk.inc(v.tensor_tensor(
                                kcache[:, kvl, kb * 1024 + kss * 512:kb * 1024 + (kss + 1) * 512],
                                kt1[:, :], kt2[:, :], ALU.add))

                for tc in range(8):
                    n = kb * 8 + tc
                    par = n % 2

                    @block.tensor
                    def _(t, kb=kb, tc=tc, n=n, par=par):
                        if n >= 2:
                            t.wait_ge(c_vv.sem, n - 1)
                        mm = None
                        for dc in range(16):
                            mm = t.matmul(pv[par][:, :], xnb_f[:, dc, tc * 128:(tc + 1) * 128],
                                          wv_sb[:, dc, :], start=(dc == 0), stop=(dc == 15))
                        c_mmv.inc(mm)

                    @block.vector
                    def _(v, kb=kb, tc=tc, n=n, par=par):
                        v.wait_ge(c_mmv.sem, n + 1)
                        c_vv.inc(v.tensor_copy(vcache[:, kb * 8 + tc, :], pv[par][:, :]))

            if g.get("dbg"):
                @block.sync
                def _(s):
                    s.wait_ge(c_vk.sem, 96)
                    s.wait_ge(c_vv.sem, 64)
                    s.dma_start(g["dbg"]["k"], kcache[:, :, :]).then_inc(c_vk.sem, 16)
                    s.dma_start(g["dbg"]["v"], vcache[:, :, :]).then_inc(c_vk.sem, 16)
                c_vk.n += 32

            @block.tensor
            def _(t):
                t.wait_ge(c_vk.sem, 96)
                t.wait_ge(c_vv.sem, 64)
            @block.scalar
            def _(sc):
                sc.wait_ge(c_vk.sem, 96)
                sc.wait_ge(c_vv.sem, 64)

        with contextlib.ExitStack() as ss:
            sbs = lambda n, shp, dt: ss.enter_context(nc.sbuf_tensor(P + "s" + n, shp, dt))
            pt_sb = sbs("pt", [128, 2, 8, 512], BF16)
            l_sb = sbs("l", [2, 1, 512], F32)
            rl_sb = sbs("rl", [2, 1, 512], F32)
            rlb = sbs("rlb", [128, 512], F32)
            obuf = sbs("ob", [128, 2, 512], BF16)
            pol = [ss.enter_context(nc.psum_tensor(P + "ol0", [128, 512], F32)),
                  ss.enter_context(nc.psum_tensor(P + "ol1", [128, 512], F32))]
            pst = [ss.enter_context(nc.psum_tensor(P + "st%d" % i, [128, 512], F32)) for i in range(4)]
            plbc = ss.enter_context(nc.psum_tensor(P + "lbc", [128, 1024], F32))
            c_sc = sem("ssc")
            c_tst = sem("stst")
            c_tav = sem("stav")
            c_vt = sem("svt")
            c_vr = sem("svr")
            c_tb = sem("stb")
            c_sto = sem("ssto")

            gg0 = [0]
            last_pt_user = {}
            LAG = 2

            neh = 0
            ehps = [(e, hp) for e in range(1, 9) for hp in range(4)]
            for e, hp in ehps:
                kvl = hp // 2
                sc0 = c_sc.n
                tst0 = c_tst.n
                tav0 = c_tav.n
                g0 = gg0[0]
                nr = 8 * e
                q_rhs = q_all[:, hp, e - 1, :, :]

                pt_waits = {}
                for g in range(nr):
                    kb, kc = g // 8, g % 8
                    pt_waits[g] = last_pt_user.get((kb % 2, kc), 0)
                    last_pt_user[(kb % 2, kc)] = tav0 + g + 1
                gg0[0] += nr

                @block.tensor
                def _(t, e=e, hp=hp, kvl=kvl, q_rhs=q_rhs, neh=neh, sc0=sc0, g0=g0, nr=nr):
                    if neh >= 1:
                        t.wait_ge(c_vt.sem, 3 * (neh - 1) + 2)
                        c_tb.inc(t.matmul(plbc[:, 0:512], ones_rf[:, :], rl_sb[(neh - 1) % 2, :, :],
                                          start=True, stop=True))
                    if neh >= 2:
                        t.wait_ge(c_vt.sem, 3 * (neh - 1))
                        t.wait_ge(c_vt.sem, 3 * (neh - 1) + 1)

                    def emit_st(g):
                        kb, kc = g // 8, g % 8
                        diag = (kb == e - 1)
                        gg = g0 + g
                        if gg >= 4:
                            t.wait_ge(c_sc.sem, gg - 3)
                        mm = t.matmul(pst[gg % 4][:, :],
                                      kcache[:, kvl, kb * 1024 + kc * 128:kb * 1024 + (kc + 1) * 128],
                                      q_rhs, start=True, stop=(not diag), skip_group_check=True)
                        if diag:
                            mm = t.matmul(pst[gg % 4][:, :], identb[:, :], masksb[:, kc, :],
                                          start=False, stop=True, skip_group_check=True)
                        c_tst.inc(mm)

                    def emit_av(g):
                        kb, kc = g // 8, g % 8
                        t.wait_ge(c_sc.sem, sc0 + g + 1)
                        first = (g == 0)
                        last = (g == nr - 1)
                        t.matmul(pol[neh % 2][:, :],
                                 vcache[:, kb * 8 + kc, kvl * 128:(kvl + 1) * 128],
                                 pt_sb[:, kb % 2, kc, :],
                                 start=first, stop=False, skip_group_check=True)
                        mm = t.matmul(plbc[0:1, 512:1024], ones_bf[:, :],
                                      pt_sb[:, kb % 2, kc, :],
                                      start=first, stop=last, skip_group_check=True)
                        c_tav.inc(mm)

                    for g in range(nr):
                        emit_st(g)
                        if g >= LAG:
                            emit_av(g - LAG)
                    for g in range(nr - LAG, nr):
                        emit_av(g)

                @block.scalar
                def _(sc, e=e, neh=neh, sc0=sc0, tst0=tst0, g0=g0, nr=nr, pt_waits=pt_waits):
                    for g in range(nr):
                        kb, kc = g // 8, g % 8
                        gg = g0 + g
                        pw = pt_waits[g]
                        if pw > 0:
                            sc.wait_ge(c_tav.sem, pw)
                        sc.wait_ge(c_tst.sem, tst0 + g + 1)
                        c_sc.inc(sc.activation(pt_sb[:, kb % 2, kc, :], pst[gg % 4][:, :],
                                               AF.Exp, bias=cst[:, 6:7], scale=cst[:, 5:6]))

                @block.vector
                def _(v, e=e, neh=neh, tav0=tav0, nr=nr):
                    v.wait_ge(c_tav.sem, tav0 + nr)
                    c_vt.inc(v.tensor_copy(l_sb[neh % 2, :, :], plbc[0:1, 512:1024]))
                    v.wait_ge(c_vt.sem, 3 * neh + 1)
                    c_vt.inc(v.reciprocal(rl_sb[neh % 2, :, :], l_sb[neh % 2, :, :]))
                    v.wait_ge(c_tb.sem, neh + 1)
                    c_vr.inc(v.tensor_copy(rlb[:, :], plbc[:, 0:512]))
                    v.wait_ge(c_vr.sem, neh + 1)
                    if neh >= 2:
                        v.wait_ge(c_sto.sem, 32 * (neh - 1))
                    c_vt.inc(v.tensor_tensor(obuf[:, neh % 2, :], pol[neh % 2][:, :],
                                             rlb[:, :], ALU.mult))

                @block.sync
                def _(s, e=e, hp=hp, neh=neh):
                    s.wait_ge(c_vt.sem, 3 * (neh + 1))
                    h0g = pas * 8 + 2 * hp
                    c_sto.inc(s.dma_start(oat_scr[h0g * 128:(h0g + 1) * 128, (e - 1) * 256:e * 256],
                                          obuf[:, neh % 2, 0:256]), 16)
                    c_sto.inc(s.dma_start(oat_scr[(h0g + 1) * 128:(h0g + 2) * 128, (e - 1) * 256:e * 256],
                                          obuf[:, neh % 2, 256:512]), 16)
                neh += 1

            @block.tensor
            def _(t):
                t.wait_ge(c_vt.sem, 3 * 31 + 2)
                c_tb.inc(t.matmul(plbc[:, 0:512], ones_rf[:, :], rl_sb[31 % 2, :, :],
                                  start=True, stop=True))

            @block.sync
            def _(s):
                s.wait_ge(c_sto.sem, 32 * 32)

            @block.vector
            def _(v):
                v.wait_ge(c_vt.sem, 3 * 32)

            @block.scalar
            def _(sc):
                sc.wait_ge(c_sc.sem, 2304)

            @block.tensor
            def _(t):
                t.wait_ge(c_tb.sem, 32)


def emit_ffn(nc, block, g):
    cst, ones_rf, ident, mk_sem = g["cst"], g["ones_rf"], g["ident"], g["mk_sem"]
    gwT, wgu, wdn, sgu, sdn = g["gwT"], g["wgu"], g["wdn"], g["sgu"], g["sdn"]
    xnq_scr, s_scr, m_scr, cmb_scr = g["xnq_scr"], g["s_scr"], g["m_scr"], g["cmb_scr"]
    xnqlo_scr = g["xnqlo_scr"]
    with contextlib.ExitStack() as ps:
        sb = lambda n, shp, dt: ps.enter_context(nc.sbuf_tensor("f_" + n, shp, dt))
        xnq_f = sb("xn", [128, 16, NT], BF16)
        c_ldx = Ctr(mk_sem("fldx"))
        c_rt = Ctr(mk_sem("frt"))
        c_rv = Ctr(mk_sem("frv"))
        c_rs = Ctr(mk_sem("frs"))
        c_cst = Ctr(mk_sem("fcst"))

        @block.sync
        def _(s):
            for dc in range(16):
                for q4 in range(4):
                    c_ldx.inc(s.dma_start(xnq_f[:, dc, q4 * 512:(q4 + 1) * 512],
                                          xnq_scr[q4 * 512:(q4 + 1) * 512, dc * 128:(dc + 1) * 128],
                                          transpose=True), 16)

        with contextlib.ExitStack() as rs:
            sbr = lambda n, shp, dt: rs.enter_context(nc.sbuf_tensor("fr" + n, shp, dt))
            cmbT = sbr("cT", [16, NT], F32)
            xnqlo_f = sbr("xlo", [128, 16, NT], BF16)
            gw_sb = sbr("gw", [128, 16, 16], BF16)
            gwf_sb = sbr("gwf", [128, 16, 16], F32)
            gwlo_sb = sbr("gwlo", [128, 16, 16], BF16)
            lg = sbr("lg", [128, 16], F32)
            gm = sbr("gm", [128, 4], F32)
            t6 = sbr("t6", [128, 8], F32)
            keep = sbr("keep", [128, 4], F32)
            ml = sbr("ml", [128, 16], F32)
            ex = sbr("ex", [128, 16], F32)
            red = sbr("red", [128, 4], F32)
            probs = sbr("probs", [128, 16], F32)
            vv = sbr("vv", [128, 8], F32)
            ind = sbr("ind", [128, 4], F32)
            eqs = sbr("eqs", [128, 16], F32)
            work = sbr("work", [128, 16], F32)
            work2 = sbr("work2", [128, 16], F32)
            strict = sbr("strict", [128, 16], F32)
            cums = sbr("cums", [128, 16], F32)
            cums2 = sbr("cums2", [128, 16], F32)
            sel = sbr("sel", [128, 16], F32)
            cmb = sbr("cmb", [128, 16], F32)
            plog = rs.enter_context(nc.psum_tensor("frpl", [128, 16], F32))
            ptp = rs.enter_context(nc.psum_tensor("frpt", [16, 128], F32))

            @block.gpsimd
            def _(gp):
                c_ldx.inc(gp.dma_start(gw_sb[:, :, :], gwT.rearrange("(dc p) e -> p dc e", p=128)), 16)

            @block.sync
            def _(s):
                c_ldx.inc(s.dma_start(gwf_sb[:, :, :], gwT.rearrange("(dc p) e -> p dc e", p=128)), 16)
                for dc in range(16):
                    for q4 in range(4):
                        c_ldx.inc(s.dma_start(xnqlo_f[:, dc, q4 * 512:(q4 + 1) * 512],
                                              xnqlo_scr[q4 * 512:(q4 + 1) * 512, dc * 128:(dc + 1) * 128],
                                              transpose=True), 16)

            @block.vector
            def _(v):
                v.wait_ge(c_ldx.sem, 16 * 66)
                c_rv.inc(v.tensor_sub(gwlo_sb[:, :, :], gwf_sb[:, :, :], gw_sb[:, :, :]))
                v.wait_ge(c_rv.sem, c_rv.n)

            rv_marks = []
            for tt in range(16):
                @block.tensor
                def _(t, tt=tt):
                    if tt == 0:
                        t.wait_ge(c_ldx.sem, 16 * 130)
                        t.wait_ge(c_rv.sem, 1)
                    if tt >= 1:
                        t.wait_ge(c_rv.sem, rv_marks[tt - 1]["end"])
                    mm = None
                    sl = slice(tt * 128, (tt + 1) * 128)
                    for dc in range(16):
                        t.matmul(plog[:, :], xnq_f[:, dc, sl], gw_sb[:, dc, :],
                                 start=(dc == 0), stop=False, skip_group_check=True)
                    for dc in range(16):
                        t.matmul(plog[:, :], xnq_f[:, dc, sl], gwlo_sb[:, dc, :],
                                 start=False, stop=False, skip_group_check=True)
                    for dc in range(16):
                        mm = t.matmul(plog[:, :], xnqlo_f[:, dc, sl], gw_sb[:, dc, :],
                                      start=False, stop=(dc == 15), skip_group_check=True)
                    c_rt.inc(mm)

                marks = {}
                rv_marks.append(marks)

                @block.vector
                def _(v, tt=tt, marks=marks):
                    def stepw(inst):
                        c_rv.inc(inst)
                        v.wait_ge(c_rv.sem, c_rv.n)
                    v.wait_ge(c_rt.sem, 2 * tt + 1)
                    stepw(v.tensor_copy(lg[:, :], plog[:, :]))
                    lgv = lg[:, :].rearrange("p (g j) -> p g j", j=4)
                    v.tensor_tensor(t6[:, 0:4], lgv[:, :, 0], lgv[:, :, 1], ALU.max)
                    stepw(v.tensor_tensor(keep[:, 0:4], lgv[:, :, 2], lgv[:, :, 3], ALU.max))
                    stepw(v.tensor_tensor(gm[:, :], t6[:, 0:4], keep[:, :], ALU.max))
                    v.tensor_tensor(t6[:, 0:1], gm[:, 0:1], gm[:, 1:2], ALU.max)
                    v.tensor_tensor(t6[:, 1:2], gm[:, 0:1], gm[:, 1:2], ALU.min)
                    v.tensor_tensor(t6[:, 2:3], gm[:, 2:3], gm[:, 3:4], ALU.max)
                    stepw(v.tensor_tensor(t6[:, 3:4], gm[:, 2:3], gm[:, 3:4], ALU.min))
                    v.tensor_tensor(t6[:, 4:5], t6[:, 0:1], t6[:, 2:3], ALU.min)
                    stepw(v.tensor_tensor(t6[:, 5:6], t6[:, 1:2], t6[:, 3:4], ALU.max))
                    stepw(v.tensor_tensor(t6[:, 6:7], t6[:, 4:5], t6[:, 5:6], ALU.max))
                    stepw(v.tensor_scalar(keep[:, :], gm[:, :], t6[:, 6:7], None, ALU.is_ge))
                    for j in range(4):
                        inst = v.tensor_tensor(ml[:, :].rearrange("p (g j) -> p g j", j=4)[:, :, j],
                                               lgv[:, :, j], keep[:, :], ALU.mult)
                    stepw(inst)
                    stepw(v.reduce_max(red[:, 0:1], ml[:, :], axis=AX.X, negate=True))
                    marks["rm"] = c_rv.n
                    v.wait_ge(c_rs.sem, tt + 1)
                    stepw(v.reduce_sum(red[:, 1:2], ex[:, :], axis=AX.X))
                    stepw(v.reciprocal(red[:, 2:3], red[:, 1:2]))
                    stepw(v.tensor_scalar(probs[:, :], ex[:, :], red[:, 2:3], None, ALU.mult))
                    stepw(v.tensor_copy(work[:, :], probs[:, :]))
                    for r in range(4):
                        stepw(v.reduce_max(vv[:, r:r + 1], work[:, :], axis=AX.X))
                        if r < 3:
                            stepw(v.tensor_scalar(eqs[:, :], work[:, :], vv[:, r:r + 1], None, ALU.is_ge))
                            stepw(v.reduce_sum(vv[:, 4 + r:5 + r], eqs[:, :], axis=AX.X))
                            if r > 0:
                                stepw(v.tensor_tensor(vv[:, 4 + r:5 + r], vv[:, 4 + r:5 + r],
                                                      vv[:, 3 + r:4 + r], ALU.add))
                            v.scalar_tensor_tensor(work2[:, :], eqs[:, :], cst[:, 2:3], work[:, :],
                                                   ALU.mult, ALU.add)
                            stepw(v.nop())
                            stepw(v.tensor_copy(work[:, :], work2[:, :]))
                    v.tensor_scalar(ind[:, 0:1], vv[:, 4:5], cst[:, 3:4], None, ALU.is_ge)
                    v.tensor_scalar(ind[:, 1:2], vv[:, 5:6], cst[:, 3:4], None, ALU.is_ge)
                    stepw(v.tensor_scalar(ind[:, 2:3], vv[:, 6:7], cst[:, 3:4], None, ALU.is_ge))
                    stepw(v.tensor_tensor(ind[:, 1:2], ind[:, 1:2], ind[:, 0:1], ALU.subtract))
                    v.tensor_tensor(ind[:, 2:3], ind[:, 2:3], ind[:, 0:1], ALU.subtract)
                    stepw(v.nop())
                    stepw(v.tensor_tensor(ind[:, 2:3], ind[:, 2:3], ind[:, 1:2], ALU.subtract))
                    v.tensor_tensor(vv[:, 7:8], ind[:, 0:1], vv[:, 0:1], ALU.mult)
                    stepw(v.tensor_tensor(t6[:, 7:8], ind[:, 1:2], vv[:, 1:2], ALU.mult))
                    stepw(v.tensor_tensor(vv[:, 7:8], vv[:, 7:8], t6[:, 7:8], ALU.add))
                    stepw(v.tensor_tensor(t6[:, 7:8], ind[:, 2:3], vv[:, 2:3], ALU.mult))
                    stepw(v.tensor_tensor(vv[:, 7:8], vv[:, 7:8], t6[:, 7:8], ALU.add))
                    stepw(v.tensor_tensor(t6[:, 7:8], ind[:, 0:1], ind[:, 1:2], ALU.add))
                    stepw(v.tensor_tensor(t6[:, 7:8], t6[:, 7:8], ind[:, 2:3], ALU.add))
                    stepw(v.scalar_tensor_tensor(t6[:, 7:8], t6[:, 7:8], cst[:, 2:3], cst[:, 1:2],
                                                 ALU.mult, ALU.add))
                    stepw(v.tensor_tensor(t6[:, 7:8], t6[:, 7:8], vv[:, 3:4], ALU.mult))
                    stepw(v.tensor_tensor(vv[:, 7:8], vv[:, 7:8], t6[:, 7:8], ALU.add))
                    v.tensor_scalar(strict[:, :], probs[:, :], vv[:, 7:8], None, ALU.is_gt)
                    stepw(v.tensor_scalar(eqs[:, :], probs[:, :], vv[:, 7:8], None, ALU.is_equal))
                    stepw(v.reduce_sum(ind[:, 3:4], strict[:, :], axis=AX.X, negate=True))
                    stepw(v.tensor_scalar(ind[:, 3:4], ind[:, 3:4], cst[:, 3:4], None, ALU.add))
                    stepw(v.tensor_copy(cums[:, :], eqs[:, :]))
                    for k, (s_, d_) in zip([1, 2, 4, 8],
                                           [(cums, cums2), (cums2, cums), (cums, cums2), (cums2, cums)]):
                        v.tensor_copy(d_[:, 0:k], s_[:, 0:k])
                        stepw(v.tensor_tensor(d_[:, k:16], s_[:, k:16], s_[:, 0:16 - k], ALU.add))
                    stepw(v.tensor_scalar(cums2[:, :], cums[:, :], ind[:, 3:4], None, ALU.is_le))
                    stepw(v.tensor_tensor(sel[:, :], eqs[:, :], cums2[:, :], ALU.mult))
                    stepw(v.tensor_tensor(sel[:, :], sel[:, :], strict[:, :], ALU.add))
                    stepw(v.scalar_tensor_tensor(cmb[:, :], sel[:, :], cst[:, 4:5], probs[:, :],
                                                 ALU.mult, ALU.mult))
                    marks["cmb"] = c_rv.n
                    v.wait_ge(c_rt.sem, 2 * (tt + 1))
                    stepw(v.tensor_copy(cmbT[:, tt * 128:(tt + 1) * 128], ptp[0:16, :]))
                    marks["end"] = c_rv.n

                @block.scalar
                def _(sc, tt=tt, marks=marks):
                    sc.wait_ge(c_rv.sem, marks["rm"])
                    c_rs.inc(sc.activation(ex[:, :], ml[:, :], AF.Exp, bias=red[:, 0:1]))

                @block.tensor
                def _(t, tt=tt, marks=marks):
                    t.wait_ge(c_rv.sem, marks["cmb"])
                    c_rt.inc(t.transpose(ptp[0:16, 0:128], cmb[:, 0:16], ident[:, :]))

            @block.sync
            def _(s):
                s.wait_ge(c_rv.sem, rv_marks[-1]["end"])
                c_cst.inc(s.dma_start(cmb_scr, cmbT[:, :]), 16)

        Y = sb("Y", [128, 16, 1024], F32)
        hs = sb("hs", [128, 11, 1024], BF16)
        wgu_sb = sb("wgu", [128, 3, 16, 256], BF16)
        wd_sb = sb("wd", [128, 2, 11, 256], BF16)
        cbc = sb("cbc", [128, 1024], F32)
        cr = sb("cr", [1, 1024], F32)
        t1s = sb("t1", [128, 2, 512], BF16)
        t2s = sb("t2", [128, 1, 512], F32)
        g_sb = sb("g", [128, 2, 1024], BF16)
        ss_sb = sb("ss", [128, 1, 1024], BF16)
        m_sb = sb("m", [128, 1, 1024], BF16)
        pg = [ps.enter_context(nc.psum_tensor("fpg%d" % i, [128, 512], F32)) for i in range(2)]
        pu = [ps.enter_context(nc.psum_tensor("fpu%d" % i, [128, 512], F32)) for i in range(2)]
        py = [ps.enter_context(nc.psum_tensor("fpy%d" % i, [128, 512], F32)) for i in range(2)]
        pbc2 = ps.enter_context(nc.psum_tensor("fpbc", [128, 512], F32))
        c_wl = Ctr(mk_sem("fwl"))
        c_tg = Ctr(mk_sem("ftg"))
        c_ty = Ctr(mk_sem("fty"))
        c_tb2 = Ctr(mk_sem("ftb"))
        c_s5 = Ctr(mk_sem("fs5"))
        c_v5 = Ctr(mk_sem("fv5"))
        c_vy = Ctr(mk_sem("fvy"))
        c_vb = Ctr(mk_sem("fvb"))
        c_crl = Ctr(mk_sem("fcrl"))
        c_sld = Ctr(mk_sem("fsld"))
        c_mst = Ctr(mk_sem("fmst"))

        experts = [("sA", None), ("sB", None)] + [("e", i) for i in range(NE)]

        for half in range(2):
            h0 = half * 1024
            for exi, (kind, ei) in enumerate(experts):
                first = exi == 0
                routed = kind == "e"
                if routed:
                    @block.sync
                    def _(s, ei=ei, half=half):
                        s.wait_ge(c_cst.sem, 16)
                        s.wait_ge(c_tb2.sem, c_tb2.n)
                        c_crl.inc(s.dma_start(cr[:, :], cmb_scr[ei:ei + 1, h0:h0 + 1024]), 16)

                    @block.tensor
                    def _(t):
                        t.wait_ge(c_crl.sem, c_crl.n)
                        t.wait_ge(c_vb.sem, c_vb.n)
                        c_tb2.inc(t.matmul(pbc2[:, :], ones_rf[:, :], cr[:, 0:512],
                                           start=True, stop=True))

                    @block.vector
                    def _(v):
                        v.wait_ge(c_tb2.sem, c_tb2.n)
                        c_vb.inc(v.tensor_copy(cbc[:, 0:512], pbc2[:, :]))

                    @block.tensor
                    def _(t):
                        t.wait_ge(c_vb.sem, c_vb.n)
                        c_tb2.inc(t.matmul(pbc2[:, :], ones_rf[:, :], cr[:, 512:1024],
                                           start=True, stop=True))

                    @block.vector
                    def _(v):
                        v.wait_ge(c_tb2.sem, c_tb2.n)
                        c_vb.inc(v.tensor_copy(cbc[:, 512:1024], pbc2[:, :]))

                if exi == 0 and half == 0:
                    pg_free = {0: None, 1: None}
                    py_free = {0: None, 1: None}
                for c in range(11):
                    par = (exi * 11 + c) % 3
                    if kind == "e":
                        gsrc = wgu[ei][:, c * 128:(c + 1) * 128]
                        usrc = wgu[ei][:, I_FF + c * 128:I_FF + (c + 1) * 128]
                    elif kind == "sA":
                        gsrc = sgu[:, c * 128:(c + 1) * 128]
                        usrc = sgu[:, SI + c * 128:SI + (c + 1) * 128]
                    else:
                        gsrc = sgu[:, I_FF + c * 128:I_FF + (c + 1) * 128]
                        usrc = sgu[:, SI + I_FF + c * 128:SI + I_FF + (c + 1) * 128]

                    @block.gpsimd
                    def _(gp, gsrc=gsrc, usrc=usrc, par=par, ntg=c_tg.n):
                        if ntg >= 4:
                            gp.wait_ge(c_tg.sem, ntg - 3)
                        c_wl.inc(gp.dma_start(wgu_sb[:, par, :, 0:128],
                                              gsrc.rearrange("(dc p) f -> p dc f", p=128)), 16)
                        c_wl.inc(gp.dma_start(wgu_sb[:, par, :, 128:256],
                                              usrc.rearrange("(dc p) f -> p dc f", p=128)), 16)

                    for ts in range(2):
                        sl = slice(h0 + ts * 512, h0 + (ts + 1) * 512)

                        @block.tensor
                        def _(t, par=par, ts=ts, sl=sl, nwl=c_wl.n):
                            if ts == 0:
                                t.wait_ge(c_wl.sem, nwl)
                            if pg_free[ts] is not None:
                                s_n, v_n = pg_free[ts]
                                t.wait_ge(c_s5.sem, s_n)
                                t.wait_ge(c_v5.sem, v_n)
                            mm = None
                            for dc in range(16):
                                mm = t.matmul(pg[ts][:, :], wgu_sb[:, par, dc, 0:128],
                                              xnq_f[:, dc, sl], start=(dc == 0), stop=(dc == 15))
                            for dc in range(16):
                                mm = t.matmul(pu[ts][:, :], wgu_sb[:, par, dc, 128:256],
                                              xnq_f[:, dc, sl], start=(dc == 0), stop=(dc == 15))
                            c_tg.inc(mm)

                        @block.scalar
                        def _(sc, ts=ts, ntg=c_tg.n):
                            sc.wait_ge(c_tg.sem, ntg)
                            c_s5.inc(sc.activation(t1s[:, ts, :], pg[ts][:, :], AF.Silu))

                        @block.vector
                        def _(v, c=c, ts=ts, ntg=c_tg.n, ns=c_s5.n, routed=routed):
                            v.wait_ge(c_tg.sem, ntg)
                            if routed:
                                v.wait_ge(c_vb.sem, c_vb.n)
                                c_v5.inc(v.tensor_tensor(t2s[:, 0, :], pu[ts][:, :],
                                                         cbc[:, ts * 512:(ts + 1) * 512], ALU.mult))
                                v.wait_ge(c_v5.sem, c_v5.n)
                                v.wait_ge(c_s5.sem, ns)
                                c_v5.inc(v.tensor_tensor(hs[:, c, ts * 512:(ts + 1) * 512],
                                                         t1s[:, ts, :], t2s[:, 0, :], ALU.mult))
                            else:
                                v.wait_ge(c_s5.sem, ns)
                                c_v5.inc(v.tensor_tensor(hs[:, c, ts * 512:(ts + 1) * 512],
                                                         t1s[:, ts, :], pu[ts][:, :], ALU.mult))
                        pg_free[ts] = (c_s5.n, c_v5.n)

                for fg in range(8):
                    par = (exi * 8 + fg) % 2
                    if kind == "e":
                        dsrc = wdn[ei][:, fg * 256:(fg + 1) * 256]
                    elif kind == "sA":
                        dsrc = sdn[0:I_FF, fg * 256:(fg + 1) * 256]
                    else:
                        dsrc = sdn[I_FF:SI, fg * 256:(fg + 1) * 256]

                    @block.gpsimd
                    def _(gp, dsrc=dsrc, par=par, nty=c_ty.n):
                        if nty >= 2:
                            gp.wait_ge(c_ty.sem, nty - 1)
                        c_wl.inc(gp.dma_start(wd_sb[:, par, :, :],
                                              dsrc.rearrange("(ic p) f -> p ic f", p=128)), 16)

                    for fc in range(2):
                        for ts in range(2):
                            sl = slice(ts * 512, (ts + 1) * 512)
                            ypar = ts

                            @block.tensor
                            def _(t, par=par, fc=fc, ts=ts, sl=sl, nwl=c_wl.n, nv=c_v5.n, fg=fg):
                                if fc == 0 and ts == 0:
                                    t.wait_ge(c_wl.sem, nwl)
                                    if fg == 0:
                                        t.wait_ge(c_v5.sem, nv)
                                if py_free[ts] is not None:
                                    t.wait_ge(c_vy.sem, py_free[ts])
                                mm = None
                                for ic in range(11):
                                    mm = t.matmul(py[ts][:, :], wd_sb[:, par, ic, fc * 128:(fc + 1) * 128],
                                                  hs[:, ic, sl], start=(ic == 0), stop=(ic == 10))
                                c_ty.inc(mm)

                            @block.vector
                            def _(v, fg=fg, fc=fc, ts=ts, nty=c_ty.n, first=first, half=half):
                                v.wait_ge(c_ty.sem, nty)
                                ysl = Y[:, fg * 2 + fc, ts * 512:(ts + 1) * 512]
                                if first:
                                    c_vy.inc(v.tensor_copy(ysl, py[ts][:, :]))
                                else:
                                    c_vy.inc(v.tensor_tensor(ysl, ysl, py[ts][:, :], ALU.add))
                            py_free[ts] = c_vy.n

            mmul_hist = {}
            for dc in range(16):
                @block.scalar
                def _(sc, dc=dc, nvy=c_vy.n):
                    sc.wait_ge(c_vy.sem, nvy)
                    if (dc - 2) in mmul_hist:
                        sc.wait_ge(c_v5.sem, mmul_hist[dc - 2])
                    c_s5.inc(sc.activation(g_sb[:, dc % 2, :], Y[:, dc, :], AF.Silu))

                @block.sync
                def _(s, dc=dc, half=half):
                    if (dc - 1) in mmul_hist:
                        s.wait_ge(c_v5.sem, mmul_hist[dc - 1])
                    c_sld.inc(s.dma_start(ss_sb[:, 0, :],
                                          s_scr[dc * 128:(dc + 1) * 128, h0:h0 + 1024]), 16)

                @block.vector
                def _(v, dc=dc, ns=c_s5.n, nsl=c_sld.n, half=half):
                    v.wait_ge(c_s5.sem, ns)
                    v.wait_ge(c_sld.sem, nsl)
                    if c_mst.n >= 16:
                        v.wait_ge(c_mst.sem, c_mst.n)
                    c_v5.inc(v.tensor_tensor(m_sb[:, 0, :], g_sb[:, dc % 2, :],
                                             ss_sb[:, 0, :], ALU.mult))
                mmul_hist[dc] = c_v5.n

                @block.scalar
                def _(sc, dc=dc, half=half, nv=c_v5.n):
                    sc.wait_ge(c_v5.sem, nv)
                    c_mst.inc(sc.dma_start(m_scr[dc * 128:(dc + 1) * 128, h0:h0 + 1024],
                                           m_sb[:, 0, :]), 16)

        @block.scalar
        def _(sc):
            sc.wait_ge(c_mst.sem, 16 * 32)


def emit_attention_pass(nc, block, pas, g):
    cst, ones_row, ones_rf = g["cst"], g["ones_row"], g["ones_rf"]
    ones_bf, masksb, identb = g["ones_bf"], g["masksb"], g["identb"]
    wq, wk, wv = g["wq"], g["wk"], g["wv"]
    xn_scr, xnq_scr, ropeK, ropeQ, oat_scr = g["xn_scr"], g["xnq_scr"], g["ropeK"], g["ropeQ"], g["oat_scr"]
    mk_sem = g["mk_sem"]
    P = f"a{pas}_"
    with contextlib.ExitStack() as ps:
        sb = lambda n, shp, dt: ps.enter_context(nc.sbuf_tensor(P + n, shp, dt))
        pt_ = lambda n, shp: ps.enter_context(nc.psum_tensor(P + n, shp, F32))
        sem = lambda n: Ctr(mk_sem(P + n))

        q_all = sb("q_all", [128, 4, 8, 2, 256], BF16)
        kcache = None
        vcache = None

        with contextlib.ExitStack() as qs:
            sbq = lambda n, shp, dt: qs.enter_context(nc.sbuf_tensor(P + "q" + n, shp, dt))
            wq_sb = sbq("w", [128, 16, 1024], BF16)
            xnq_f = sbq("xn", [128, 16, NT], BF16)
            ropeq = sbq("rope", [128, 2, NT], F32)
            qsw = sbq("sw", [128, 512], F32)
            qt1 = sbq("t1", [128, 512], F32)
            qt2 = sbq("t2", [128, 512], F32)
            pq = [qs.enter_context(nc.psum_tensor(P + "qp0", [128, 512], F32)),
                  qs.enter_context(nc.psum_tensor(P + "qp1", [128, 512], F32))]
            c_ld = sem("qld")
            c_mm = sem("qmm")
            c_v = sem("qv")

            @block.gpsimd
            def _(gp):
                c_ld.inc(gp.dma_start(wq_sb[:, :, :],
                                      wq[:, pas * 1024:(pas + 1) * 1024].rearrange("(dc p) c -> p dc c", p=128)), 16)

            @block.sync
            def _(s):
                c_ld.inc(s.dma_start(ropeq[:, 0, :], ropeQ[0]), 16)
                c_ld.inc(s.dma_start(ropeq[:, 1, :], ropeQ[1]), 16)
                for dc in range(16):
                    for q4 in range(4):
                        c_ld.inc(s.dma_start(xnq_f[:, dc, q4 * 512:(q4 + 1) * 512],
                                             xnq_scr[q4 * 512:(q4 + 1) * 512, dc * 128:(dc + 1) * 128],
                                             transpose=True), 16)

            n_chunk = 0
            for h in range(8):
                for q4 in range(4):
                    par = n_chunk % 2
                    sl = slice(q4 * 512, (q4 + 1) * 512)

                    @block.tensor
                    def _(t, h=h, par=par, sl=sl, n=n_chunk):
                        if n == 0:
                            t.wait_ge(c_ld.sem, 16 * 67)
                        if n >= 2:
                            t.wait_ge(c_v.sem, 3 * (n - 2) + 2)
                        mm = None
                        for dc in range(16):
                            mm = t.matmul(pq[par][:, :], wq_sb[:, dc, h * 128:(h + 1) * 128],
                                          xnq_f[:, dc, sl], start=(dc == 0), stop=(dc == 15))
                        c_mm.inc(mm)

                    @block.vector
                    def _(v, h=h, par=par, sl=sl, n=n_chunk):
                        v.wait_ge(c_mm.sem, n + 1)
                        v.tensor_scalar(qsw[0:64, :], pq[par][64:128, :], cst[0:64, 2:3], None, ALU.mult)
                        c_v.inc(v.tensor_copy(qsw[64:128, :], pq[par][0:64, :]))
                        v.tensor_tensor(qt1[:, :], pq[par][:, :], ropeq[:, 0, sl], ALU.mult)
                        v.wait_ge(c_v.sem, 3 * n + 1)
                        c_v.inc(v.tensor_tensor(qt2[:, :], qsw[:, :], ropeq[:, 1, sl], ALU.mult))
                        v.wait_ge(c_v.sem, 3 * n + 2)
                        dst = q_all[:, h // 2, 2 * q4:2 * q4 + 2, h % 2, :]
                        c_v.inc(v.tensor_tensor(dst, qt1[:, :].rearrange("p (e c) -> p e c", c=256),
                                                qt2[:, :].rearrange("p (e c) -> p e c", c=256), ALU.add))
                    n_chunk += 1

            @block.tensor
            def _(t):
                t.wait_ge(c_v.sem, 3 * 32)
            c_qv = c_v

        if g.get("dbg"):
            @block.sync
            def _(s):
                s.wait_ge(c_qv.sem, 96)
                s.dma_start(g["dbg"]["q"], q_all[:, :, :]).then_inc(c_qv.sem, 16)
            c_qv.n += 16

        kcache = sb("kcache", [128, 2, S], BF16)
        vcache = sb("vcache", [128, 64, 256], BF16)
        with contextlib.ExitStack() as ks:
            sbk = lambda n, shp, dt: ks.enter_context(nc.sbuf_tensor(P + "k" + n, shp, dt))
            wk_sb = sbk("wk", [128, 16, 256], BF16)
            wv_sb = sbk("wv", [128, 16, 256], BF16)
            xnb_f = sbk("xnb", [128, 16, 1024], BF16)
            ropek = sbk("rope", [128, 2, 1024], F32)
            ksw = sbk("sw", [128, 512], F32)
            kt1 = sbk("t1", [128, 512], F32)
            kt2 = sbk("t2", [128, 512], F32)
            pk = [ks.enter_context(nc.psum_tensor(P + "kp0", [128, 512], F32)),
                  ks.enter_context(nc.psum_tensor(P + "kp1", [128, 512], F32))]
            pv = [ks.enter_context(nc.psum_tensor(P + "vp0", [128, 256], F32)),
                  ks.enter_context(nc.psum_tensor(P + "vp1", [128, 256], F32))]
            c_ld = sem("kld")
            c_mmk = sem("kmmk")
            c_mmv = sem("kmmv")
            c_vk = sem("kvk")
            c_vv = sem("kvv")

            @block.gpsimd
            def _(gp):
                gp.wait_ge(c_qv.sem, 96)
                c_ld.inc(gp.dma_start(wk_sb[:, :, :],
                                      wk[:, pas * 256:(pas + 1) * 256].rearrange("(dc p) c -> p dc c", p=128)), 16)
                c_ld.inc(gp.dma_start(wv_sb[:, :, :],
                                      wv[:, pas * 256:(pas + 1) * 256].rearrange("(dc p) c -> p dc c", p=128)), 16)
                gp.wait_ge(c_ld.sem, 32)

            for kb in range(8):
                @block.sync
                def _(s, kb=kb):
                    if kb == 0:
                        s.wait_ge(c_qv.sem, 3 * 32)
                    if kb > 0:
                        s.wait_ge(c_mmv.sem, 8 * kb)
                    for dc in range(16):
                        c_ld.inc(s.dma_start(xnb_f[:, dc, :],
                                             xn_scr[kb * 1024:(kb + 1) * 1024, dc * 128:(dc + 1) * 128],
                                             transpose=True), 16)
                    c_ld.inc(s.dma_start(ropek[:, 0, :], ropeK[0][:, kb * 1024:(kb + 1) * 1024]), 16)
                    c_ld.inc(s.dma_start(ropek[:, 1, :], ropeK[1][:, kb * 1024:(kb + 1) * 1024]), 16)

                for kvl in range(2):
                    for kss in range(2):
                        n = kb * 4 + kvl * 2 + kss
                        par = n % 2
                        sl = slice(kss * 512, (kss + 1) * 512)

                        @block.tensor
                        def _(t, kb=kb, kvl=kvl, n=n, par=par, sl=sl):
                            if kvl == 0 and kss == 0:
                                t.wait_ge(c_ld.sem, 16 * (2 + 18 * (kb + 1)))
                            if n >= 2:
                                t.wait_ge(c_vk.sem, 3 * (n - 2) + 2)
                            mm = None
                            for dc in range(16):
                                mm = t.matmul(pk[par][:, :], wk_sb[:, dc, kvl * 128:(kvl + 1) * 128],
                                              xnb_f[:, dc, sl], start=(dc == 0), stop=(dc == 15))
                            c_mmk.inc(mm)

                        @block.vector
                        def _(v, kb=kb, kvl=kvl, kss=kss, n=n, par=par, sl=sl):
                            v.wait_ge(c_mmk.sem, n + 1)
                            v.tensor_scalar(ksw[0:64, :], pk[par][64:128, :], cst[0:64, 2:3], None, ALU.mult)
                            c_vk.inc(v.tensor_copy(ksw[64:128, :], pk[par][0:64, :]))
                            v.tensor_tensor(kt1[:, :], pk[par][:, :], ropek[:, 0, sl], ALU.mult)
                            v.wait_ge(c_vk.sem, 3 * n + 1)
                            c_vk.inc(v.tensor_tensor(kt2[:, :], ksw[:, :], ropek[:, 1, sl], ALU.mult))
                            v.wait_ge(c_vk.sem, 3 * n + 2)
                            c_vk.inc(v.tensor_tensor(
                                kcache[:, kvl, kb * 1024 + kss * 512:kb * 1024 + (kss + 1) * 512],
                                kt1[:, :], kt2[:, :], ALU.add))

                for tc in range(8):
                    n = kb * 8 + tc
                    par = n % 2

                    @block.tensor
                    def _(t, kb=kb, tc=tc, n=n, par=par):
                        if n >= 2:
                            t.wait_ge(c_vv.sem, n - 1)
                        mm = None
                        for dc in range(16):
                            mm = t.matmul(pv[par][:, :], xnb_f[:, dc, tc * 128:(tc + 1) * 128],
                                          wv_sb[:, dc, :], start=(dc == 0), stop=(dc == 15))
                        c_mmv.inc(mm)

                    @block.vector
                    def _(v, kb=kb, tc=tc, n=n, par=par):
                        v.wait_ge(c_mmv.sem, n + 1)
                        c_vv.inc(v.tensor_copy(vcache[:, kb * 8 + tc, :], pv[par][:, :]))

            if g.get("dbg"):
                @block.sync
                def _(s):
                    s.wait_ge(c_vk.sem, 96)
                    s.wait_ge(c_vv.sem, 64)
                    s.dma_start(g["dbg"]["k"], kcache[:, :, :]).then_inc(c_vk.sem, 16)
                    s.dma_start(g["dbg"]["v"], vcache[:, :, :]).then_inc(c_vk.sem, 16)
                c_vk.n += 32

            @block.tensor
            def _(t):
                t.wait_ge(c_vk.sem, 96)
                t.wait_ge(c_vv.sem, 64)
            @block.scalar
            def _(sc):
                sc.wait_ge(c_vk.sem, 96)
                sc.wait_ge(c_vv.sem, 64)

        with contextlib.ExitStack() as ss:
            sbs = lambda n, shp, dt: ss.enter_context(nc.sbuf_tensor(P + "s" + n, shp, dt))
            pt_sb = sbs("pt", [128, 2, 8, 512], BF16)
            l_sb = sbs("l", [2, 1, 512], F32)
            rl_sb = sbs("rl", [2, 1, 512], F32)
            rlb = sbs("rlb", [128, 512], F32)
            obuf = sbs("ob", [128, 2, 512], BF16)
            pol = [ss.enter_context(nc.psum_tensor(P + "ol0", [128, 512], F32)),
                  ss.enter_context(nc.psum_tensor(P + "ol1", [128, 512], F32))]
            pst = [ss.enter_context(nc.psum_tensor(P + "st0", [128, 1024], F32)),
                  ss.enter_context(nc.psum_tensor(P + "st1", [128, 1024], F32))]
            pl_ = ss.enter_context(nc.psum_tensor(P + "L", [1, 512], F32))
            pbc = ss.enter_context(nc.psum_tensor(P + "bc", [128, 512], F32))
            c_sc = sem("ssc")
            c_tst = sem("stst")
            c_tav = sem("stav")
            c_vt = sem("svt")
            c_vr = sem("svr")
            c_tb = sem("stb")
            c_sto = sem("ssto")

            grounds = 0
            last_pt_user = {}

            neh = 0
            ehps = [(e, hp) for e in range(1, 9) for hp in range(4)]
            for e, hp in ehps:
                kvl = hp // 2
                sc0 = c_sc.n
                tst0 = c_tst.n
                tav0 = c_tav.n
                gr0 = grounds
                q_rhs = q_all[:, hp, e - 1, :, :]

                pt_waits = {}
                for kb in range(e):
                    for r in range(4):
                        pt_waits[(kb, r)] = last_pt_user.get((kb % 2, r), 0)
                for kb in range(e):
                    for r in range(4):
                        last_pt_user[(kb % 2, r)] = tav0 + 4 * kb + r + 1
                grounds += 4 * e

                @block.tensor
                def _(t, e=e, hp=hp, kvl=kvl, q_rhs=q_rhs, neh=neh, sc0=sc0, gr0=gr0):
                    if neh >= 1:
                        t.wait_ge(c_vt.sem, 3 * (neh - 1) + 2)
                        c_tb.inc(t.matmul(pbc[:, :], ones_rf[:, :], rl_sb[(neh - 1) % 2, :, :],
                                          start=True, stop=True))
                    if neh >= 2:
                        t.wait_ge(c_vt.sem, 3 * (neh - 1))
                        t.wait_ge(c_vt.sem, 3 * (neh - 1) + 1)
                    for kb in range(e):
                        diag = (kb == e - 1)
                        for r in range(4):
                            grr = gr0 + 4 * kb + r
                            if grr >= 2:
                                t.wait_ge(c_sc.sem, grr - 1)
                            mm = None
                            for kc in (2 * r, 2 * r + 1):
                                half = slice((kc % 2) * 512, (kc % 2 + 1) * 512)
                                mm = t.matmul(pst[grr % 2][:, half],
                                              kcache[:, kvl, kb * 1024 + kc * 128:kb * 1024 + (kc + 1) * 128],
                                              q_rhs, start=True, stop=(not diag),
                                              skip_group_check=True)
                                if diag:
                                    mm = t.matmul(pst[grr % 2][:, half], identb[:, :],
                                                  masksb[:, kc, :], start=False, stop=True,
                                                  skip_group_check=True)
                            c_tst.inc(mm)
                            t.wait_ge(c_sc.sem, sc0 + 4 * kb + r + 1)
                            first_av = (kb == 0 and r == 0)
                            last_av = (kb == e - 1 and r == 3)
                            mm = None
                            for kc in (2 * r, 2 * r + 1):
                                t.matmul(pol[neh % 2][:, :],
                                         vcache[:, kb * 8 + kc, kvl * 128:(kvl + 1) * 128],
                                         pt_sb[:, kb % 2, kc, :],
                                         start=(first_av and kc == 2 * r), stop=False,
                                         skip_group_check=True)
                                mm = t.matmul(pl_[0:1, :], ones_bf[:, :],
                                              pt_sb[:, kb % 2, kc, :],
                                              start=(first_av and kc == 2 * r),
                                              stop=(last_av and kc == 2 * r + 1),
                                              skip_group_check=True)
                            c_tav.inc(mm)

                @block.scalar
                def _(sc, e=e, neh=neh, sc0=sc0, tst0=tst0, gr0=gr0, pt_waits=pt_waits):
                    for kb in range(e):
                        for r in range(4):
                            grr = gr0 + 4 * kb + r
                            pw = pt_waits[(kb, r)]
                            if pw > 0:
                                sc.wait_ge(c_tav.sem, pw)
                            sc.wait_ge(c_tst.sem, tst0 + 4 * kb + r + 1)
                            c_sc.inc(sc.activation(pt_sb[:, kb % 2, 2 * r:2 * r + 2, :],
                                                   pst[grr % 2][:, :],
                                                   AF.Exp, bias=cst[:, 6:7], scale=cst[:, 5:6]))

                @block.vector
                def _(v, e=e, neh=neh, tav0=tav0):
                    v.wait_ge(c_tav.sem, tav0 + 4 * e)
                    c_vt.inc(v.tensor_copy(l_sb[neh % 2, :, :], pl_[0:1, :]))
                    v.wait_ge(c_vt.sem, 3 * neh + 1)
                    c_vt.inc(v.reciprocal(rl_sb[neh % 2, :, :], l_sb[neh % 2, :, :]))
                    v.wait_ge(c_tb.sem, neh + 1)
                    c_vr.inc(v.tensor_copy(rlb[:, :], pbc[:, :]))
                    v.wait_ge(c_vr.sem, neh + 1)
                    if neh >= 2:
                        v.wait_ge(c_sto.sem, 32 * (neh - 1))
                    c_vt.inc(v.tensor_tensor(obuf[:, neh % 2, :], pol[neh % 2][:, :],
                                             rlb[:, :], ALU.mult))

                @block.sync
                def _(s, e=e, hp=hp, neh=neh):
                    s.wait_ge(c_vt.sem, 3 * (neh + 1))
                    h0g = pas * 8 + 2 * hp
                    c_sto.inc(s.dma_start(oat_scr[h0g * 128:(h0g + 1) * 128, (e - 1) * 256:e * 256],
                                          obuf[:, neh % 2, 0:256]), 16)
                    c_sto.inc(s.dma_start(oat_scr[(h0g + 1) * 128:(h0g + 2) * 128, (e - 1) * 256:e * 256],
                                          obuf[:, neh % 2, 256:512]), 16)
                neh += 1

            @block.tensor
            def _(t):
                t.wait_ge(c_vt.sem, 3 * 31 + 2)
                c_tb.inc(t.matmul(pbc[:, :], ones_rf[:, :], rl_sb[31 % 2, :, :], start=True, stop=True))

            @block.sync
            def _(s):
                s.wait_ge(c_sto.sem, 32 * 32)

            @block.vector
            def _(v):
                v.wait_ge(c_vt.sem, 3 * 32)

            @block.scalar
            def _(sc):
                sc.wait_ge(c_sc.sem, 576)

            @block.tensor
            def _(t):
                t.wait_ge(c_tb.sem, 32)


def emit_ffn(nc, block, g):
    cst, ones_rf, ident, mk_sem = g["cst"], g["ones_rf"], g["ident"], g["mk_sem"]
    gwT, wgu, wdn, sgu, sdn = g["gwT"], g["wgu"], g["wdn"], g["sgu"], g["sdn"]
    xnq_scr, s_scr, m_scr, cmb_scr = g["xnq_scr"], g["s_scr"], g["m_scr"], g["cmb_scr"]
    xnqlo_scr = g["xnqlo_scr"]
    with contextlib.ExitStack() as ps:
        sb = lambda n, shp, dt: ps.enter_context(nc.sbuf_tensor("f_" + n, shp, dt))
        xnq_f = sb("xn", [128, 16, NT], BF16)
        c_ldx = Ctr(mk_sem("fldx"))
        c_rt = Ctr(mk_sem("frt"))
        c_rv = Ctr(mk_sem("frv"))
        c_rs = Ctr(mk_sem("frs"))
        c_cst = Ctr(mk_sem("fcst"))

        @block.sync
        def _(s):
            for dc in range(16):
                for q4 in range(4):
                    c_ldx.inc(s.dma_start(xnq_f[:, dc, q4 * 512:(q4 + 1) * 512],
                                          xnq_scr[q4 * 512:(q4 + 1) * 512, dc * 128:(dc + 1) * 128],
                                          transpose=True), 16)

        with contextlib.ExitStack() as rs:
            sbr = lambda n, shp, dt: rs.enter_context(nc.sbuf_tensor("fr" + n, shp, dt))
            cmbT = sbr("cT", [16, NT], F32)
            xnqlo_f = sbr("xlo", [128, 16, NT], BF16)
            gw_sb = sbr("gw", [128, 16, 16], BF16)
            gwf_sb = sbr("gwf", [128, 16, 16], F32)
            gwlo_sb = sbr("gwlo", [128, 16, 16], BF16)
            lg = sbr("lg", [128, 16], F32)
            gm = sbr("gm", [128, 4], F32)
            t6 = sbr("t6", [128, 8], F32)
            keep = sbr("keep", [128, 4], F32)
            ml = sbr("ml", [128, 16], F32)
            ex = sbr("ex", [128, 16], F32)
            red = sbr("red", [128, 4], F32)
            probs = sbr("probs", [128, 16], F32)
            vv = sbr("vv", [128, 8], F32)
            ind = sbr("ind", [128, 4], F32)
            eqs = sbr("eqs", [128, 16], F32)
            work = sbr("work", [128, 16], F32)
            work2 = sbr("work2", [128, 16], F32)
            strict = sbr("strict", [128, 16], F32)
            cums = sbr("cums", [128, 16], F32)
            cums2 = sbr("cums2", [128, 16], F32)
            sel = sbr("sel", [128, 16], F32)
            cmb = sbr("cmb", [128, 16], F32)
            plog = rs.enter_context(nc.psum_tensor("frpl", [128, 16], F32))
            ptp = rs.enter_context(nc.psum_tensor("frpt", [16, 128], F32))

            @block.gpsimd
            def _(gp):
                c_ldx.inc(gp.dma_start(gw_sb[:, :, :], gwT.rearrange("(dc p) e -> p dc e", p=128)), 16)

            @block.sync
            def _(s):
                c_ldx.inc(s.dma_start(gwf_sb[:, :, :], gwT.rearrange("(dc p) e -> p dc e", p=128)), 16)
                for dc in range(16):
                    for q4 in range(4):
                        c_ldx.inc(s.dma_start(xnqlo_f[:, dc, q4 * 512:(q4 + 1) * 512],
                                              xnqlo_scr[q4 * 512:(q4 + 1) * 512, dc * 128:(dc + 1) * 128],
                                              transpose=True), 16)

            @block.vector
            def _(v):
                v.wait_ge(c_ldx.sem, 16 * 66)
                c_rv.inc(v.tensor_sub(gwlo_sb[:, :, :], gwf_sb[:, :, :], gw_sb[:, :, :]))
                v.wait_ge(c_rv.sem, c_rv.n)

            rv_marks = []
            for tt in range(16):
                @block.tensor
                def _(t, tt=tt):
                    if tt == 0:
                        t.wait_ge(c_ldx.sem, 16 * 130)
                        t.wait_ge(c_rv.sem, 1)
                    if tt >= 1:
                        t.wait_ge(c_rv.sem, rv_marks[tt - 1]["end"])
                    mm = None
                    sl = slice(tt * 128, (tt + 1) * 128)
                    for dc in range(16):
                        t.matmul(plog[:, :], xnq_f[:, dc, sl], gw_sb[:, dc, :],
                                 start=(dc == 0), stop=False, skip_group_check=True)
                    for dc in range(16):
                        t.matmul(plog[:, :], xnq_f[:, dc, sl], gwlo_sb[:, dc, :],
                                 start=False, stop=False, skip_group_check=True)
                    for dc in range(16):
                        mm = t.matmul(plog[:, :], xnqlo_f[:, dc, sl], gw_sb[:, dc, :],
                                      start=False, stop=(dc == 15), skip_group_check=True)
                    c_rt.inc(mm)

                marks = {}
                rv_marks.append(marks)

                @block.vector
                def _(v, tt=tt, marks=marks):
                    def stepw(inst):
                        c_rv.inc(inst)
                        v.wait_ge(c_rv.sem, c_rv.n)
                    v.wait_ge(c_rt.sem, 2 * tt + 1)
                    stepw(v.tensor_copy(lg[:, :], plog[:, :]))
                    lgv = lg[:, :].rearrange("p (g j) -> p g j", j=4)
                    v.tensor_tensor(t6[:, 0:4], lgv[:, :, 0], lgv[:, :, 1], ALU.max)
                    stepw(v.tensor_tensor(keep[:, 0:4], lgv[:, :, 2], lgv[:, :, 3], ALU.max))
                    stepw(v.tensor_tensor(gm[:, :], t6[:, 0:4], keep[:, :], ALU.max))
                    v.tensor_tensor(t6[:, 0:1], gm[:, 0:1], gm[:, 1:2], ALU.max)
                    v.tensor_tensor(t6[:, 1:2], gm[:, 0:1], gm[:, 1:2], ALU.min)
                    v.tensor_tensor(t6[:, 2:3], gm[:, 2:3], gm[:, 3:4], ALU.max)
                    stepw(v.tensor_tensor(t6[:, 3:4], gm[:, 2:3], gm[:, 3:4], ALU.min))
                    v.tensor_tensor(t6[:, 4:5], t6[:, 0:1], t6[:, 2:3], ALU.min)
                    stepw(v.tensor_tensor(t6[:, 5:6], t6[:, 1:2], t6[:, 3:4], ALU.max))
                    stepw(v.tensor_tensor(t6[:, 6:7], t6[:, 4:5], t6[:, 5:6], ALU.max))
                    stepw(v.tensor_scalar(keep[:, :], gm[:, :], t6[:, 6:7], None, ALU.is_ge))
                    for j in range(4):
                        inst = v.tensor_tensor(ml[:, :].rearrange("p (g j) -> p g j", j=4)[:, :, j],
                                               lgv[:, :, j], keep[:, :], ALU.mult)
                    stepw(inst)
                    stepw(v.reduce_max(red[:, 0:1], ml[:, :], axis=AX.X, negate=True))
                    marks["rm"] = c_rv.n
                    v.wait_ge(c_rs.sem, tt + 1)
                    stepw(v.reduce_sum(red[:, 1:2], ex[:, :], axis=AX.X))
                    stepw(v.reciprocal(red[:, 2:3], red[:, 1:2]))
                    stepw(v.tensor_scalar(probs[:, :], ex[:, :], red[:, 2:3], None, ALU.mult))
                    stepw(v.tensor_copy(work[:, :], probs[:, :]))
                    for r in range(4):
                        stepw(v.reduce_max(vv[:, r:r + 1], work[:, :], axis=AX.X))
                        if r < 3:
                            stepw(v.tensor_scalar(eqs[:, :], work[:, :], vv[:, r:r + 1], None, ALU.is_ge))
                            stepw(v.reduce_sum(vv[:, 4 + r:5 + r], eqs[:, :], axis=AX.X))
                            if r > 0:
                                stepw(v.tensor_tensor(vv[:, 4 + r:5 + r], vv[:, 4 + r:5 + r],
                                                      vv[:, 3 + r:4 + r], ALU.add))
                            v.scalar_tensor_tensor(work2[:, :], eqs[:, :], cst[:, 2:3], work[:, :],
                                                   ALU.mult, ALU.add)
                            stepw(v.nop())
                            stepw(v.tensor_copy(work[:, :], work2[:, :]))
                    v.tensor_scalar(ind[:, 0:1], vv[:, 4:5], cst[:, 3:4], None, ALU.is_ge)
                    v.tensor_scalar(ind[:, 1:2], vv[:, 5:6], cst[:, 3:4], None, ALU.is_ge)
                    stepw(v.tensor_scalar(ind[:, 2:3], vv[:, 6:7], cst[:, 3:4], None, ALU.is_ge))
                    stepw(v.tensor_tensor(ind[:, 1:2], ind[:, 1:2], ind[:, 0:1], ALU.subtract))
                    v.tensor_tensor(ind[:, 2:3], ind[:, 2:3], ind[:, 0:1], ALU.subtract)
                    stepw(v.nop())
                    stepw(v.tensor_tensor(ind[:, 2:3], ind[:, 2:3], ind[:, 1:2], ALU.subtract))
                    v.tensor_tensor(vv[:, 7:8], ind[:, 0:1], vv[:, 0:1], ALU.mult)
                    stepw(v.tensor_tensor(t6[:, 7:8], ind[:, 1:2], vv[:, 1:2], ALU.mult))
                    stepw(v.tensor_tensor(vv[:, 7:8], vv[:, 7:8], t6[:, 7:8], ALU.add))
                    stepw(v.tensor_tensor(t6[:, 7:8], ind[:, 2:3], vv[:, 2:3], ALU.mult))
                    stepw(v.tensor_tensor(vv[:, 7:8], vv[:, 7:8], t6[:, 7:8], ALU.add))
                    stepw(v.tensor_tensor(t6[:, 7:8], ind[:, 0:1], ind[:, 1:2], ALU.add))
                    stepw(v.tensor_tensor(t6[:, 7:8], t6[:, 7:8], ind[:, 2:3], ALU.add))
                    stepw(v.scalar_tensor_tensor(t6[:, 7:8], t6[:, 7:8], cst[:, 2:3], cst[:, 1:2],
                                                 ALU.mult, ALU.add))
                    stepw(v.tensor_tensor(t6[:, 7:8], t6[:, 7:8], vv[:, 3:4], ALU.mult))
                    stepw(v.tensor_tensor(vv[:, 7:8], vv[:, 7:8], t6[:, 7:8], ALU.add))
                    v.tensor_scalar(strict[:, :], probs[:, :], vv[:, 7:8], None, ALU.is_gt)
                    stepw(v.tensor_scalar(eqs[:, :], probs[:, :], vv[:, 7:8], None, ALU.is_equal))
                    stepw(v.reduce_sum(ind[:, 3:4], strict[:, :], axis=AX.X, negate=True))
                    stepw(v.tensor_scalar(ind[:, 3:4], ind[:, 3:4], cst[:, 3:4], None, ALU.add))
                    stepw(v.tensor_copy(cums[:, :], eqs[:, :]))
                    for k, (s_, d_) in zip([1, 2, 4, 8],
                                           [(cums, cums2), (cums2, cums), (cums, cums2), (cums2, cums)]):
                        v.tensor_copy(d_[:, 0:k], s_[:, 0:k])
                        stepw(v.tensor_tensor(d_[:, k:16], s_[:, k:16], s_[:, 0:16 - k], ALU.add))
                    stepw(v.tensor_scalar(cums2[:, :], cums[:, :], ind[:, 3:4], None, ALU.is_le))
                    stepw(v.tensor_tensor(sel[:, :], eqs[:, :], cums2[:, :], ALU.mult))
                    stepw(v.tensor_tensor(sel[:, :], sel[:, :], strict[:, :], ALU.add))
                    stepw(v.scalar_tensor_tensor(cmb[:, :], sel[:, :], cst[:, 4:5], probs[:, :],
                                                 ALU.mult, ALU.mult))
                    marks["cmb"] = c_rv.n
                    v.wait_ge(c_rt.sem, 2 * (tt + 1))
                    stepw(v.tensor_copy(cmbT[:, tt * 128:(tt + 1) * 128], ptp[0:16, :]))
                    marks["end"] = c_rv.n

                @block.scalar
                def _(sc, tt=tt, marks=marks):
                    sc.wait_ge(c_rv.sem, marks["rm"])
                    c_rs.inc(sc.activation(ex[:, :], ml[:, :], AF.Exp, bias=red[:, 0:1]))

                @block.tensor
                def _(t, tt=tt, marks=marks):
                    t.wait_ge(c_rv.sem, marks["cmb"])
                    c_rt.inc(t.transpose(ptp[0:16, 0:128], cmb[:, 0:16], ident[:, :]))

            @block.sync
            def _(s):
                s.wait_ge(c_rv.sem, rv_marks[-1]["end"])
                c_cst.inc(s.dma_start(cmb_scr, cmbT[:, :]), 16)

        Y = sb("Y", [128, 16, 1024], F32)
        hs = sb("hs", [128, 11, 1024], BF16)
        wgu_sb = sb("wgu", [128, 3, 16, 256], BF16)
        wd_sb = sb("wd", [128, 2, 11, 256], BF16)
        cbc = sb("cbc", [128, 1024], F32)
        cr = sb("cr", [1, 1024], F32)
        t1s = sb("t1", [128, 2, 512], BF16)
        t2s = sb("t2", [128, 1, 512], F32)
        g_sb = sb("g", [128, 2, 1024], BF16)
        ss_sb = sb("ss", [128, 1, 1024], BF16)
        m_sb = sb("m", [128, 1, 1024], BF16)
        pg = [ps.enter_context(nc.psum_tensor("fpg%d" % i, [128, 512], F32)) for i in range(2)]
        pu = [ps.enter_context(nc.psum_tensor("fpu%d" % i, [128, 512], F32)) for i in range(2)]
        py = [ps.enter_context(nc.psum_tensor("fpy%d" % i, [128, 512], F32)) for i in range(2)]
        pbc2 = ps.enter_context(nc.psum_tensor("fpbc", [128, 512], F32))
        c_wl = Ctr(mk_sem("fwl"))
        c_tg = Ctr(mk_sem("ftg"))
        c_ty = Ctr(mk_sem("fty"))
        c_tb2 = Ctr(mk_sem("ftb"))
        c_s5 = Ctr(mk_sem("fs5"))
        c_v5 = Ctr(mk_sem("fv5"))
        c_vy = Ctr(mk_sem("fvy"))
        c_vb = Ctr(mk_sem("fvb"))
        c_crl = Ctr(mk_sem("fcrl"))
        c_sld = Ctr(mk_sem("fsld"))
        c_mst = Ctr(mk_sem("fmst"))

        experts = [("sA", None), ("sB", None)] + [("e", i) for i in range(NE)]

        for half in range(2):
            h0 = half * 1024
            for exi, (kind, ei) in enumerate(experts):
                first = exi == 0
                routed = kind == "e"
                if routed:
                    @block.sync
                    def _(s, ei=ei, half=half):
                        s.wait_ge(c_cst.sem, 16)
                        s.wait_ge(c_tb2.sem, c_tb2.n)
                        c_crl.inc(s.dma_start(cr[:, :], cmb_scr[ei:ei + 1, h0:h0 + 1024]), 16)

                    @block.tensor
                    def _(t):
                        t.wait_ge(c_crl.sem, c_crl.n)
                        t.wait_ge(c_vb.sem, c_vb.n)
                        c_tb2.inc(t.matmul(pbc2[:, :], ones_rf[:, :], cr[:, 0:512],
                                           start=True, stop=True))

                    @block.vector
                    def _(v):
                        v.wait_ge(c_tb2.sem, c_tb2.n)
                        c_vb.inc(v.tensor_copy(cbc[:, 0:512], pbc2[:, :]))

                    @block.tensor
                    def _(t):
                        t.wait_ge(c_vb.sem, c_vb.n)
                        c_tb2.inc(t.matmul(pbc2[:, :], ones_rf[:, :], cr[:, 512:1024],
                                           start=True, stop=True))

                    @block.vector
                    def _(v):
                        v.wait_ge(c_tb2.sem, c_tb2.n)
                        c_vb.inc(v.tensor_copy(cbc[:, 512:1024], pbc2[:, :]))

                if exi == 0 and half == 0:
                    pg_free = {0: None, 1: None}
                    py_free = {0: None, 1: None}
                for c in range(11):
                    par = (exi * 11 + c) % 3
                    if kind == "e":
                        gsrc = wgu[ei][:, c * 128:(c + 1) * 128]
                        usrc = wgu[ei][:, I_FF + c * 128:I_FF + (c + 1) * 128]
                    elif kind == "sA":
                        gsrc = sgu[:, c * 128:(c + 1) * 128]
                        usrc = sgu[:, SI + c * 128:SI + (c + 1) * 128]
                    else:
                        gsrc = sgu[:, I_FF + c * 128:I_FF + (c + 1) * 128]
                        usrc = sgu[:, SI + I_FF + c * 128:SI + I_FF + (c + 1) * 128]

                    @block.gpsimd
                    def _(gp, gsrc=gsrc, usrc=usrc, par=par, ntg=c_tg.n):
                        if ntg >= 4:
                            gp.wait_ge(c_tg.sem, ntg - 3)
                        c_wl.inc(gp.dma_start(wgu_sb[:, par, :, 0:128],
                                              gsrc.rearrange("(dc p) f -> p dc f", p=128)), 16)
                        c_wl.inc(gp.dma_start(wgu_sb[:, par, :, 128:256],
                                              usrc.rearrange("(dc p) f -> p dc f", p=128)), 16)

                    for ts in range(2):
                        sl = slice(h0 + ts * 512, h0 + (ts + 1) * 512)

                        @block.tensor
                        def _(t, par=par, ts=ts, sl=sl, nwl=c_wl.n):
                            if ts == 0:
                                t.wait_ge(c_wl.sem, nwl)
                            if pg_free[ts] is not None:
                                s_n, v_n = pg_free[ts]
                                t.wait_ge(c_s5.sem, s_n)
                                t.wait_ge(c_v5.sem, v_n)
                            mm = None
                            for dc in range(16):
                                mm = t.matmul(pg[ts][:, :], wgu_sb[:, par, dc, 0:128],
                                              xnq_f[:, dc, sl], start=(dc == 0), stop=(dc == 15))
                            for dc in range(16):
                                mm = t.matmul(pu[ts][:, :], wgu_sb[:, par, dc, 128:256],
                                              xnq_f[:, dc, sl], start=(dc == 0), stop=(dc == 15))
                            c_tg.inc(mm)

                        @block.scalar
                        def _(sc, ts=ts, ntg=c_tg.n):
                            sc.wait_ge(c_tg.sem, ntg)
                            c_s5.inc(sc.activation(t1s[:, ts, :], pg[ts][:, :], AF.Silu))

                        @block.vector
                        def _(v, c=c, ts=ts, ntg=c_tg.n, ns=c_s5.n, routed=routed):
                            v.wait_ge(c_tg.sem, ntg)
                            if routed:
                                v.wait_ge(c_vb.sem, c_vb.n)
                                c_v5.inc(v.tensor_tensor(t2s[:, 0, :], pu[ts][:, :],
                                                         cbc[:, ts * 512:(ts + 1) * 512], ALU.mult))
                                v.wait_ge(c_v5.sem, c_v5.n)
                                v.wait_ge(c_s5.sem, ns)
                                c_v5.inc(v.tensor_tensor(hs[:, c, ts * 512:(ts + 1) * 512],
                                                         t1s[:, ts, :], t2s[:, 0, :], ALU.mult))
                            else:
                                v.wait_ge(c_s5.sem, ns)
                                c_v5.inc(v.tensor_tensor(hs[:, c, ts * 512:(ts + 1) * 512],
                                                         t1s[:, ts, :], pu[ts][:, :], ALU.mult))
                        pg_free[ts] = (c_s5.n, c_v5.n)

                for fg in range(8):
                    par = (exi * 8 + fg) % 2
                    if kind == "e":
                        dsrc = wdn[ei][:, fg * 256:(fg + 1) * 256]
                    elif kind == "sA":
                        dsrc = sdn[0:I_FF, fg * 256:(fg + 1) * 256]
                    else:
                        dsrc = sdn[I_FF:SI, fg * 256:(fg + 1) * 256]

                    @block.gpsimd
                    def _(gp, dsrc=dsrc, par=par, nty=c_ty.n):
                        if nty >= 2:
                            gp.wait_ge(c_ty.sem, nty - 1)
                        c_wl.inc(gp.dma_start(wd_sb[:, par, :, :],
                                              dsrc.rearrange("(ic p) f -> p ic f", p=128)), 16)

                    for fc in range(2):
                        for ts in range(2):
                            sl = slice(ts * 512, (ts + 1) * 512)
                            ypar = ts

                            @block.tensor
                            def _(t, par=par, fc=fc, ts=ts, sl=sl, nwl=c_wl.n, nv=c_v5.n, fg=fg):
                                if fc == 0 and ts == 0:
                                    t.wait_ge(c_wl.sem, nwl)
                                    if fg == 0:
                                        t.wait_ge(c_v5.sem, nv)
                                if py_free[ts] is not None:
                                    t.wait_ge(c_vy.sem, py_free[ts])
                                mm = None
                                for ic in range(11):
                                    mm = t.matmul(py[ts][:, :], wd_sb[:, par, ic, fc * 128:(fc + 1) * 128],
                                                  hs[:, ic, sl], start=(ic == 0), stop=(ic == 10))
                                c_ty.inc(mm)

                            @block.vector
                            def _(v, fg=fg, fc=fc, ts=ts, nty=c_ty.n, first=first, half=half):
                                v.wait_ge(c_ty.sem, nty)
                                ysl = Y[:, fg * 2 + fc, ts * 512:(ts + 1) * 512]
                                if first:
                                    c_vy.inc(v.tensor_copy(ysl, py[ts][:, :]))
                                else:
                                    c_vy.inc(v.tensor_tensor(ysl, ysl, py[ts][:, :], ALU.add))
                            py_free[ts] = c_vy.n

            mmul_hist = {}
            for dc in range(16):
                @block.scalar
                def _(sc, dc=dc, nvy=c_vy.n):
                    sc.wait_ge(c_vy.sem, nvy)
                    if (dc - 2) in mmul_hist:
                        sc.wait_ge(c_v5.sem, mmul_hist[dc - 2])
                    c_s5.inc(sc.activation(g_sb[:, dc % 2, :], Y[:, dc, :], AF.Silu))

                @block.sync
                def _(s, dc=dc, half=half):
                    if (dc - 1) in mmul_hist:
                        s.wait_ge(c_v5.sem, mmul_hist[dc - 1])
                    c_sld.inc(s.dma_start(ss_sb[:, 0, :],
                                          s_scr[dc * 128:(dc + 1) * 128, h0:h0 + 1024]), 16)

                @block.vector
                def _(v, dc=dc, ns=c_s5.n, nsl=c_sld.n, half=half):
                    v.wait_ge(c_s5.sem, ns)
                    v.wait_ge(c_sld.sem, nsl)
                    if c_mst.n >= 16:
                        v.wait_ge(c_mst.sem, c_mst.n)
                    c_v5.inc(v.tensor_tensor(m_sb[:, 0, :], g_sb[:, dc % 2, :],
                                             ss_sb[:, 0, :], ALU.mult))
                mmul_hist[dc] = c_v5.n

                @block.scalar
                def _(sc, dc=dc, half=half, nv=c_v5.n):
                    sc.wait_ge(c_v5.sem, nv)
                    c_mst.inc(sc.dma_start(m_scr[dc * 128:(dc + 1) * 128, h0:h0 + 1024],
                                           m_sb[:, 0, :]), 16)

        @block.scalar
        def _(sc):
            sc.wait_ge(c_mst.sem, 16 * 32)


def emit_attention_pass(nc, block, pas, g):
    cst, ones_row, ones_rf = g["cst"], g["ones_row"], g["ones_rf"]
    ones_bf, masksb, identb = g["ones_bf"], g["masksb"], g["identb"]
    wq, wk, wv = g["wq"], g["wk"], g["wv"]
    xn_scr, xnq_scr, ropeK, ropeQ, oat_scr = g["xn_scr"], g["xnq_scr"], g["ropeK"], g["ropeQ"], g["oat_scr"]
    mk_sem = g["mk_sem"]
    P = f"a{pas}_"
    with contextlib.ExitStack() as ps:
        sb = lambda n, shp, dt: ps.enter_context(nc.sbuf_tensor(P + n, shp, dt))
        pt_ = lambda n, shp: ps.enter_context(nc.psum_tensor(P + n, shp, F32))
        sem = lambda n: Ctr(mk_sem(P + n))

        q_all = sb("q_all", [128, 4, 8, 2, 256], BF16)
        kcache = None
        vcache = None

        with contextlib.ExitStack() as qs:
            sbq = lambda n, shp, dt: qs.enter_context(nc.sbuf_tensor(P + "q" + n, shp, dt))
            wq_sb = sbq("w", [128, 16, 1024], BF16)
            xnq_f = sbq("xn", [128, 16, NT], BF16)
            ropeq = sbq("rope", [128, 2, NT], F32)
            qsw = sbq("sw", [128, 512], F32)
            qt1 = sbq("t1", [128, 512], F32)
            qt2 = sbq("t2", [128, 512], F32)
            pq = [qs.enter_context(nc.psum_tensor(P + "qp0", [128, 512], F32)),
                  qs.enter_context(nc.psum_tensor(P + "qp1", [128, 512], F32))]
            c_ld = sem("qld")
            c_mm = sem("qmm")
            c_v = sem("qv")

            @block.gpsimd
            def _(gp):
                c_ld.inc(gp.dma_start(wq_sb[:, :, :],
                                      wq[:, pas * 1024:(pas + 1) * 1024].rearrange("(dc p) c -> p dc c", p=128)), 16)

            @block.sync
            def _(s):
                c_ld.inc(s.dma_start(ropeq[:, 0, :], ropeQ[0]), 16)
                c_ld.inc(s.dma_start(ropeq[:, 1, :], ropeQ[1]), 16)
                for dc in range(16):
                    for q4 in range(4):
                        c_ld.inc(s.dma_start(xnq_f[:, dc, q4 * 512:(q4 + 1) * 512],
                                             xnq_scr[q4 * 512:(q4 + 1) * 512, dc * 128:(dc + 1) * 128],
                                             transpose=True), 16)

            n_chunk = 0
            for h in range(8):
                for q4 in range(4):
                    par = n_chunk % 2
                    sl = slice(q4 * 512, (q4 + 1) * 512)

                    @block.tensor
                    def _(t, h=h, par=par, sl=sl, n=n_chunk):
                        if n == 0:
                            t.wait_ge(c_ld.sem, 16 * 67)
                        if n >= 2:
                            t.wait_ge(c_v.sem, 3 * (n - 2) + 2)
                        mm = None
                        for dc in range(16):
                            mm = t.matmul(pq[par][:, :], wq_sb[:, dc, h * 128:(h + 1) * 128],
                                          xnq_f[:, dc, sl], start=(dc == 0), stop=(dc == 15))
                        c_mm.inc(mm)

                    @block.vector
                    def _(v, h=h, par=par, sl=sl, n=n_chunk):
                        v.wait_ge(c_mm.sem, n + 1)
                        v.tensor_scalar(qsw[0:64, :], pq[par][64:128, :], cst[0:64, 2:3], None, ALU.mult)
                        c_v.inc(v.tensor_copy(qsw[64:128, :], pq[par][0:64, :]))
                        v.tensor_tensor(qt1[:, :], pq[par][:, :], ropeq[:, 0, sl], ALU.mult)
                        v.wait_ge(c_v.sem, 3 * n + 1)
                        c_v.inc(v.tensor_tensor(qt2[:, :], qsw[:, :], ropeq[:, 1, sl], ALU.mult))
                        v.wait_ge(c_v.sem, 3 * n + 2)
                        dst = q_all[:, h // 2, 2 * q4:2 * q4 + 2, h % 2, :]
                        c_v.inc(v.tensor_tensor(dst, qt1[:, :].rearrange("p (e c) -> p e c", c=256),
                                                qt2[:, :].rearrange("p (e c) -> p e c", c=256), ALU.add))
                    n_chunk += 1

            @block.tensor
            def _(t):
                t.wait_ge(c_v.sem, 3 * 32)
            c_qv = c_v

        if g.get("dbg"):
            @block.sync
            def _(s):
                s.wait_ge(c_qv.sem, 96)
                s.dma_start(g["dbg"]["q"], q_all[:, :, :]).then_inc(c_qv.sem, 16)
            c_qv.n += 16

        kcache = sb("kcache", [128, 2, S], BF16)
        vcache = sb("vcache", [128, 64, 256], BF16)
        with contextlib.ExitStack() as ks:
            sbk = lambda n, shp, dt: ks.enter_context(nc.sbuf_tensor(P + "k" + n, shp, dt))
            wk_sb = sbk("wk", [128, 16, 256], BF16)
            wv_sb = sbk("wv", [128, 16, 256], BF16)
            xnb_f = sbk("xnb", [128, 16, 1024], BF16)
            ropek = sbk("rope", [128, 2, 1024], F32)
            ksw = sbk("sw", [128, 512], F32)
            kt1 = sbk("t1", [128, 512], F32)
            kt2 = sbk("t2", [128, 512], F32)
            pk = [ks.enter_context(nc.psum_tensor(P + "kp0", [128, 512], F32)),
                  ks.enter_context(nc.psum_tensor(P + "kp1", [128, 512], F32))]
            pv = [ks.enter_context(nc.psum_tensor(P + "vp0", [128, 256], F32)),
                  ks.enter_context(nc.psum_tensor(P + "vp1", [128, 256], F32))]
            c_ld = sem("kld")
            c_mmk = sem("kmmk")
            c_mmv = sem("kmmv")
            c_vk = sem("kvk")
            c_vv = sem("kvv")

            @block.gpsimd
            def _(gp):
                gp.wait_ge(c_qv.sem, 96)
                c_ld.inc(gp.dma_start(wk_sb[:, :, :],
                                      wk[:, pas * 256:(pas + 1) * 256].rearrange("(dc p) c -> p dc c", p=128)), 16)
                c_ld.inc(gp.dma_start(wv_sb[:, :, :],
                                      wv[:, pas * 256:(pas + 1) * 256].rearrange("(dc p) c -> p dc c", p=128)), 16)
                gp.wait_ge(c_ld.sem, 32)

            for kb in range(8):
                @block.sync
                def _(s, kb=kb):
                    if kb == 0:
                        s.wait_ge(c_qv.sem, 3 * 32)
                    if kb > 0:
                        s.wait_ge(c_mmv.sem, 8 * kb)
                    for dc in range(16):
                        c_ld.inc(s.dma_start(xnb_f[:, dc, :],
                                             xn_scr[kb * 1024:(kb + 1) * 1024, dc * 128:(dc + 1) * 128],
                                             transpose=True), 16)
                    c_ld.inc(s.dma_start(ropek[:, 0, :], ropeK[0][:, kb * 1024:(kb + 1) * 1024]), 16)
                    c_ld.inc(s.dma_start(ropek[:, 1, :], ropeK[1][:, kb * 1024:(kb + 1) * 1024]), 16)

                for kvl in range(2):
                    for kss in range(2):
                        n = kb * 4 + kvl * 2 + kss
                        par = n % 2
                        sl = slice(kss * 512, (kss + 1) * 512)

                        @block.tensor
                        def _(t, kb=kb, kvl=kvl, n=n, par=par, sl=sl):
                            if kvl == 0 and kss == 0:
                                t.wait_ge(c_ld.sem, 16 * (2 + 18 * (kb + 1)))
                            if n >= 2:
                                t.wait_ge(c_vk.sem, 3 * (n - 2) + 2)
                            mm = None
                            for dc in range(16):
                                mm = t.matmul(pk[par][:, :], wk_sb[:, dc, kvl * 128:(kvl + 1) * 128],
                                              xnb_f[:, dc, sl], start=(dc == 0), stop=(dc == 15))
                            c_mmk.inc(mm)

                        @block.vector
                        def _(v, kb=kb, kvl=kvl, kss=kss, n=n, par=par, sl=sl):
                            v.wait_ge(c_mmk.sem, n + 1)
                            v.tensor_scalar(ksw[0:64, :], pk[par][64:128, :], cst[0:64, 2:3], None, ALU.mult)
                            c_vk.inc(v.tensor_copy(ksw[64:128, :], pk[par][0:64, :]))
                            v.tensor_tensor(kt1[:, :], pk[par][:, :], ropek[:, 0, sl], ALU.mult)
                            v.wait_ge(c_vk.sem, 3 * n + 1)
                            c_vk.inc(v.tensor_tensor(kt2[:, :], ksw[:, :], ropek[:, 1, sl], ALU.mult))
                            v.wait_ge(c_vk.sem, 3 * n + 2)
                            c_vk.inc(v.tensor_tensor(
                                kcache[:, kvl, kb * 1024 + kss * 512:kb * 1024 + (kss + 1) * 512],
                                kt1[:, :], kt2[:, :], ALU.add))

                for tc in range(8):
                    n = kb * 8 + tc
                    par = n % 2

                    @block.tensor
                    def _(t, kb=kb, tc=tc, n=n, par=par):
                        if n >= 2:
                            t.wait_ge(c_vv.sem, n - 1)
                        mm = None
                        for dc in range(16):
                            mm = t.matmul(pv[par][:, :], xnb_f[:, dc, tc * 128:(tc + 1) * 128],
                                          wv_sb[:, dc, :], start=(dc == 0), stop=(dc == 15))
                        c_mmv.inc(mm)

                    @block.vector
                    def _(v, kb=kb, tc=tc, n=n, par=par):
                        v.wait_ge(c_mmv.sem, n + 1)
                        c_vv.inc(v.tensor_copy(vcache[:, kb * 8 + tc, :], pv[par][:, :]))

            if g.get("dbg"):
                @block.sync
                def _(s):
                    s.wait_ge(c_vk.sem, 96)
                    s.wait_ge(c_vv.sem, 64)
                    s.dma_start(g["dbg"]["k"], kcache[:, :, :]).then_inc(c_vk.sem, 16)
                    s.dma_start(g["dbg"]["v"], vcache[:, :, :]).then_inc(c_vk.sem, 16)
                c_vk.n += 32

            @block.tensor
            def _(t):
                t.wait_ge(c_vk.sem, 96)
                t.wait_ge(c_vv.sem, 64)
            @block.scalar
            def _(sc):
                sc.wait_ge(c_vk.sem, 96)
                sc.wait_ge(c_vv.sem, 64)

        with contextlib.ExitStack() as ss:
            sbs = lambda n, shp, dt: ss.enter_context(nc.sbuf_tensor(P + "s" + n, shp, dt))
            pt_sb = sbs("pt", [128, 2, 8, 512], BF16)
            mdiag = sbs("md", [128, 4, 1024], F32)
            l_sb = sbs("l", [1, 512], F32)
            rl_sb = sbs("rl", [1, 512], F32)
            rlb = sbs("rlb", [128, 512], F32)
            obuf = sbs("ob", [128, 2, 512], BF16)
            pol = [ss.enter_context(nc.psum_tensor(P + "ol0", [128, 512], F32)),
                  ss.enter_context(nc.psum_tensor(P + "ol1", [128, 512], F32))]
            pst = [ss.enter_context(nc.psum_tensor(P + "st0", [128, 1024], F32)),
                  ss.enter_context(nc.psum_tensor(P + "st1", [128, 1024], F32))]
            pl_ = ss.enter_context(nc.psum_tensor(P + "L", [1, 512], F32))
            pbc = ss.enter_context(nc.psum_tensor(P + "bc", [128, 512], F32))
            c_sc = sem("ssc")
            c_vd = sem("svd")
            c_tst = sem("stst")
            c_tav = sem("stav")
            c_vt = sem("svt")
            c_vr = sem("svr")
            c_tb = sem("stb")
            c_sto = sem("ssto")

            grounds = 0
            meta = []
            ndiag = 0
            last_pt_user = {}

            neh = 0
            for e in range(1, 9):
                for hp in range(4):
                    kvl = hp // 2
                    sc0 = c_sc.n
                    vd0 = c_vd.n
                    tst0 = c_tst.n
                    tav0 = c_tav.n
                    gr0 = grounds
                    q_rhs = q_all[:, hp, e - 1, :, :]

                    pt_waits = {}
                    for kb in range(e):
                        diag = (kb == e - 1)
                        for r in range(4):
                            gidx = gr0 + 4 * kb + r
                            if diag:
                                ndiag += 1
                                meta.append(("vd", ndiag))
                            else:
                                meta.append(("sc", gidx + 1))
                            pt_waits[(kb, r)] = last_pt_user.get((kb % 2, r), 0)
                    for kb in range(e):
                        for r in range(4):
                            last_pt_user[(kb % 2, r)] = tav0 + 4 * kb + r + 1
                    grounds += 4 * e

                    @block.tensor
                    def _(t, e=e, hp=hp, kvl=kvl, q_rhs=q_rhs, neh=neh, sc0=sc0, gr0=gr0):
                        if neh >= 2:
                            t.wait_ge(c_vt.sem, 3 * (neh - 1))
                        if neh >= 1:
                            t.wait_ge(c_vt.sem, 3 * (neh - 1) + 1)
                        for kb in range(e):
                            for r in range(4):
                                grr = gr0 + 4 * kb + r
                                if grr >= 2:
                                    kind, cnt = meta[grr - 2]
                                    t.wait_ge((c_sc if kind == "sc" else c_vd).sem, cnt)
                                mm = None
                                for kc in (2 * r, 2 * r + 1):
                                    mm = t.matmul(pst[grr % 2][:, (kc % 2) * 512:(kc % 2 + 1) * 512],
                                                  kcache[:, kvl, kb * 1024 + kc * 128:kb * 1024 + (kc + 1) * 128],
                                                  q_rhs, start=True, stop=True,
                                                  skip_group_check=True)
                                c_tst.inc(mm)
                                t.wait_ge(c_sc.sem, sc0 + 4 * kb + r + 1)
                                first_av = (kb == 0 and r == 0)
                                last_av = (kb == e - 1 and r == 3)
                                mm = None
                                for kc in (2 * r, 2 * r + 1):
                                    t.matmul(pol[neh % 2][:, :],
                                             vcache[:, kb * 8 + kc, kvl * 128:(kvl + 1) * 128],
                                             pt_sb[:, kb % 2, kc, :],
                                             start=(first_av and kc == 2 * r), stop=False,
                                             skip_group_check=True)
                                    mm = t.matmul(pl_[0:1, :], ones_bf[:, :],
                                                  pt_sb[:, kb % 2, kc, :],
                                                  start=(first_av and kc == 2 * r),
                                                  stop=(last_av and kc == 2 * r + 1),
                                                  skip_group_check=True)
                                c_tav.inc(mm)
                        t.wait_ge(c_vt.sem, 3 * neh + 2)
                        c_tb.inc(t.matmul(pbc[:, :], ones_rf[:, :], rl_sb[:, :], start=True, stop=True))

                    @block.scalar
                    def _(sc, e=e, neh=neh, sc0=sc0, vd0=vd0, tst0=tst0, gr0=gr0, pt_waits=pt_waits):
                        for kb in range(e):
                            diag = (kb == e - 1)
                            for r in range(4):
                                grr = gr0 + 4 * kb + r
                                pw = pt_waits[(kb, r)]
                                if pw > 0:
                                    sc.wait_ge(c_tav.sem, pw)
                                if diag:
                                    sc.wait_ge(c_vd.sem, vd0 + r + 1)
                                    src = mdiag[:, r, :]
                                else:
                                    sc.wait_ge(c_tst.sem, tst0 + 4 * kb + r + 1)
                                    src = pst[grr % 2][:, :]
                                c_sc.inc(sc.activation(pt_sb[:, kb % 2, 2 * r:2 * r + 2, :], src,
                                                       AF.Exp, bias=cst[:, 6:7], scale=cst[:, 5:6]))

                    @block.vector
                    def _(v, e=e, neh=neh, tst0=tst0, tav0=tav0, gr0=gr0):
                        kb = e - 1
                        for r in range(4):
                            grr = gr0 + 4 * kb + r
                            v.wait_ge(c_tst.sem, tst0 + 4 * kb + r + 1)
                            c_vd.inc(v.tensor_tensor(mdiag[:, r, :], pst[grr % 2][:, :],
                                                     masksb[:, 2 * r:2 * r + 2, :], ALU.add))
                        v.wait_ge(c_tav.sem, tav0 + 4 * e)
                        c_vt.inc(v.tensor_copy(l_sb[:, :], pl_[0:1, :]))
                        v.wait_ge(c_vt.sem, 3 * neh + 1)
                        if neh >= 1:
                            v.wait_ge(c_tb.sem, neh)
                        c_vt.inc(v.reciprocal(rl_sb[:, :], l_sb[:, :]))
                        v.wait_ge(c_tb.sem, neh + 1)
                        c_vr.inc(v.tensor_copy(rlb[:, :], pbc[:, :]))
                        v.wait_ge(c_vr.sem, neh + 1)
                        if neh >= 2:
                            v.wait_ge(c_sto.sem, 32 * (neh - 1))
                        c_vt.inc(v.tensor_tensor(obuf[:, neh % 2, :], pol[neh % 2][:, :],
                                                 rlb[:, :], ALU.mult))

                    @block.sync
                    def _(s, e=e, hp=hp, neh=neh):
                        s.wait_ge(c_vt.sem, 3 * (neh + 1))
                        h0g = pas * 8 + 2 * hp
                        c_sto.inc(s.dma_start(oat_scr[h0g * 128:(h0g + 1) * 128, (e - 1) * 256:e * 256],
                                              obuf[:, neh % 2, 0:256]), 16)
                        c_sto.inc(s.dma_start(oat_scr[(h0g + 1) * 128:(h0g + 2) * 128, (e - 1) * 256:e * 256],
                                              obuf[:, neh % 2, 256:512]), 16)
                    neh += 1

            @block.sync
            def _(s):
                s.wait_ge(c_sto.sem, 32 * 32)

            @block.vector
            def _(v):
                v.wait_ge(c_vt.sem, 3 * 32)

            @block.scalar
            def _(sc):
                sc.wait_ge(c_sc.sem, 576)

            @block.tensor
            def _(t):
                t.wait_ge(c_tb.sem, 32)


def host_prepare(inputs, core, batch, c):
    x = np.asarray(inputs["L0_attn_x"])
    xb = np.ascontiguousarray(x[batch])
    rows = []
    for e in range(1, 9):
        for sub in (c, 7 - c):
            t0 = (e - 1) * 1024 + sub * 128
            rows.append(np.arange(t0, t0 + 128))
    rows = np.concatenate(rows)
    xqv = np.ascontiguousarray(xb[rows])

    inv = (1.0 / (10000.0 ** (np.arange(0, HD, 2) / HD))).astype(np.float64)
    posK = np.arange(S)
    angK = posK[None, :] * np.concatenate([inv, inv])[:, None]
    ropeK = np.stack([np.cos(angK), np.sin(angK)]).astype(np.float32)
    angQ = rows[None, :] * np.concatenate([inv, inv])[:, None]
    ropeQ = np.stack([np.cos(angQ), np.sin(angQ)]).astype(np.float32)

    maskD = np.zeros((8, 128, 512), np.float32)
    for kc in range(8):
        for r, sub in enumerate((c, 7 - c)):
            kabs = kc * 128 + np.arange(128)[:, None]
            qabs = sub * 128 + np.arange(128)[None, :]
            blk = np.where(kabs <= qabs, 0.0, MASK_NEG)
            maskD[kc, :, r * 128:(r + 1) * 128] = blk
            maskD[kc, :, 256 + r * 128:256 + (r + 1) * 128] = blk

    return {
        "xb": xb,
        "xq": xqv,
        "wq": np.asarray(inputs["L0_attn_Wq"]),
        "wk": np.asarray(inputs["L0_attn_Wk"]),
        "wv": np.asarray(inputs["L0_attn_Wv"]),
        "wo": np.asarray(inputs["L0_attn_Wo"]),
        "gwT": np.ascontiguousarray(np.asarray(inputs["L0_ffn_gate_w"]).T),
        "wgu": np.asarray(inputs["L0_ffn_w_gate_up"]),
        "wdn": np.asarray(inputs["L0_ffn_w_down"]),
        "sgu": np.asarray(inputs["L0_ffn_shared_gate_up"]),
        "sdn": np.asarray(inputs["L0_ffn_shared_down"]),
        "identin": np.eye(128, dtype=np.float32),
        "ropeK": ropeK,
        "ropeQ": ropeQ,
        "maskD": maskD,
    }, rows


def kernel(**inputs):
    nc = build(debug=False)
    in_maps = []
    row_list = []
    for core in range(8):
        batch, c = divmod(core, 4)
        m, rows = host_prepare(inputs, core, batch, c)
        in_maps.append(m)
        row_list.append(rows)
    res = run_bass_kernel_spmd(nc, in_maps, core_ids=list(range(8)))
    out = np.zeros((2, S, D), np.float32)
    for core in range(8):
        batch = core // 4
        out[batch, row_list[core]] = res.results[core]["out"]
    return out


# version 06
# speedup vs baseline: 1.1975x; 1.1975x over previous
import numpy as np
import ml_dtypes
import contextlib
import concourse.bass as bass
import concourse.mybir as mybir
from concourse.bass_utils import run_bass_kernel_spmd

F32 = mybir.dt.float32
BF16 = mybir.dt.bfloat16
AF = mybir.ActivationFunctionType
ALU = mybir.AluOpType
AX = mybir.AxisListType

S = 8192
D = 2048
HD = 128
NH = 16
NKV = 4
I_FF = 1408
NE = 16
SI = 2816
NT = 2048
EXP_BIAS = -8.0
EXP_SCALE = float(1.0 / np.sqrt(HD))
MASK_NEG = -1.0e9


class Ctr:
    def __init__(self, sem):
        self.sem = sem
        self.n = 0

    def inc(self, inst, k=1):
        inst.then_inc(self.sem, k)
        self.n += k
        return self.n


def build(debug=False, upto='p6'):
    nc = bass.Bass("TRN2", target_bir_lowering=False, debug=False)

    xb = nc.dram_tensor("xb", [S, D], F32, kind="ExternalInput").ap()
    xq = nc.dram_tensor("xq", [NT, D], F32, kind="ExternalInput").ap()
    wq = nc.dram_tensor("wq", [D, D], F32, kind="ExternalInput").ap()
    wk = nc.dram_tensor("wk", [D, NKV * HD], F32, kind="ExternalInput").ap()
    wv = nc.dram_tensor("wv", [D, NKV * HD], F32, kind="ExternalInput").ap()
    wo = nc.dram_tensor("wo", [D, D], F32, kind="ExternalInput").ap()
    gwT = nc.dram_tensor("gwT", [D, NE], F32, kind="ExternalInput").ap()
    wgu = nc.dram_tensor("wgu", [NE, D, 2 * I_FF], F32, kind="ExternalInput").ap()
    wdn = nc.dram_tensor("wdn", [NE, I_FF, D], F32, kind="ExternalInput").ap()
    sgu = nc.dram_tensor("sgu", [D, 2 * SI], F32, kind="ExternalInput").ap()
    sdn = nc.dram_tensor("sdn", [SI, D], F32, kind="ExternalInput").ap()
    identin = nc.dram_tensor("identin", [128, 128], F32, kind="ExternalInput").ap()
    ropeK = nc.dram_tensor("ropeK", [2, 128, S], F32, kind="ExternalInput").ap()
    ropeQ = nc.dram_tensor("ropeQ", [2, 128, NT], F32, kind="ExternalInput").ap()
    maskD = nc.dram_tensor("maskD", [8, 128, 512], F32, kind="ExternalInput").ap()

    out = nc.dram_tensor("out", [NT, D], F32, kind="ExternalOutput").ap()

    skind = "ExternalOutput" if debug else "Internal"
    xn_scr = nc.dram_tensor("xn_scr", [S, D], BF16, kind=skind).ap()
    xnq_scr = nc.dram_tensor("xnq_scr", [NT, D], BF16, kind=skind).ap()
    xnqlo_scr = nc.dram_tensor("xnqlo_scr", [NT, D], BF16, kind=skind).ap()
    oat_scr = nc.dram_tensor("oat_scr", [D, NT], BF16, kind=skind).ap()
    s_scr = nc.dram_tensor("s_scr", [D, NT], BF16, kind=skind).ap()
    m_scr = nc.dram_tensor("m_scr", [D, NT], BF16, kind=skind).ap()
    cmb_scr = nc.dram_tensor("cmb_scr", [NE, NT], F32, kind=skind).ap()
    dbg_q = nc.dram_tensor("dbg_q", [128, 8 * NT], BF16, kind=skind).ap()
    dbg_k = nc.dram_tensor("dbg_k", [128, 2 * S], BF16, kind=skind).ap()
    dbg_v = nc.dram_tensor("dbg_v", [128, 64 * 256], BF16, kind=skind).ap()

    with contextlib.ExitStack() as stk:
        cst = stk.enter_context(nc.sbuf_tensor("cst", [128, 8], F32))
        ones_bf = stk.enter_context(nc.sbuf_tensor("ones_bf", [128, 1], BF16))
        ones_row = stk.enter_context(nc.sbuf_tensor("ones_row", [1, 128], BF16))
        ones_rf = stk.enter_context(nc.sbuf_tensor("ones_rf", [1, 128], F32))
        ident = stk.enter_context(nc.sbuf_tensor("ident", [128, 128], F32))
        attn_scope = contextlib.ExitStack()
        masksb = attn_scope.enter_context(nc.sbuf_tensor("masksb", [128, 8, 512], BF16))
        identb = attn_scope.enter_context(nc.sbuf_tensor("identb", [128, 128], BF16))

        ph = stk.enter_context(nc.semaphore("ph"))
        c_init = Ctr(stk.enter_context(nc.semaphore("c_init")))
        phn = [0]

        block = stk.enter_context(nc.Block())
        mk_sem = lambda n: stk.enter_context(nc.semaphore(n))

        def barrier(tag):
            phn[0] += 5
            tgt = phn[0]
            for name in ("sync", "scalar", "vector", "tensor", "gpsimd"):
                sec = getattr(block, name)

                @sec
                def _(eng, tgt=tgt):
                    eng.nop().then_inc(ph, 1)
                    eng.wait_ge(ph, tgt)

        @block.gpsimd
        def _(g):
            g.memset(cst[:, 0:1], 1e-5)
            g.memset(cst[:, 1:2], 1.0)
            g.memset(cst[:, 2:3], -1.0)
            g.memset(cst[:, 3:4], 4.0)
            g.memset(cst[:, 4:5], 16.0)
            g.memset(cst[:, 5:6], EXP_SCALE)
            g.memset(cst[:, 6:7], EXP_BIAS)
            g.memset(ones_bf[:, :], 1.0)
            g.memset(ones_row[:, :], 1.0)
            c_init.inc(g.memset(ones_rf[:, :], 1.0))

        @block.sync
        def _(s):
            c_init.inc(s.dma_start(ident[:, :], identin), 16)

        @block.gpsimd
        def _(g):
            c_init.inc(g.dma_start(masksb[:, :, :], maskD.rearrange("kc k q -> k kc q")), 16)
            c_init.inc(g.dma_start(identb[:, :], identin), 16)

        def wait_init(eng):
            eng.wait_ge(c_init.sem, 49)

        for _name in ("sync", "scalar", "vector", "tensor", "gpsimd"):
            @getattr(block, _name)
            def _(eng):
                wait_init(eng)

        with contextlib.ExitStack() as ps:
            NXB = 4
            xbuf = ps.enter_context(nc.sbuf_tensor("p1_xbuf", [128, NXB, D], F32))
            xnb = ps.enter_context(nc.sbuf_tensor("p1_xnb", [128, 2, D], BF16))
            lob = ps.enter_context(nc.sbuf_tensor("p1_lob", [128, 2, D], BF16))
            xnf = ps.enter_context(nc.sbuf_tensor("p1_xnf", [128, D], F32))
            stats = ps.enter_context(nc.sbuf_tensor("p1_stats", [128, 4, 6], F32))
            aggr = ps.enter_context(nc.sbuf_tensor("p1_aggr", [128, 2], F32))
            mv = ps.enter_context(nc.sbuf_tensor("p1_mv", [128, 4], F32))
            c_ld = Ctr(mk_sem("p1ld"))
            c_v = Ctr(mk_sem("p1v"))
            c_sc = Ctr(mk_sem("p1sc"))
            c_st = Ctr(mk_sem("p1st"))

            tiles = [("b", i) for i in range(64)] + [("l", i) for i in range(16)]
            cnt = [8 if k == "l" else 6 for k, _ in tiles]
            vc = [0]
            for cn in cnt:
                vc.append(vc[-1] + cn)
            nst = [2 if k == "l" else 1 for k, _ in tiles]
            spre = [0]
            for sn in nst:
                spre.append(spre[-1] + sn)

            @block.sync
            def _(s):
                for t, (kind, i) in enumerate(tiles):
                    src_ = xb[i * 128:(i + 1) * 128, :] if kind == "b" else xq[i * 128:(i + 1) * 128, :]
                    if t >= NXB:
                        s.wait_ge(c_v.sem, vc[t - NXB + 1])
                    c_ld.inc(s.dma_start(xbuf[:, t % NXB, :], src_), 16)

            @block.vector
            def _(v):
                for t, (kind, i) in enumerate(tiles):
                    v.wait_ge(c_ld.sem, 16 * (t + 1))
                    x_t = xbuf[:, t % NXB, :]
                    for j in range(3):
                        v.bn_stats(stats[:, j, :], x_t[:, j * 512:(j + 1) * 512])
                    c_v.inc(v.bn_stats(stats[:, 3, :], x_t[:, 3 * 512:4 * 512]))
                    v.wait_ge(c_v.sem, c_v.n)
                    c_v.inc(v.bn_aggr(aggr[:, :], stats[:, :, :]))
                    v.wait_ge(c_v.sem, c_v.n)
                    c_v.inc(v.tensor_scalar(mv[:, 0:1], aggr[:, 1:2], cst[:, 0:1], None, ALU.add))
                    v.wait_ge(c_sc.sem, t + 1)
                    c_v.inc(v.reciprocal(mv[:, 2:3], mv[:, 1:2]))
                    v.wait_ge(c_v.sem, c_v.n)
                    c_v.inc(v.scalar_tensor_tensor(mv[:, 3:4], aggr[:, 0:1], cst[:, 2:3], mv[:, 2:3],
                                                   ALU.mult, ALU.mult))
                    v.wait_ge(c_v.sem, c_v.n)
                    if t >= 2:
                        v.wait_ge(c_st.sem, 16 * spre[t - 1])
                    if kind == "l":
                        c_v.inc(v.tensor_scalar(xnf[:, :], x_t, mv[:, 2:3], mv[:, 3:4], ALU.mult, ALU.add))
                        v.wait_ge(c_v.sem, c_v.n)
                        c_v.inc(v.tensor_copy(xnb[:, t % 2, :], xnf[:, :]))
                        v.wait_ge(c_v.sem, c_v.n)
                        c_v.inc(v.tensor_sub(lob[:, t % 2, :], xnf[:, :], xnb[:, t % 2, :]))
                    else:
                        c_v.inc(v.tensor_scalar(xnb[:, t % 2, :], x_t, mv[:, 2:3], mv[:, 3:4],
                                                ALU.mult, ALU.add))

            @block.scalar
            def _(sc):
                for t in range(len(tiles)):
                    sc.wait_ge(c_v.sem, vc[t] + 3)
                    c_sc.inc(sc.activation(mv[:, 1:2], mv[:, 0:1], AF.Sqrt))

            @block.gpsimd
            def _(g):
                for t, (kind, i) in enumerate(tiles):
                    g.wait_ge(c_v.sem, vc[t + 1])
                    if kind == "b":
                        c_st.inc(g.dma_start(xn_scr[i * 128:(i + 1) * 128, :], xnb[:, t % 2, :]), 16)
                    else:
                        c_st.inc(g.dma_start(xnq_scr[i * 128:(i + 1) * 128, :], xnb[:, t % 2, :]), 16)
                        c_st.inc(g.dma_start(xnqlo_scr[i * 128:(i + 1) * 128, :], lob[:, t % 2, :]), 16)

            @block.sync
            def _(s):
                s.wait_ge(c_st.sem, 16 * spre[-1])

            @block.vector
            def _(v):
                v.wait_ge(c_v.sem, vc[-1])

            @block.scalar
            def _(sc):
                sc.wait_ge(c_sc.sem, len(tiles))

            @block.gpsimd
            def _(g2):
                g2.wait_ge(c_st.sem, 16 * spre[-1])
        barrier("ln")

        for pas in range(2):
            emit_attention_pass(nc, block, pas, dict(
                cst=cst, ones_bf=ones_bf, ones_row=ones_row, ones_rf=ones_rf,
                masksb=masksb, identb=identb, wq=wq, wk=wk, wv=wv, xn_scr=xn_scr,
                xnq_scr=xnq_scr, ropeK=ropeK, ropeQ=ropeQ, oat_scr=oat_scr,
                mk_sem=mk_sem,
                dbg=(dict(q=dbg_q, k=dbg_k, v=dbg_v) if (debug and pas == 0) else None)))
            barrier(f"attn{pas}")
        attn_scope.close()
        if upto == 'p3':
            return nc

        with contextlib.ExitStack() as ps:
            sb4 = lambda n, shp, dt: ps.enter_context(nc.sbuf_tensor("p4" + n, shp, dt))
            wo_sb = sb4("wo", [128, 16, D], BF16)
            o_sb = sb4("o", [128, 2, 16, 512], BF16)
            s_sb = sb4("s", [128, 2, 512], BF16)
            pso = [ps.enter_context(nc.psum_tensor("p4p0", [128, 512], F32)),
                   ps.enter_context(nc.psum_tensor("p4p1", [128, 512], F32))]
            c_lw = Ctr(mk_sem("p4lw"))
            c_lo = Ctr(mk_sem("p4lo"))
            c_mm = Ctr(mk_sem("p4mm"))
            c_ev = Ctr(mk_sem("p4ev"))
            c_st4 = Ctr(mk_sem("p4st"))

            @block.gpsimd
            def _(gp):
                for i in range(4):
                    c_lw.inc(gp.dma_start(wo_sb[:, 4 * i:4 * (i + 1), :],
                                          wo[512 * i:512 * (i + 1), :].rearrange("(hc p) d -> p hc d", p=128)), 16)

            @block.sync
            def _(s):
                for qs4 in range(4):
                    if qs4 >= 2:
                        s.wait_ge(c_mm.sem, 16 * (qs4 - 1))
                    for hc in range(16):
                        c_lo.inc(s.dma_start(o_sb[:, qs4 % 2, hc, :],
                                             oat_scr[hc * 128:(hc + 1) * 128, qs4 * 512:(qs4 + 1) * 512]), 16)

            nmm = 0
            for qs4 in range(4):
                for dc in range(16):
                    @block.tensor
                    def _(t, qs4=qs4, dc=dc, n=nmm):
                        if n == 0:
                            t.wait_ge(c_lw.sem, 16 * 4)
                        if dc == 0:
                            t.wait_ge(c_lo.sem, 16 * 16 * (qs4 + 1))
                        if n >= 2:
                            t.wait_ge(c_ev.sem, n - 1)
                        mm = None
                        for hc in range(16):
                            mm = t.matmul(pso[n % 2][:, :], wo_sb[:, hc, dc * 128:(dc + 1) * 128],
                                          o_sb[:, qs4 % 2, hc, :], start=(hc == 0), stop=(hc == 15))
                        c_mm.inc(mm)

                    @block.vector
                    def _(v, qs4=qs4, dc=dc, n=nmm):
                        v.wait_ge(c_mm.sem, n + 1)
                        if n >= 2:
                            v.wait_ge(c_st4.sem, 16 * (n - 1))
                        c_ev.inc(v.tensor_copy(s_sb[:, n % 2, :], pso[n % 2][:, :]))

                    @block.scalar
                    def _(sc, qs4=qs4, dc=dc, n=nmm):
                        sc.wait_ge(c_ev.sem, n + 1)
                        c_st4.inc(sc.dma_start(s_scr[dc * 128:(dc + 1) * 128, qs4 * 512:(qs4 + 1) * 512],
                                               s_sb[:, n % 2, :]), 16)
                    nmm += 1

            @block.scalar
            def _(sc):
                sc.wait_ge(c_st4.sem, 16 * 64)
        barrier("oproj")
        if upto == 'p4':
            return nc

        emit_ffn(nc, block, dict(
            cst=cst, ones_rf=ones_rf, ident=ident, mk_sem=mk_sem,
            gwT=gwT, wgu=wgu, wdn=wdn, sgu=sgu, sdn=sdn,
            xnq_scr=xnq_scr, xnqlo_scr=xnqlo_scr, s_scr=s_scr, m_scr=m_scr, cmb_scr=cmb_scr))
        barrier("ffn")
        if upto == 'p5':
            return nc

        with contextlib.ExitStack() as ps:
            sb6 = lambda n, shp, dt: ps.enter_context(nc.sbuf_tensor("p6" + n, shp, dt))
            m_tm = sb6("m", [128, 2, D], BF16)
            x_tm = sb6("x", [128, 2, D], F32)
            o_tm = sb6("o", [128, 2, D], F32)
            c_l6 = Ctr(mk_sem("p6l"))
            c_v6 = Ctr(mk_sem("p6v"))
            c_o6 = Ctr(mk_sem("p6o"))

            for qt in range(16):
                @block.sync
                def _(s, qt=qt):
                    if qt >= 2:
                        s.wait_ge(c_v6.sem, qt - 1)
                    c_l6.inc(s.dma_start(m_tm[:, qt % 2, :],
                                         m_scr[:, qt * 128:(qt + 1) * 128], transpose=True), 16)
                    c_l6.inc(s.dma_start(x_tm[:, qt % 2, :], xq[qt * 128:(qt + 1) * 128, :]), 16)

                @block.vector
                def _(v, qt=qt):
                    v.wait_ge(c_l6.sem, 32 * (qt + 1))
                    if qt >= 2:
                        v.wait_ge(c_o6.sem, 16 * (qt - 1))
                    c_v6.inc(v.tensor_tensor(o_tm[:, qt % 2, :], x_tm[:, qt % 2, :],
                                             m_tm[:, qt % 2, :], ALU.add))

                @block.scalar
                def _(sc, qt=qt):
                    sc.wait_ge(c_v6.sem, qt + 1)
                    c_o6.inc(sc.dma_start(out[qt * 128:(qt + 1) * 128, :], o_tm[:, qt % 2, :]), 16)

            @block.scalar
            def _(sc):
                sc.wait_ge(c_o6.sem, 16 * 16)

    return nc


def emit_ffn(nc, block, g):
    cst, ones_rf, ident, mk_sem = g["cst"], g["ones_rf"], g["ident"], g["mk_sem"]
    gwT, wgu, wdn, sgu, sdn = g["gwT"], g["wgu"], g["wdn"], g["sgu"], g["sdn"]
    xnq_scr, s_scr, m_scr, cmb_scr = g["xnq_scr"], g["s_scr"], g["m_scr"], g["cmb_scr"]
    xnqlo_scr = g["xnqlo_scr"]
    with contextlib.ExitStack() as ps:
        sb = lambda n, shp, dt: ps.enter_context(nc.sbuf_tensor("f_" + n, shp, dt))
        xnq_f = sb("xn", [128, 16, NT], BF16)
        c_ldx = Ctr(mk_sem("fldx"))
        c_rt = Ctr(mk_sem("frt"))
        c_rv = Ctr(mk_sem("frv"))
        c_rs = Ctr(mk_sem("frs"))
        c_cst = Ctr(mk_sem("fcst"))

        @block.sync
        def _(s):
            for dc in range(16):
                for q4 in range(4):
                    c_ldx.inc(s.dma_start(xnq_f[:, dc, q4 * 512:(q4 + 1) * 512],
                                          xnq_scr[q4 * 512:(q4 + 1) * 512, dc * 128:(dc + 1) * 128],
                                          transpose=True), 16)

        with contextlib.ExitStack() as rs:
            sbr = lambda n, shp, dt: rs.enter_context(nc.sbuf_tensor("fr" + n, shp, dt))
            cmbT = sbr("cT", [16, NT], F32)
            xnqlo_f = sbr("xlo", [128, 16, NT], BF16)
            gw_sb = sbr("gw", [128, 16, 16], BF16)
            gwf_sb = sbr("gwf", [128, 16, 16], F32)
            gwlo_sb = sbr("gwlo", [128, 16, 16], BF16)
            lg = sbr("lg", [128, 16], F32)
            gm = sbr("gm", [128, 4], F32)
            t6 = sbr("t6", [128, 8], F32)
            keep = sbr("keep", [128, 4], F32)
            ml = sbr("ml", [128, 16], F32)
            ex = sbr("ex", [128, 16], F32)
            red = sbr("red", [128, 4], F32)
            probs = sbr("probs", [128, 16], F32)
            vv = sbr("vv", [128, 8], F32)
            ind = sbr("ind", [128, 4], F32)
            eqs = sbr("eqs", [128, 16], F32)
            work = sbr("work", [128, 16], F32)
            work2 = sbr("work2", [128, 16], F32)
            strict = sbr("strict", [128, 16], F32)
            cums = sbr("cums", [128, 16], F32)
            cums2 = sbr("cums2", [128, 16], F32)
            sel = sbr("sel", [128, 16], F32)
            cmb = sbr("cmb", [128, 16], F32)
            plog = rs.enter_context(nc.psum_tensor("frpl", [128, 16], F32))
            ptp = rs.enter_context(nc.psum_tensor("frpt", [16, 128], F32))

            @block.gpsimd
            def _(gp):
                c_ldx.inc(gp.dma_start(gw_sb[:, :, :], gwT.rearrange("(dc p) e -> p dc e", p=128)), 16)

            @block.sync
            def _(s):
                c_ldx.inc(s.dma_start(gwf_sb[:, :, :], gwT.rearrange("(dc p) e -> p dc e", p=128)), 16)
                for dc in range(16):
                    for q4 in range(4):
                        c_ldx.inc(s.dma_start(xnqlo_f[:, dc, q4 * 512:(q4 + 1) * 512],
                                              xnqlo_scr[q4 * 512:(q4 + 1) * 512, dc * 128:(dc + 1) * 128],
                                              transpose=True), 16)

            @block.vector
            def _(v):
                v.wait_ge(c_ldx.sem, 16 * 66)
                c_rv.inc(v.tensor_sub(gwlo_sb[:, :, :], gwf_sb[:, :, :], gw_sb[:, :, :]))
                v.wait_ge(c_rv.sem, c_rv.n)

            rv_marks = []
            for tt in range(16):
                @block.tensor
                def _(t, tt=tt):
                    if tt == 0:
                        t.wait_ge(c_ldx.sem, 16 * 130)
                        t.wait_ge(c_rv.sem, 1)
                    if tt >= 1:
                        t.wait_ge(c_rv.sem, rv_marks[tt - 1]["end"])
                    mm = None
                    sl = slice(tt * 128, (tt + 1) * 128)
                    for dc in range(16):
                        t.matmul(plog[:, :], xnq_f[:, dc, sl], gw_sb[:, dc, :],
                                 start=(dc == 0), stop=False, skip_group_check=True)
                    for dc in range(16):
                        t.matmul(plog[:, :], xnq_f[:, dc, sl], gwlo_sb[:, dc, :],
                                 start=False, stop=False, skip_group_check=True)
                    for dc in range(16):
                        mm = t.matmul(plog[:, :], xnqlo_f[:, dc, sl], gw_sb[:, dc, :],
                                      start=False, stop=(dc == 15), skip_group_check=True)
                    c_rt.inc(mm)

                marks = {}
                rv_marks.append(marks)

                @block.vector
                def _(v, tt=tt, marks=marks):
                    def stepw(inst):
                        c_rv.inc(inst)
                        v.wait_ge(c_rv.sem, c_rv.n)
                    v.wait_ge(c_rt.sem, 2 * tt + 1)
                    stepw(v.tensor_copy(lg[:, :], plog[:, :]))
                    lgv = lg[:, :].rearrange("p (g j) -> p g j", j=4)
                    v.tensor_tensor(t6[:, 0:4], lgv[:, :, 0], lgv[:, :, 1], ALU.max)
                    stepw(v.tensor_tensor(keep[:, 0:4], lgv[:, :, 2], lgv[:, :, 3], ALU.max))
                    stepw(v.tensor_tensor(gm[:, :], t6[:, 0:4], keep[:, :], ALU.max))
                    v.tensor_tensor(t6[:, 0:1], gm[:, 0:1], gm[:, 1:2], ALU.max)
                    v.tensor_tensor(t6[:, 1:2], gm[:, 0:1], gm[:, 1:2], ALU.min)
                    v.tensor_tensor(t6[:, 2:3], gm[:, 2:3], gm[:, 3:4], ALU.max)
                    stepw(v.tensor_tensor(t6[:, 3:4], gm[:, 2:3], gm[:, 3:4], ALU.min))
                    v.tensor_tensor(t6[:, 4:5], t6[:, 0:1], t6[:, 2:3], ALU.min)
                    stepw(v.tensor_tensor(t6[:, 5:6], t6[:, 1:2], t6[:, 3:4], ALU.max))
                    stepw(v.tensor_tensor(t6[:, 6:7], t6[:, 4:5], t6[:, 5:6], ALU.max))
                    stepw(v.tensor_scalar(keep[:, :], gm[:, :], t6[:, 6:7], None, ALU.is_ge))
                    for j in range(4):
                        inst = v.tensor_tensor(ml[:, :].rearrange("p (g j) -> p g j", j=4)[:, :, j],
                                               lgv[:, :, j], keep[:, :], ALU.mult)
                    stepw(inst)
                    stepw(v.reduce_max(red[:, 0:1], ml[:, :], axis=AX.X, negate=True))
                    marks["rm"] = c_rv.n
                    v.wait_ge(c_rs.sem, tt + 1)
                    stepw(v.reduce_sum(red[:, 1:2], ex[:, :], axis=AX.X))
                    stepw(v.reciprocal(red[:, 2:3], red[:, 1:2]))
                    stepw(v.tensor_scalar(probs[:, :], ex[:, :], red[:, 2:3], None, ALU.mult))
                    stepw(v.tensor_copy(work[:, :], probs[:, :]))
                    for r in range(4):
                        stepw(v.reduce_max(vv[:, r:r + 1], work[:, :], axis=AX.X))
                        if r < 3:
                            stepw(v.tensor_scalar(eqs[:, :], work[:, :], vv[:, r:r + 1], None, ALU.is_ge))
                            stepw(v.reduce_sum(vv[:, 4 + r:5 + r], eqs[:, :], axis=AX.X))
                            if r > 0:
                                stepw(v.tensor_tensor(vv[:, 4 + r:5 + r], vv[:, 4 + r:5 + r],
                                                      vv[:, 3 + r:4 + r], ALU.add))
                            v.scalar_tensor_tensor(work2[:, :], eqs[:, :], cst[:, 2:3], work[:, :],
                                                   ALU.mult, ALU.add)
                            stepw(v.nop())
                            stepw(v.tensor_copy(work[:, :], work2[:, :]))
                    v.tensor_scalar(ind[:, 0:1], vv[:, 4:5], cst[:, 3:4], None, ALU.is_ge)
                    v.tensor_scalar(ind[:, 1:2], vv[:, 5:6], cst[:, 3:4], None, ALU.is_ge)
                    stepw(v.tensor_scalar(ind[:, 2:3], vv[:, 6:7], cst[:, 3:4], None, ALU.is_ge))
                    stepw(v.tensor_tensor(ind[:, 1:2], ind[:, 1:2], ind[:, 0:1], ALU.subtract))
                    v.tensor_tensor(ind[:, 2:3], ind[:, 2:3], ind[:, 0:1], ALU.subtract)
                    stepw(v.nop())
                    stepw(v.tensor_tensor(ind[:, 2:3], ind[:, 2:3], ind[:, 1:2], ALU.subtract))
                    v.tensor_tensor(vv[:, 7:8], ind[:, 0:1], vv[:, 0:1], ALU.mult)
                    stepw(v.tensor_tensor(t6[:, 7:8], ind[:, 1:2], vv[:, 1:2], ALU.mult))
                    stepw(v.tensor_tensor(vv[:, 7:8], vv[:, 7:8], t6[:, 7:8], ALU.add))
                    stepw(v.tensor_tensor(t6[:, 7:8], ind[:, 2:3], vv[:, 2:3], ALU.mult))
                    stepw(v.tensor_tensor(vv[:, 7:8], vv[:, 7:8], t6[:, 7:8], ALU.add))
                    stepw(v.tensor_tensor(t6[:, 7:8], ind[:, 0:1], ind[:, 1:2], ALU.add))
                    stepw(v.tensor_tensor(t6[:, 7:8], t6[:, 7:8], ind[:, 2:3], ALU.add))
                    stepw(v.scalar_tensor_tensor(t6[:, 7:8], t6[:, 7:8], cst[:, 2:3], cst[:, 1:2],
                                                 ALU.mult, ALU.add))
                    stepw(v.tensor_tensor(t6[:, 7:8], t6[:, 7:8], vv[:, 3:4], ALU.mult))
                    stepw(v.tensor_tensor(vv[:, 7:8], vv[:, 7:8], t6[:, 7:8], ALU.add))
                    v.tensor_scalar(strict[:, :], probs[:, :], vv[:, 7:8], None, ALU.is_gt)
                    stepw(v.tensor_scalar(eqs[:, :], probs[:, :], vv[:, 7:8], None, ALU.is_equal))
                    stepw(v.reduce_sum(ind[:, 3:4], strict[:, :], axis=AX.X, negate=True))
                    stepw(v.tensor_scalar(ind[:, 3:4], ind[:, 3:4], cst[:, 3:4], None, ALU.add))
                    stepw(v.tensor_copy(cums[:, :], eqs[:, :]))
                    for k, (s_, d_) in zip([1, 2, 4, 8],
                                           [(cums, cums2), (cums2, cums), (cums, cums2), (cums2, cums)]):
                        v.tensor_copy(d_[:, 0:k], s_[:, 0:k])
                        stepw(v.tensor_tensor(d_[:, k:16], s_[:, k:16], s_[:, 0:16 - k], ALU.add))
                    stepw(v.tensor_scalar(cums2[:, :], cums[:, :], ind[:, 3:4], None, ALU.is_le))
                    stepw(v.tensor_tensor(sel[:, :], eqs[:, :], cums2[:, :], ALU.mult))
                    stepw(v.tensor_tensor(sel[:, :], sel[:, :], strict[:, :], ALU.add))
                    stepw(v.scalar_tensor_tensor(cmb[:, :], sel[:, :], cst[:, 4:5], probs[:, :],
                                                 ALU.mult, ALU.mult))
                    marks["cmb"] = c_rv.n
                    v.wait_ge(c_rt.sem, 2 * (tt + 1))
                    stepw(v.tensor_copy(cmbT[:, tt * 128:(tt + 1) * 128], ptp[0:16, :]))
                    marks["end"] = c_rv.n

                @block.scalar
                def _(sc, tt=tt, marks=marks):
                    sc.wait_ge(c_rv.sem, marks["rm"])
                    c_rs.inc(sc.activation(ex[:, :], ml[:, :], AF.Exp, bias=red[:, 0:1]))

                @block.tensor
                def _(t, tt=tt, marks=marks):
                    t.wait_ge(c_rv.sem, marks["cmb"])
                    c_rt.inc(t.transpose(ptp[0:16, 0:128], cmb[:, 0:16], ident[:, :]))

            @block.sync
            def _(s):
                s.wait_ge(c_rv.sem, rv_marks[-1]["end"])
                c_cst.inc(s.dma_start(cmb_scr, cmbT[:, :]), 16)

        Y = sb("Y", [128, 16, 1024], F32)
        hs = sb("hs", [128, 11, 1024], BF16)
        wgu_sb = sb("wgu", [128, 3, 16, 256], BF16)
        wd_sb = sb("wd", [128, 2, 11, 256], BF16)
        cbc = sb("cbc", [128, 1024], F32)
        cr = sb("cr", [1, 1024], F32)
        t1s = sb("t1", [128, 2, 512], BF16)
        t2s = sb("t2", [128, 1, 512], F32)
        g_sb = sb("g", [128, 2, 1024], BF16)
        ss_sb = sb("ss", [128, 1, 1024], BF16)
        m_sb = sb("m", [128, 1, 1024], BF16)
        pg = [ps.enter_context(nc.psum_tensor("fpg%d" % i, [128, 512], F32)) for i in range(2)]
        pu = [ps.enter_context(nc.psum_tensor("fpu%d" % i, [128, 512], F32)) for i in range(2)]
        py = [ps.enter_context(nc.psum_tensor("fpy%d" % i, [128, 512], F32)) for i in range(2)]
        pbc2 = ps.enter_context(nc.psum_tensor("fpbc", [128, 512], F32))
        c_wl = Ctr(mk_sem("fwl"))
        c_tg = Ctr(mk_sem("ftg"))
        c_ty = Ctr(mk_sem("fty"))
        c_tb2 = Ctr(mk_sem("ftb"))
        c_s5 = Ctr(mk_sem("fs5"))
        c_v5 = Ctr(mk_sem("fv5"))
        c_vy = Ctr(mk_sem("fvy"))
        c_vb = Ctr(mk_sem("fvb"))
        c_crl = Ctr(mk_sem("fcrl"))
        c_sld = Ctr(mk_sem("fsld"))
        c_mst = Ctr(mk_sem("fmst"))

        experts = [("sA", None), ("sB", None)] + [("e", i) for i in range(NE)]

        for half in range(2):
            h0 = half * 1024
            for exi, (kind, ei) in enumerate(experts):
                first = exi == 0
                routed = kind == "e"
                if routed:
                    @block.sync
                    def _(s, ei=ei, half=half):
                        s.wait_ge(c_cst.sem, 16)
                        s.wait_ge(c_tb2.sem, c_tb2.n)
                        c_crl.inc(s.dma_start(cr[:, :], cmb_scr[ei:ei + 1, h0:h0 + 1024]), 16)

                    @block.tensor
                    def _(t):
                        t.wait_ge(c_crl.sem, c_crl.n)
                        t.wait_ge(c_vb.sem, c_vb.n)
                        c_tb2.inc(t.matmul(pbc2[:, :], ones_rf[:, :], cr[:, 0:512],
                                           start=True, stop=True))

                    @block.vector
                    def _(v):
                        v.wait_ge(c_tb2.sem, c_tb2.n)
                        c_vb.inc(v.tensor_copy(cbc[:, 0:512], pbc2[:, :]))

                    @block.tensor
                    def _(t):
                        t.wait_ge(c_vb.sem, c_vb.n)
                        c_tb2.inc(t.matmul(pbc2[:, :], ones_rf[:, :], cr[:, 512:1024],
                                           start=True, stop=True))

                    @block.vector
                    def _(v):
                        v.wait_ge(c_tb2.sem, c_tb2.n)
                        c_vb.inc(v.tensor_copy(cbc[:, 512:1024], pbc2[:, :]))

                if exi == 0 and half == 0:
                    pg_free = {0: None, 1: None}
                    py_free = {0: None, 1: None}
                for c in range(11):
                    par = (exi * 11 + c) % 3
                    if kind == "e":
                        gsrc = wgu[ei][:, c * 128:(c + 1) * 128]
                        usrc = wgu[ei][:, I_FF + c * 128:I_FF + (c + 1) * 128]
                    elif kind == "sA":
                        gsrc = sgu[:, c * 128:(c + 1) * 128]
                        usrc = sgu[:, SI + c * 128:SI + (c + 1) * 128]
                    else:
                        gsrc = sgu[:, I_FF + c * 128:I_FF + (c + 1) * 128]
                        usrc = sgu[:, SI + I_FF + c * 128:SI + I_FF + (c + 1) * 128]

                    @block.gpsimd
                    def _(gp, gsrc=gsrc, usrc=usrc, par=par, ntg=c_tg.n):
                        if ntg >= 4:
                            gp.wait_ge(c_tg.sem, ntg - 3)
                        c_wl.inc(gp.dma_start(wgu_sb[:, par, :, 0:128],
                                              gsrc.rearrange("(dc p) f -> p dc f", p=128)), 16)
                        c_wl.inc(gp.dma_start(wgu_sb[:, par, :, 128:256],
                                              usrc.rearrange("(dc p) f -> p dc f", p=128)), 16)

                    for ts in range(2):
                        sl = slice(h0 + ts * 512, h0 + (ts + 1) * 512)

                        @block.tensor
                        def _(t, par=par, ts=ts, sl=sl, nwl=c_wl.n):
                            if ts == 0:
                                t.wait_ge(c_wl.sem, nwl)
                            if pg_free[ts] is not None:
                                s_n, v_n = pg_free[ts]
                                t.wait_ge(c_s5.sem, s_n)
                                t.wait_ge(c_v5.sem, v_n)
                            mm = None
                            for dc in range(16):
                                mm = t.matmul(pg[ts][:, :], wgu_sb[:, par, dc, 0:128],
                                              xnq_f[:, dc, sl], start=(dc == 0), stop=(dc == 15))
                            for dc in range(16):
                                mm = t.matmul(pu[ts][:, :], wgu_sb[:, par, dc, 128:256],
                                              xnq_f[:, dc, sl], start=(dc == 0), stop=(dc == 15))
                            c_tg.inc(mm)

                        @block.scalar
                        def _(sc, ts=ts, ntg=c_tg.n):
                            sc.wait_ge(c_tg.sem, ntg)
                            c_s5.inc(sc.activation(t1s[:, ts, :], pg[ts][:, :], AF.Silu))

                        @block.vector
                        def _(v, c=c, ts=ts, ntg=c_tg.n, ns=c_s5.n, routed=routed):
                            v.wait_ge(c_tg.sem, ntg)
                            if routed:
                                v.wait_ge(c_vb.sem, c_vb.n)
                                c_v5.inc(v.tensor_tensor(t2s[:, 0, :], pu[ts][:, :],
                                                         cbc[:, ts * 512:(ts + 1) * 512], ALU.mult))
                                v.wait_ge(c_v5.sem, c_v5.n)
                                v.wait_ge(c_s5.sem, ns)
                                c_v5.inc(v.tensor_tensor(hs[:, c, ts * 512:(ts + 1) * 512],
                                                         t1s[:, ts, :], t2s[:, 0, :], ALU.mult))
                            else:
                                v.wait_ge(c_s5.sem, ns)
                                c_v5.inc(v.tensor_tensor(hs[:, c, ts * 512:(ts + 1) * 512],
                                                         t1s[:, ts, :], pu[ts][:, :], ALU.mult))
                        pg_free[ts] = (c_s5.n, c_v5.n)

                for fg in range(8):
                    par = (exi * 8 + fg) % 2
                    if kind == "e":
                        dsrc = wdn[ei][:, fg * 256:(fg + 1) * 256]
                    elif kind == "sA":
                        dsrc = sdn[0:I_FF, fg * 256:(fg + 1) * 256]
                    else:
                        dsrc = sdn[I_FF:SI, fg * 256:(fg + 1) * 256]

                    @block.gpsimd
                    def _(gp, dsrc=dsrc, par=par, nty=c_ty.n):
                        if nty >= 2:
                            gp.wait_ge(c_ty.sem, nty - 1)
                        c_wl.inc(gp.dma_start(wd_sb[:, par, :, :],
                                              dsrc.rearrange("(ic p) f -> p ic f", p=128)), 16)

                    for fc in range(2):
                        for ts in range(2):
                            sl = slice(ts * 512, (ts + 1) * 512)
                            ypar = ts

                            @block.tensor
                            def _(t, par=par, fc=fc, ts=ts, sl=sl, nwl=c_wl.n, nv=c_v5.n, fg=fg):
                                if fc == 0 and ts == 0:
                                    t.wait_ge(c_wl.sem, nwl)
                                    if fg == 0:
                                        t.wait_ge(c_v5.sem, nv)
                                if py_free[ts] is not None:
                                    t.wait_ge(c_vy.sem, py_free[ts])
                                mm = None
                                for ic in range(11):
                                    mm = t.matmul(py[ts][:, :], wd_sb[:, par, ic, fc * 128:(fc + 1) * 128],
                                                  hs[:, ic, sl], start=(ic == 0), stop=(ic == 10))
                                c_ty.inc(mm)

                            @block.vector
                            def _(v, fg=fg, fc=fc, ts=ts, nty=c_ty.n, first=first, half=half):
                                v.wait_ge(c_ty.sem, nty)
                                ysl = Y[:, fg * 2 + fc, ts * 512:(ts + 1) * 512]
                                if first:
                                    c_vy.inc(v.tensor_copy(ysl, py[ts][:, :]))
                                else:
                                    c_vy.inc(v.tensor_tensor(ysl, ysl, py[ts][:, :], ALU.add))
                            py_free[ts] = c_vy.n

            mmul_hist = {}
            for dc in range(16):
                @block.scalar
                def _(sc, dc=dc, nvy=c_vy.n):
                    sc.wait_ge(c_vy.sem, nvy)
                    if (dc - 2) in mmul_hist:
                        sc.wait_ge(c_v5.sem, mmul_hist[dc - 2])
                    c_s5.inc(sc.activation(g_sb[:, dc % 2, :], Y[:, dc, :], AF.Silu))

                @block.sync
                def _(s, dc=dc, half=half):
                    if (dc - 1) in mmul_hist:
                        s.wait_ge(c_v5.sem, mmul_hist[dc - 1])
                    c_sld.inc(s.dma_start(ss_sb[:, 0, :],
                                          s_scr[dc * 128:(dc + 1) * 128, h0:h0 + 1024]), 16)

                @block.vector
                def _(v, dc=dc, ns=c_s5.n, nsl=c_sld.n, half=half):
                    v.wait_ge(c_s5.sem, ns)
                    v.wait_ge(c_sld.sem, nsl)
                    if c_mst.n >= 16:
                        v.wait_ge(c_mst.sem, c_mst.n)
                    c_v5.inc(v.tensor_tensor(m_sb[:, 0, :], g_sb[:, dc % 2, :],
                                             ss_sb[:, 0, :], ALU.mult))
                mmul_hist[dc] = c_v5.n

                @block.scalar
                def _(sc, dc=dc, half=half, nv=c_v5.n):
                    sc.wait_ge(c_v5.sem, nv)
                    c_mst.inc(sc.dma_start(m_scr[dc * 128:(dc + 1) * 128, h0:h0 + 1024],
                                           m_sb[:, 0, :]), 16)

        @block.scalar
        def _(sc):
            sc.wait_ge(c_mst.sem, 16 * 32)


def emit_attention_pass(nc, block, pas, g):
    cst, ones_row, ones_rf = g["cst"], g["ones_row"], g["ones_rf"]
    ones_bf, masksb, identb = g["ones_bf"], g["masksb"], g["identb"]
    wq, wk, wv = g["wq"], g["wk"], g["wv"]
    xn_scr, xnq_scr, ropeK, ropeQ, oat_scr = g["xn_scr"], g["xnq_scr"], g["ropeK"], g["ropeQ"], g["oat_scr"]
    mk_sem = g["mk_sem"]
    P = f"a{pas}_"
    with contextlib.ExitStack() as ps:
        sb = lambda n, shp, dt: ps.enter_context(nc.sbuf_tensor(P + n, shp, dt))
        pt_ = lambda n, shp: ps.enter_context(nc.psum_tensor(P + n, shp, F32))
        sem = lambda n: Ctr(mk_sem(P + n))

        q_all = sb("q_all", [128, 4, 8, 2, 256], BF16)
        kcache = None
        vcache = None

        with contextlib.ExitStack() as qs:
            sbq = lambda n, shp, dt: qs.enter_context(nc.sbuf_tensor(P + "q" + n, shp, dt))
            wq_sb = sbq("w", [128, 16, 1024], BF16)
            xnq_f = sbq("xn", [128, 16, NT], BF16)
            ropeq = sbq("rope", [128, 2, NT], F32)
            qsw = sbq("sw", [128, 512], F32)
            qt1 = sbq("t1", [128, 512], F32)
            qt2 = sbq("t2", [128, 512], F32)
            pq = [qs.enter_context(nc.psum_tensor(P + "qp0", [128, 512], F32)),
                  qs.enter_context(nc.psum_tensor(P + "qp1", [128, 512], F32))]
            c_ld = sem("qld")
            c_mm = sem("qmm")
            c_v = sem("qv")

            @block.gpsimd
            def _(gp):
                c_ld.inc(gp.dma_start(wq_sb[:, :, :],
                                      wq[:, pas * 1024:(pas + 1) * 1024].rearrange("(dc p) c -> p dc c", p=128)), 16)

            @block.sync
            def _(s):
                c_ld.inc(s.dma_start(ropeq[:, 0, :], ropeQ[0]), 16)
                c_ld.inc(s.dma_start(ropeq[:, 1, :], ropeQ[1]), 16)
                for dc in range(16):
                    for q4 in range(4):
                        c_ld.inc(s.dma_start(xnq_f[:, dc, q4 * 512:(q4 + 1) * 512],
                                             xnq_scr[q4 * 512:(q4 + 1) * 512, dc * 128:(dc + 1) * 128],
                                             transpose=True), 16)

            n_chunk = 0
            for h in range(8):
                for q4 in range(4):
                    par = n_chunk % 2
                    sl = slice(q4 * 512, (q4 + 1) * 512)

                    @block.tensor
                    def _(t, h=h, par=par, sl=sl, n=n_chunk):
                        if n == 0:
                            t.wait_ge(c_ld.sem, 16 * 67)
                        if n >= 2:
                            t.wait_ge(c_v.sem, 3 * (n - 2) + 2)
                        mm = None
                        for dc in range(16):
                            mm = t.matmul(pq[par][:, :], wq_sb[:, dc, h * 128:(h + 1) * 128],
                                          xnq_f[:, dc, sl], start=(dc == 0), stop=(dc == 15))
                        c_mm.inc(mm)

                    @block.vector
                    def _(v, h=h, par=par, sl=sl, n=n_chunk):
                        v.wait_ge(c_mm.sem, n + 1)
                        v.tensor_scalar(qsw[0:64, :], pq[par][64:128, :], cst[0:64, 2:3], None, ALU.mult)
                        c_v.inc(v.tensor_copy(qsw[64:128, :], pq[par][0:64, :]))
                        v.tensor_tensor(qt1[:, :], pq[par][:, :], ropeq[:, 0, sl], ALU.mult)
                        v.wait_ge(c_v.sem, 3 * n + 1)
                        c_v.inc(v.tensor_tensor(qt2[:, :], qsw[:, :], ropeq[:, 1, sl], ALU.mult))
                        v.wait_ge(c_v.sem, 3 * n + 2)
                        dst = q_all[:, h // 2, 2 * q4:2 * q4 + 2, h % 2, :]
                        c_v.inc(v.tensor_tensor(dst, qt1[:, :].rearrange("p (e c) -> p e c", c=256),
                                                qt2[:, :].rearrange("p (e c) -> p e c", c=256), ALU.add))
                    n_chunk += 1

            @block.tensor
            def _(t):
                t.wait_ge(c_v.sem, 3 * 32)
            c_qv = c_v

        if g.get("dbg"):
            @block.sync
            def _(s):
                s.wait_ge(c_qv.sem, 96)
                s.dma_start(g["dbg"]["q"], q_all[:, :, :]).then_inc(c_qv.sem, 16)
            c_qv.n += 16

        kcache = sb("kcache", [128, 2, S], BF16)
        vcache = sb("vcache", [128, 64, 256], BF16)
        with contextlib.ExitStack() as ks:
            sbk = lambda n, shp, dt: ks.enter_context(nc.sbuf_tensor(P + "k" + n, shp, dt))
            wk_sb = sbk("wk", [128, 16, 256], BF16)
            wv_sb = sbk("wv", [128, 16, 256], BF16)
            xnb_f = sbk("xnb", [128, 16, 1024], BF16)
            ropek = sbk("rope", [128, 2, 1024], F32)
            ksw = sbk("sw", [128, 512], F32)
            kt1 = sbk("t1", [128, 512], F32)
            kt2 = sbk("t2", [128, 512], F32)
            pk = [ks.enter_context(nc.psum_tensor(P + "kp0", [128, 512], F32)),
                  ks.enter_context(nc.psum_tensor(P + "kp1", [128, 512], F32))]
            pv = [ks.enter_context(nc.psum_tensor(P + "vp0", [128, 256], F32)),
                  ks.enter_context(nc.psum_tensor(P + "vp1", [128, 256], F32))]
            c_ld = sem("kld")
            c_mmk = sem("kmmk")
            c_mmv = sem("kmmv")
            c_vk = sem("kvk")
            c_vv = sem("kvv")

            @block.gpsimd
            def _(gp):
                gp.wait_ge(c_qv.sem, 96)
                c_ld.inc(gp.dma_start(wk_sb[:, :, :],
                                      wk[:, pas * 256:(pas + 1) * 256].rearrange("(dc p) c -> p dc c", p=128)), 16)
                c_ld.inc(gp.dma_start(wv_sb[:, :, :],
                                      wv[:, pas * 256:(pas + 1) * 256].rearrange("(dc p) c -> p dc c", p=128)), 16)
                gp.wait_ge(c_ld.sem, 32)

            for kb in range(8):
                @block.sync
                def _(s, kb=kb):
                    if kb == 0:
                        s.wait_ge(c_qv.sem, 3 * 32)
                    if kb > 0:
                        s.wait_ge(c_mmv.sem, 8 * kb)
                    for dc in range(16):
                        c_ld.inc(s.dma_start(xnb_f[:, dc, :],
                                             xn_scr[kb * 1024:(kb + 1) * 1024, dc * 128:(dc + 1) * 128],
                                             transpose=True), 16)
                    c_ld.inc(s.dma_start(ropek[:, 0, :], ropeK[0][:, kb * 1024:(kb + 1) * 1024]), 16)
                    c_ld.inc(s.dma_start(ropek[:, 1, :], ropeK[1][:, kb * 1024:(kb + 1) * 1024]), 16)

                for kvl in range(2):
                    for kss in range(2):
                        n = kb * 4 + kvl * 2 + kss
                        par = n % 2
                        sl = slice(kss * 512, (kss + 1) * 512)

                        @block.tensor
                        def _(t, kb=kb, kvl=kvl, n=n, par=par, sl=sl):
                            if kvl == 0 and kss == 0:
                                t.wait_ge(c_ld.sem, 16 * (2 + 18 * (kb + 1)))
                            if n >= 2:
                                t.wait_ge(c_vk.sem, 3 * (n - 2) + 2)
                            mm = None
                            for dc in range(16):
                                mm = t.matmul(pk[par][:, :], wk_sb[:, dc, kvl * 128:(kvl + 1) * 128],
                                              xnb_f[:, dc, sl], start=(dc == 0), stop=(dc == 15))
                            c_mmk.inc(mm)

                        @block.vector
                        def _(v, kb=kb, kvl=kvl, kss=kss, n=n, par=par, sl=sl):
                            v.wait_ge(c_mmk.sem, n + 1)
                            v.tensor_scalar(ksw[0:64, :], pk[par][64:128, :], cst[0:64, 2:3], None, ALU.mult)
                            c_vk.inc(v.tensor_copy(ksw[64:128, :], pk[par][0:64, :]))
                            v.tensor_tensor(kt1[:, :], pk[par][:, :], ropek[:, 0, sl], ALU.mult)
                            v.wait_ge(c_vk.sem, 3 * n + 1)
                            c_vk.inc(v.tensor_tensor(kt2[:, :], ksw[:, :], ropek[:, 1, sl], ALU.mult))
                            v.wait_ge(c_vk.sem, 3 * n + 2)
                            c_vk.inc(v.tensor_tensor(
                                kcache[:, kvl, kb * 1024 + kss * 512:kb * 1024 + (kss + 1) * 512],
                                kt1[:, :], kt2[:, :], ALU.add))

                for tc in range(8):
                    n = kb * 8 + tc
                    par = n % 2

                    @block.tensor
                    def _(t, kb=kb, tc=tc, n=n, par=par):
                        if n >= 2:
                            t.wait_ge(c_vv.sem, n - 1)
                        mm = None
                        for dc in range(16):
                            mm = t.matmul(pv[par][:, :], xnb_f[:, dc, tc * 128:(tc + 1) * 128],
                                          wv_sb[:, dc, :], start=(dc == 0), stop=(dc == 15))
                        c_mmv.inc(mm)

                    @block.vector
                    def _(v, kb=kb, tc=tc, n=n, par=par):
                        v.wait_ge(c_mmv.sem, n + 1)
                        c_vv.inc(v.tensor_copy(vcache[:, kb * 8 + tc, :], pv[par][:, :]))

            if g.get("dbg"):
                @block.sync
                def _(s):
                    s.wait_ge(c_vk.sem, 96)
                    s.wait_ge(c_vv.sem, 64)
                    s.dma_start(g["dbg"]["k"], kcache[:, :, :]).then_inc(c_vk.sem, 16)
                    s.dma_start(g["dbg"]["v"], vcache[:, :, :]).then_inc(c_vk.sem, 16)
                c_vk.n += 32

            @block.tensor
            def _(t):
                t.wait_ge(c_vk.sem, 96)
                t.wait_ge(c_vv.sem, 64)
            @block.scalar
            def _(sc):
                sc.wait_ge(c_vk.sem, 96)
                sc.wait_ge(c_vv.sem, 64)

        with contextlib.ExitStack() as ss:
            sbs = lambda n, shp, dt: ss.enter_context(nc.sbuf_tensor(P + "s" + n, shp, dt))
            pt_sb = sbs("pt", [128, 2, 8, 512], BF16)
            l_sb = sbs("l", [1, 2, 512], F32)
            rl_sb = sbs("rl", [1, 2, 512], F32)
            rlb = sbs("rlb", [128, 512], F32)
            obuf = sbs("ob", [128, 2, 512], BF16)
            pol = [ss.enter_context(nc.psum_tensor(P + "ol0", [128, 512], F32)),
                  ss.enter_context(nc.psum_tensor(P + "ol1", [128, 512], F32))]
            pst = [ss.enter_context(nc.psum_tensor(P + "st%d" % i, [128, 512], F32)) for i in range(4)]
            plbc = ss.enter_context(nc.psum_tensor(P + "lbc", [128, 1024], F32))
            c_sc = sem("ssc")
            c_tst = sem("stst")
            c_tav = sem("stav")
            c_vt = sem("svt")
            c_vr = sem("svr")
            c_tb = sem("stb")
            c_sto = sem("ssto")

            gg0 = [0]
            last_pt_user = {}
            LAG = 2

            neh = 0
            ehps = [(e, hp) for e in range(1, 9) for hp in range(4)]
            for e, hp in ehps:
                kvl = hp // 2
                sc0 = c_sc.n
                tst0 = c_tst.n
                tav0 = c_tav.n
                g0 = gg0[0]
                nr = 8 * e
                q_rhs = q_all[:, hp, e - 1, :, :]

                pt_waits = {}
                for g in range(nr):
                    kb, kc = g // 8, g % 8
                    pt_waits[g] = last_pt_user.get((kb % 2, kc), 0)
                    last_pt_user[(kb % 2, kc)] = tav0 + g + 1
                gg0[0] += nr

                @block.tensor
                def _(t, e=e, hp=hp, kvl=kvl, q_rhs=q_rhs, neh=neh, sc0=sc0, g0=g0, nr=nr):
                    if neh >= 1:
                        t.wait_ge(c_vt.sem, 3 * (neh - 1) + 2)
                        c_tb.inc(t.matmul(plbc[:, 0:512], ones_rf[:, :], rl_sb[:, (neh - 1) % 2, :],
                                          start=True, stop=True))
                    if neh >= 2:
                        t.wait_ge(c_vt.sem, 3 * (neh - 1))
                        t.wait_ge(c_vt.sem, 3 * (neh - 1) + 1)

                    def emit_st(g):
                        kb, kc = g // 8, g % 8
                        diag = (kb == e - 1)
                        gg = g0 + g
                        if gg >= 4:
                            t.wait_ge(c_sc.sem, gg - 3)
                        mm = t.matmul(pst[gg % 4][:, :],
                                      kcache[:, kvl, kb * 1024 + kc * 128:kb * 1024 + (kc + 1) * 128],
                                      q_rhs, start=True, stop=(not diag), skip_group_check=True)
                        if diag:
                            mm = t.matmul(pst[gg % 4][:, :], identb[:, :], masksb[:, kc, :],
                                          start=False, stop=True, skip_group_check=True)
                        c_tst.inc(mm)

                    def emit_av(g):
                        kb, kc = g // 8, g % 8
                        t.wait_ge(c_sc.sem, sc0 + g + 1)
                        first = (g == 0)
                        last = (g == nr - 1)
                        t.matmul(pol[neh % 2][:, :],
                                 vcache[:, kb * 8 + kc, kvl * 128:(kvl + 1) * 128],
                                 pt_sb[:, kb % 2, kc, :],
                                 start=first, stop=False, skip_group_check=True)
                        mm = t.matmul(plbc[0:1, 512:1024], ones_bf[:, :],
                                      pt_sb[:, kb % 2, kc, :],
                                      start=first, stop=last, skip_group_check=True)
                        c_tav.inc(mm)

                    for g in range(nr):
                        emit_st(g)
                        if g >= LAG:
                            emit_av(g - LAG)
                    for g in range(nr - LAG, nr):
                        emit_av(g)

                @block.scalar
                def _(sc, e=e, neh=neh, sc0=sc0, tst0=tst0, g0=g0, nr=nr, pt_waits=pt_waits):
                    for g in range(nr):
                        kb, kc = g // 8, g % 8
                        gg = g0 + g
                        pw = pt_waits[g]
                        if pw > 0:
                            sc.wait_ge(c_tav.sem, pw)
                        sc.wait_ge(c_tst.sem, tst0 + g + 1)
                        c_sc.inc(sc.activation(pt_sb[:, kb % 2, kc, :], pst[gg % 4][:, :],
                                               AF.Exp, bias=cst[:, 6:7], scale=cst[:, 5:6]))

                @block.vector
                def _(v, e=e, neh=neh, tav0=tav0, nr=nr):
                    v.wait_ge(c_tav.sem, tav0 + nr)
                    c_vt.inc(v.tensor_copy(l_sb[:, neh % 2, :], plbc[0:1, 512:1024]))
                    v.wait_ge(c_vt.sem, 3 * neh + 1)
                    c_vt.inc(v.reciprocal(rl_sb[:, neh % 2, :], l_sb[:, neh % 2, :]))
                    v.wait_ge(c_tb.sem, neh + 1)
                    c_vr.inc(v.tensor_copy(rlb[:, :], plbc[:, 0:512]))
                    v.wait_ge(c_vr.sem, neh + 1)
                    if neh >= 2:
                        v.wait_ge(c_sto.sem, 32 * (neh - 1))
                    c_vt.inc(v.tensor_tensor(obuf[:, neh % 2, :], pol[neh % 2][:, :],
                                             rlb[:, :], ALU.mult))

                @block.sync
                def _(s, e=e, hp=hp, neh=neh):
                    s.wait_ge(c_vt.sem, 3 * (neh + 1))
                    h0g = pas * 8 + 2 * hp
                    c_sto.inc(s.dma_start(oat_scr[h0g * 128:(h0g + 1) * 128, (e - 1) * 256:e * 256],
                                          obuf[:, neh % 2, 0:256]), 16)
                    c_sto.inc(s.dma_start(oat_scr[(h0g + 1) * 128:(h0g + 2) * 128, (e - 1) * 256:e * 256],
                                          obuf[:, neh % 2, 256:512]), 16)
                neh += 1

            @block.tensor
            def _(t):
                t.wait_ge(c_vt.sem, 3 * 31 + 2)
                c_tb.inc(t.matmul(plbc[:, 0:512], ones_rf[:, :], rl_sb[:, 31 % 2, :],
                                  start=True, stop=True))

            @block.sync
            def _(s):
                s.wait_ge(c_sto.sem, c_sto.n)

            @block.vector
            def _(v):
                v.wait_ge(c_vt.sem, c_vt.n)

            @block.scalar
            def _(sc):
                sc.wait_ge(c_sc.sem, c_sc.n)

            @block.tensor
            def _(t):
                t.wait_ge(c_tb.sem, c_tb.n)



def host_prepare(inputs, core, batch, c):
    x = np.asarray(inputs["L0_attn_x"])
    xb = np.ascontiguousarray(x[batch])
    rows = []
    for e in range(1, 9):
        for sub in (c, 7 - c):
            t0 = (e - 1) * 1024 + sub * 128
            rows.append(np.arange(t0, t0 + 128))
    rows = np.concatenate(rows)
    xqv = np.ascontiguousarray(xb[rows])

    inv = (1.0 / (10000.0 ** (np.arange(0, HD, 2) / HD))).astype(np.float64)
    posK = np.arange(S)
    angK = posK[None, :] * np.concatenate([inv, inv])[:, None]
    ropeK = np.stack([np.cos(angK), np.sin(angK)]).astype(np.float32)
    angQ = rows[None, :] * np.concatenate([inv, inv])[:, None]
    ropeQ = np.stack([np.cos(angQ), np.sin(angQ)]).astype(np.float32)

    maskD = np.zeros((8, 128, 512), np.float32)
    for kc in range(8):
        for r, sub in enumerate((c, 7 - c)):
            kabs = kc * 128 + np.arange(128)[:, None]
            qabs = sub * 128 + np.arange(128)[None, :]
            blk = np.where(kabs <= qabs, 0.0, MASK_NEG)
            maskD[kc, :, r * 128:(r + 1) * 128] = blk
            maskD[kc, :, 256 + r * 128:256 + (r + 1) * 128] = blk

    return {
        "xb": xb,
        "xq": xqv,
        "wq": np.asarray(inputs["L0_attn_Wq"]),
        "wk": np.asarray(inputs["L0_attn_Wk"]),
        "wv": np.asarray(inputs["L0_attn_Wv"]),
        "wo": np.asarray(inputs["L0_attn_Wo"]),
        "gwT": np.ascontiguousarray(np.asarray(inputs["L0_ffn_gate_w"]).T),
        "wgu": np.asarray(inputs["L0_ffn_w_gate_up"]),
        "wdn": np.asarray(inputs["L0_ffn_w_down"]),
        "sgu": np.asarray(inputs["L0_ffn_shared_gate_up"]),
        "sdn": np.asarray(inputs["L0_ffn_shared_down"]),
        "identin": np.eye(128, dtype=np.float32),
        "ropeK": ropeK,
        "ropeQ": ropeQ,
        "maskD": maskD,
    }, rows


def kernel(**inputs):
    nc = build(debug=False)
    in_maps = []
    row_list = []
    for core in range(8):
        batch, c = divmod(core, 4)
        m, rows = host_prepare(inputs, core, batch, c)
        in_maps.append(m)
        row_list.append(rows)
    res = run_bass_kernel_spmd(nc, in_maps, core_ids=list(range(8)))
    out = np.zeros((2, S, D), np.float32)
    for core in range(8):
        batch = core // 4
        out[batch, row_list[core]] = res.results[core]["out"]
    return out
